# version 53
# baseline (speedup 1.0000x reference)
import numpy as np
import concourse.bass as bass
import concourse.mybir as mybir

F32 = mybir.dt.float32
BF16 = mybir.dt.bfloat16
U8 = mybir.dt.uint8
OP = mybir.AluOpType
AF = mybir.ActivationFunctionType

T = 2048
D = 1024
CH = 8
NTOK = 16
TH = 1024
NEG30 = -30.0
LN_EPS = 1e-5
LOG100 = float(np.log(100.0))
LN_1EM9 = float(np.log(1e-9))

_CACHE = {}
_TRACE_EMIT = False


class Plan:

    ENGINES = ("q", "g", "t", "s", "v")

    def __init__(self):
        self.ops = {e: [] for e in self.ENGINES}
        self.marked = set()
        self.fence_pending = {}
        self.chain_cum = {}
        self.src = {}

    def emit(self, eng, fn, deps=(), dma=False):
        deps = [d for d in deps if d is not None]
        deps = self.fence_pending.pop(eng, []) + deps
        for d in deps:
            self.marked.add((d[0], d[1]))
        idx = len(self.ops[eng])
        self.ops[eng].append([fn, deps, dma, False])
        if _TRACE_EMIT:
            import traceback
            fr = traceback.extract_stack(limit=3)[0]
            self.src[(eng, idx)] = f"{fr.lineno}"
        return (eng, idx, dma)

    def fence(self):
        last = {}
        for e in self.ENGINES:
            if self.ops[e]:
                last[e] = (e, len(self.ops[e]) - 1, self.ops[e][-1][2])
        for e in self.ENGINES:
            toks = [t for o, t in last.items() if o != e]
            self.fence_pending.setdefault(e, []).extend(toks)

    NDMA = {"q": 8, "g": 4}

    def resolve(self):
        self.cum = {e: [0] * len(self.ops[e]) for e in self.ENGINES}
        self.dma_slot = {}
        for e in self.ENGINES:
            c_cmp = 0
            k = 0
            per_slot = [0] * self.NDMA.get(e, 1)
            for i, op in enumerate(self.ops[e]):
                if op[2] or (e, i) in self.marked:
                    op[3] = True
                    if not op[2]:
                        c_cmp += 1
                if op[2]:
                    j = k % len(per_slot)
                    per_slot[j] += 16
                    self.dma_slot[(e, i)] = (j, per_slot[j])
                    k += 1
                    self.cum[e][i] = per_slot[j]
                else:
                    self.cum[e][i] = c_cmp

    CHAINED = ("g", "s", "v")

    def replay(self, eng_name, eng_obj, sems, chain_sem=None, lag=1,
               chain_counts=None):
        if chain_sem is not None:
            eng_obj = _ChainEng(eng_obj, chain_sem, lag)
        self.chain_cum[eng_name] = [0] * len(self.ops[eng_name])
        maxw = {}
        for i, (fn, deps, dma, inc) in enumerate(self.ops[eng_name]):
            for d in deps:
                de, di, ddma = d[0], d[1], d[2]
                hard_d = len(d) == 4 and d[3]
                if ddma:
                    j, val = self.dma_slot[(de, di)]
                    sem = sems[(de, "dma", j)]
                elif de in self.CHAINED:
                    if de == eng_name:
                        continue
                    sem = sems[(de, "ch")]
                    val = chain_counts[de][di] if chain_counts else 0
                else:
                    if de == eng_name and not hard_d:
                        continue
                    sem = sems[(de, False)]
                    val = self.cum[de][di]
                if maxw.get(id(sem), -1) >= val:
                    continue
                maxw[id(sem)] = val
                eng_obj.wait_ge(sem, val)
            instr = fn(eng_obj)
            if chain_sem is not None:
                self.chain_cum[eng_name][i] = eng_obj._n
            if dma:
                j, val = self.dma_slot[(eng_name, i)]
                sem = sems[(eng_name, "dma", j)]
                if val > 16:
                    instr._wait_ge(sem, val - 16)
                instr.then_inc(sem, 16)
            elif inc and chain_sem is None:
                assert instr is not None, f"marked op {eng_name}[{i}] is None"
                instr.then_inc(sems[(eng_name, False)], 1)


class _ChainEng:

    _SKIP = frozenset((
        "wait_ge", "wait_op", "sem_inc", "sem_clear", "maybe_wait_ge",
        "dma_start", "add_instruction", "preload_act_table",
    ))

    def __init__(self, eng, sem, lag):
        self._eng = eng
        self._sem = sem
        self._lag = lag
        self._n = 0

    def __getattr__(self, name):
        f = getattr(self._eng, name)
        if name in self._SKIP or not callable(f):
            return f

        def wrap(*a, **k):
            r = f(*a, **k)
            if hasattr(r, "then_inc"):
                n = self._n
                w = n + 1 - self._lag
                if w > 0:
                    r._wait_ge(self._sem, w)
                r.then_inc(self._sem, 1)
                self._n = n + 1
            return r

        return wrap


def hard(tok):
    return (tok[0], tok[1], tok[2], True)


def _fm(W):
    R, C = W.shape
    return np.ascontiguousarray(
        W.reshape(R // 128, 128, C).transpose(1, 0, 2).reshape(128, -1))


def _col8(v):
    return np.ascontiguousarray(v.reshape(8, 128).T)


def _prep_shared(inp):
    wd = np.concatenate([inp[f"L0_ffn_Wd{i}"] for i in range(4)], axis=0)
    wu = np.concatenate([inp[f"L0_ffn_Wu{i}"] for i in range(4)], axis=1)
    small = np.zeros((128, 324), np.float32)
    small[:, 0:8] = _col8(inp["L0_attn_time_mix_k"])
    small[:, 8:16] = _col8(inp["L0_attn_time_mix_v"])
    small[:, 16:24] = _col8(inp["L0_attn_time_mix_r"])
    small[:, 24:32] = _col8(inp["L0_attn_time_decay"])
    small[:, 32:40] = _col8(inp["L0_attn_time_first"])
    small[:, 40:42] = np.ascontiguousarray(inp["L0_ffn_bp"].reshape(2, 128).T)
    small[:, 42:50] = np.ascontiguousarray(
        np.asarray(inp["L0_ffn_sim"]).reshape(2, 128, 4)
        .transpose(1, 0, 2).reshape(128, 8))
    small[:, 50] = float(np.asarray(inp["L0_ffn_temp"]).reshape(-1)[0])
    small[:, 51] = 1.0
    sel = np.zeros((4, 15), np.float32)
    sel[0, 0] = sel[1, 1:3] = sel[2, 3:7] = sel[3, 7:15] = 1.0
    small[32:36, 52:67] = sel
    small[:, 67] = np.log(1e-9)
    small[0, 68:196] = 1.0
    small[64, 68:196] = 1.0
    small[:, 196:324] = np.eye(128, dtype=np.float32)
    ms = [_fm(np.ascontiguousarray(np.asarray(inp["L0_attn_W_" + p]).T))
          .reshape(128, 8, 8, 128) for p in "kvr"]
    return dict(
        wkvr=np.ascontiguousarray(
            np.stack(ms, axis=1).transpose(0, 3, 1, 2, 4)
            .reshape(128, 3 * CH * D)),
        wot=_fm(np.ascontiguousarray(np.asarray(inp["L0_attn_W_o"]).T)),
        wt=_fm(np.ascontiguousarray(np.asarray(inp["L0_ffn_W"]).T)),
        wpt=_fm(np.ascontiguousarray(np.asarray(inp["L0_ffn_Wp"]).T)),
        wdt=_fm(np.ascontiguousarray(np.asarray(wd).T)),
        wut=np.ascontiguousarray(np.asarray(wu).T),
        small=small)


def build_program(debug=False):
    _, _, p1 = _build_program(debug, None)
    nc, dbg, _ = _build_program(debug, p1.chain_cum)
    return nc, dbg


def _build_program(debug=False, chain_counts=None):
    nc = bass.Bass("TRN2", target_bir_lowering=False)
    P = Plan()
    E = P.emit

    x_h = nc.dram_tensor("x", [T, D], F32, kind="ExternalInput")
    out_h = nc.dram_tensor("out", [T, D], F32, kind="ExternalOutput")
    din = {n: nc.dram_tensor(n, s, F32, kind="ExternalInput") for n, s in [
        ("wkvr", [128, 3 * CH * D]),
        ("wot", [128, CH * D]), ("wt", [128, CH * D]), ("wpt", [128, CH * 256]),
        ("wdt", [128, CH * 15]), ("wut", [15, D]), ("small", [128, 324])]}

    dbg = {}

    def dram_dbg(name, shape, dtype):
        ap = nc.dram_tensor("dbg_" + name, shape, dtype,
                            kind="ExternalOutput").ap()
        dbg[name] = ap
        return ap

    x_t = x_h.ap().rearrange("(j p) d -> p j d", p=128)
    out_t = out_h.ap().rearrange("(j p) d -> p j d", p=128)

    from contextlib import ExitStack
    ctx = ExitStack()
    sb = lambda n, s, d: ctx.enter_context(nc.sbuf_tensor(n, s, d)).ap()

    xn = sb("xn", [128, CH * T], BF16)
    y = sb("y", [128, CH * T], BF16)
    ws_all = [sb(f"ws_all{i}", [128, 3 * CH * 128], BF16) for i in range(2)]
    ws = {p: [ws_all[i][:, pi * CH * 128:(pi + 1) * CH * 128]
              for i in range(2)] for pi, p in enumerate("kvr")}
    hcat = sb("hcat", [15, T], BF16)
    wut_sb = sb("wut_sb", [15, D], BF16)
    wdt_sb = sb("wdt_sb", [128, CH * 15], BF16)
    small = sb("small_sb", [128, 324], F32)

    stat = sb("stat", [128, 112], F32)
    car = sb("car", [128, 16], F32)
    aux = sb("aux", [128, 48], F32)
    tiny = sb("tiny", [4, 64], F32)
    selS = small[32:36, 52:67]
    ones1 = small[0:1, 68:196]
    ones64 = small[64:65, 68:196]
    onesc = small[:, 51:52]
    simnf = aux[:, 0:8]

    AR = 124976
    arena = ctx.enter_context(nc.sbuf_tensor("arena", [128, AR], U8)).ap()

    def view(off, nbytes, dtype, cols):
        ap = arena[:, off:off + nbytes].bitcast(dtype)
        assert ap.shape[1] == cols, (ap.shape, cols)
        return ap

    xt0 = [view(i * 4096, 4096, F32, 1024) for i in range(16)]
    xn_tm = [view(65536 + i * 4096, 4096, F32, 1024) for i in range(4)]
    wot_sb = view(0, 16384, BF16, CH * D)
    wt_sb = view(16384, 16384, BF16, CH * D)
    VP = view(81920, 8192, F32, 2048)
    raw_sb = VP[0:4, :]
    G4sb = VP[32:36, :]
    rn = VP[64:65, :]
    rb = VP[96:97, :]
    proj_bf = view(90112, 16384, F32, 2 * T)
    proj2 = view(106496, 4096, F32, T // 2)
    gb = 73728
    LG = view(gb, 256, F32, 64)
    GD = view(gb + 256, 256, F32, 64)
    GE = view(gb + 512, 256, F32, 64)
    SEL = view(gb + 768, 256, F32, 64)
    NB = view(gb + 1024, 256, F32, 64)
    GT4 = view(gb + 1280, 256, F32, 64)
    GS = view(gb + 1536, 64, F32, 16)
    GR = view(gb + 1600, 64, F32, 16)
    PH3 = 32768
    mixh = {p: view(PH3 + i * 16384, 16384, BF16, CH * TH)
            for i, p in enumerate("kvr")}

    sc_off = PH3 + 51200
    v_sb = view(sc_off + 1 * 4096, 4096, F32, TH)
    A_b = view(sc_off + 3 * 4096, 4096, F32, TH)
    B_b = view(sc_off + 4 * 4096, 4096, F32, TH)
    DD = [view(sc_off + 0 * 4096, 4096, F32, TH), view(112672, 4096, F32, TH)]
    EE = [view(sc_off + 2 * 4096, 4096, F32, TH), view(116768, 4096, F32, TH)]
    pbuf = view(sc_off + 20480, 4112, F32, 1028)
    qbuf = view(sc_off + 24592, 4112, F32, 1028)
    qbuf2 = view(120864, 4112, F32, 1028)
    PH5 = 32768
    xt5 = [view(PH5 + i * 4096, 4096, F32, 1024) for i in range(2)]
    wpt_sb = view(110592, 4096, BF16, CH * 256)
    OT = [view(PH5 + 8192 + i * 4096, 4096, F32, 1024) for i in range(2)]
    EP = [view(PH5 + 16384 + i * 4096, 4096, F32, 1024) for i in range(2)]
    GZ = [view(PH5 + 24576 + i * 4096, 4096, F32, 1024) for i in range(2)]

    pp = [ctx.enter_context(nc.psum_tensor(f"pp{i}", [128, 1024], F32)).ap()
          for i in range(4)]

    xn3 = xn.rearrange("p (c t) -> p c t", c=CH)
    y3 = y.rearrange("p (c t) -> p c t", c=CH)
    wot3 = wot_sb.rearrange("p (c t) -> p c t", c=CH)
    wt3 = wt_sb.rearrange("p (c t) -> p c t", c=CH)
    wsl_all = din["wkvr"].ap().rearrange("p (s r c e) -> p s r c e",
                                         s=8, r=3, c=CH)

    tmcol = {"k": lambda c: small[:, 0 + c:1 + c],
             "v": lambda c: small[:, 8 + c:9 + c],
             "r": lambda c: small[:, 16 + c:17 + c]}
    wcol = lambda c: small[:, 24 + c:25 + c]
    ucol = lambda c: small[:, 32 + c:33 + c]

    t_small = E("q", lambda q: q.dma_start(out=small, in_=din["small"].ap()),
                dma=True)
    t_wpt = E("g", lambda g: g.dma_start(out=wpt_sb, in_=din["wpt"].ap()),
              dma=True)
    t_wdt = E("g", lambda g: g.dma_start(out=wdt_sb, in_=din["wdt"].ap()),
              dma=True)
    t_wut = E("g", lambda g: g.dma_start(out=wut_sb, in_=din["wut"].ap()),
              dma=True)

    def load_slices(c, h, deps):
        ktok = E("g", lambda g, c=c: g.dma_start(
            out=ws_all[c % 2][:, 0:CH * 128].rearrange(
                "p (c e) -> p c e", c=CH),
            in_=wsl_all[:, c, 0, :, :]),
            deps=deps.get("k", []), dma=True)
        vrtok = E("g", lambda g, c=c: g.dma_start(
            out=ws_all[c % 2][:, CH * 128:3 * CH * 128].rearrange(
                "p (r c e) -> p r c e", r=2, c=CH),
            in_=wsl_all[:, c, 1:3, :, :]),
            deps=deps.get("v", []) + deps.get("r", []), dma=True)
        return {"k": ktok, "v": vrtok, "r": vrtok}

    ident = small[:, 196:324]
    t_ident = t_small

    t_sel = t_small

    slice_toks = {(0, 0): load_slices(0, 0, {}), (1, 0): load_slices(1, 0, {})}
    E("v", lambda v: v.memset(car, 0.0))
    t_om = E("v", lambda v: v.tensor_scalar(
        out=aux[:, 24:48], in0=small[:, 0:24], scalar1=-1.0, scalar2=1.0,
        op0=OP.mult, op1=OP.add), deps=[t_small])

    t_tab6 = E("s", lambda s: s.add_instruction(mybir.InstLoadActFuncSet(
        name=s.bass.get_next_instruction_name(), act_func_set_id=6, ins=[],
        outs=[])))
    t_s2 = E("v", lambda v: v.tensor_mul(out=stat[:, 64:72],
                                         in0=small[:, 42:50],
                                         in1=small[:, 42:50]),
             deps=[t_small])
    sim_mm = E("t", lambda t: t.matmul(pp[2][0:1, 0:8], onesc,
                                       stat[:, 64:72], start=True, stop=True),
               deps=[t_s2])
    t_s3 = E("s", lambda s: s.copy(out=tiny[0:1, 8:16], in_=pp[2][0:1, 0:8]),
             deps=[sim_mm, t_tab6])
    t_s4a = E("v", lambda v: v.tensor_add(
        out=tiny[0:1, 16:20], in0=tiny[0:1, 8:12], in1=tiny[0:1, 12:16]),
        deps=[t_s3])
    t_s4 = E("v", lambda v: v.tensor_scalar(
        out=tiny[0:1, 16:20], in0=tiny[0:1, 16:20],
        scalar1=1e-24, scalar2=None, op0=OP.max),
        deps=[hard(t_s4a)])
    t_s5a = E("s", lambda s: s.activation(
        out=tiny[0:1, 20:24], in_=tiny[0:1, 16:20], func=AF.Ln), deps=[t_s4])
    t_s5 = E("s", lambda s: s.activation(
        out=tiny[0:1, 20:24], in_=tiny[0:1, 20:24], func=AF.Exp,
        scale=-0.5), deps=[hard(t_s5a)])
    simb_mm = E("t", lambda t: t.matmul(pp[2][0:128, 0:4], ones1,
                                        tiny[0:1, 20:24], start=True,
                                        stop=True), deps=[t_s5, t_s3])
    t_s6 = E("v", lambda v: (
        v.tensor_mul(out=simnf[:, 0:4], in0=small[:, 42:46],
                     in1=pp[2][0:128, 0:4]),
        v.tensor_mul(out=simnf[:, 4:8], in0=small[:, 46:50],
                     in1=pp[2][0:128, 0:4]))[-1],
        deps=[simb_mm])

    xload = [None] * NTOK
    norm_tok = [None] * NTOK
    sstat_tok = [None] * NTOK
    vstat_tok = [None] * NTOK
    tr_tok = [None] * NTOK
    evac_tok = []
    stat_scr_v = view(114688, 2048, F32, 512)
    stat_scr_s = view(116736, 2048, F32, 512)
    xb4 = [view(16384 * i, 16384, F32, 4096) for i in range(4)]
    pr_mm, pr_ev, h_mm, hc_ev = [], [], [], []
    nrm_mm, nrm_ev = [], []

    def emit_wave(w0, extra_deps):
        sl = lambda base, w0=w0: stat[:, base + w0:base + w0 + 4]
        t_v0 = E("v", lambda v, sl=sl, w0=w0: (
            v.tensor_add(out=sl(16), in0=sl(16),
                         in1=stat[:, 96 + w0:100 + w0]),
            v.tensor_add(out=sl(0), in0=sl(0),
                         in1=stat[:, 80 + w0:84 + w0]))[-1],
            deps=[sstat_tok[w0 + 3]] + extra_deps)
        t_v1 = E("v", lambda v, sl=sl: (
            v.tensor_scalar(out=sl(32), in0=sl(0), scalar1=1.0 / D,
                            scalar2=None, op0=OP.mult),
            v.tensor_scalar(out=sl(48), in0=sl(16), scalar1=1.0 / D,
                            scalar2=LN_EPS, op0=OP.mult, op1=OP.add),
            v.tensor_mul(out=sl(0), in0=sl(32), in1=sl(32)),
            v.tensor_copy(out=aux[:, 8:12], in_=car[:, 0:4]),
            v.tensor_sub(out=sl(48), in0=sl(48), in1=sl(0)))[-1],
            deps=[t_v0])
        t_r0 = E("s", lambda s, sl=sl: (
            s.activation(out=sl(64), in_=sl(48), func=AF.Ln),
            s.activation(out=sl(64), in_=sl(64), func=AF.Exp,
                         scale=-0.5))[-1],
            deps=[t_v1])
        for jj in range(w0, w0 + 4):
            norm_tok[jj] = E("v", lambda v, jj=jj: v.tensor_scalar(
                out=xn_tm[jj % 4], in0=xt0[jj],
                scalar1=stat[:, 32 + jj:33 + jj],
                scalar2=stat[:, 64 + jj:65 + jj],
                op0=OP.subtract, op1=OP.mult),
                deps=[t_r0, tr_tok[jj - 4] if jj >= 4 else None])
        for jj in range(w0, w0 + 4):
            tr_tok[jj] = E("t", lambda t, jj=jj: [
                t.transpose(pp[jj % 2][:, 128 * i:128 * i + 128],
                            xn_tm[jj % 4][:, 128 * i:128 * i + 128], ident)
                for i in range(8)][-1],
                deps=[norm_tok[jj], t_ident] +
                     ([evac_tok[2 * (jj - 2)], evac_tok[2 * (jj - 2) + 1]]
                      if jj >= 2 else []))
            for half in (0, 1):
                srcp = pp[jj % 2][:, 512 * half:512 * half + 512].rearrange(
                    "p (c t) -> p c t", c=4)
                dst = xn3[:, 4 * half:4 * half + 4, 128 * jj:128 * jj + 128]
                if half == 0:
                    tok = E("v", lambda v, srcp=srcp, dst=dst:
                            v.tensor_copy(out=dst, in_=srcp),
                            deps=[tr_tok[jj]])
                else:
                    tok = E("s", lambda s, srcp=srcp, dst=dst:
                            s.copy(out=dst, in_=srcp),
                            deps=[tr_tok[jj]])
                evac_tok.append(tok)
        nt = w0 // 4
        xn_done = [evac_tok[-1], evac_tok[-2]]
        for q2 in range(2):
            def fproj(t, nt=nt, q2=q2):
                last = None
                for cd in range(CH):
                    last = t.matmul(
                        pp[2][:, 512 * q2:512 * q2 + 512],
                        wpt_sb[:, cd * 256 + q2 * 128:cd * 256 + q2 * 128 + 128],
                        xn3[:, cd, 512 * nt:512 * nt + 512],
                        start=(cd == 0), stop=(cd == 7))
                return last
            pp2_free = pr_ev[-1] if pr_ev else t_s6
            pr_mm.append(E("t", fproj, deps=[t_wpt, pp2_free] + xn_done))
            col = q2 * T + 512 * nt
            ps = pp[2][:, 512 * q2:512 * q2 + 512]
            pr_ev.append(E("s", lambda s, ps=ps, col=col, q2=q2: (
                s.activation(out=proj_bf[:, col:col + 512], in_=ps,
                             func=AF.Identity, bias=small[:, 40 + q2:41 + q2]),
                s.activation(out=proj2[:, 512 * q2:512 * q2 + 512], in_=ps,
                             func=AF.Square, bias=small[:, 40 + q2:41 + q2]))[-1],
                deps=[pr_mm[-1], t_small, nrm_mm[-1] if nrm_mm else None]))
        nrm_mm.append(E("t", lambda t: [
            t.matmul(pp[3][0:1, 512:1024], onesc,
                     proj2[:, 512 * q2:512 * q2 + 512],
                     start=(q2 == 0), stop=(q2 == 1))
            for q2 in range(2)][-1],
            deps=[pr_ev[-1], pr_ev[-2], t_sel,
                  nrm_ev[-1] if nrm_ev else None]))
        nrm_ev.append(E("s", lambda s, nt=nt: s.copy(
            out=rb[:, 512 * nt:512 * nt + 512], in_=pp[3][0:1, 512:1024]),
            deps=[nrm_mm[-1]]))
        def fh(t, nt=nt):
            last = None
            for cd in range(CH):
                last = t.matmul(pp[3][0:15, 0:512],
                                wdt_sb[:, 15 * cd:15 * cd + 15],
                                xn3[:, cd, 512 * nt:512 * nt + 512],
                                start=(cd == 0), stop=(cd == 7))
            return last
        h_mm.append(E("t", fh, deps=[t_wdt, hc_ev[-1] if hc_ev else None]
                      + xn_done))
        hc_ev.append(E("v", lambda v, nt=nt: v.tensor_copy(
            out=hcat[:, 512 * nt:512 * nt + 512], in_=pp[3][0:15, 0:512]),
            deps=[h_mm[-1]]))

    xbat = [(0, 2), (2, 2), (4, 4), (8, 2), (10, 2), (12, 1), (13, 1),
            (14, 1), (15, 1)]
    for j0, nb in xbat:
        bt = E("q", lambda q, j0=j0, nb=nb: q.dma_start(
            out=view(16384 * (j0 // 4) + 4096 * (j0 % 4), 4096 * nb, F32,
                     1024 * nb),
            in_=x_t[:, j0:j0 + nb, :]), dma=True)
        for k in range(nb):
            xload[j0 + k] = bt
    for j in range(NTOK):
        if False:
            pass
        E("v", lambda v, j=j: v.tensor_scalar(
            out=stat_scr_v, in0=xt0[j][:, 0:512], scalar1=0.0, scalar2=None,
            op0=OP.add, op1=OP.add, accum_out=stat[:, j:j + 1]),
            deps=[xload[j]])
        vstat_tok[j] = E("v", lambda v, j=j: v.tensor_scalar(
            out=stat_scr_v, in0=xt0[j][:, 512:1024], scalar1=0.0,
            scalar2=None, op0=OP.add, op1=OP.add,
            accum_out=stat[:, 80 + j:81 + j]),
            deps=[xload[j]])
        E("s", lambda s, j=j: s.activation(
            out=stat_scr_s, in_=xt0[j][:, 0:512], func=AF.Square,
            accum_out=stat[:, 16 + j:17 + j]),
            deps=[xload[j], t_tab6])
        sstat_tok[j] = E("s", lambda s, j=j: s.activation(
            out=stat_scr_s, in_=xt0[j][:, 512:1024], func=AF.Square,
            accum_out=stat[:, 96 + j:97 + j]),
            deps=[xload[j], t_tab6])
        if j in (4, 8, 12):
            emit_wave(j - 4, [])
    emit_wave(12, [])
    ph0_v = evac_tok[-2]
    ph0_s = evac_tok[-1]

    t_wot = E("g", lambda g: g.dma_start(out=wot_sb, in_=din["wot"].ap()),
              deps=[norm_tok[7], sstat_tok[7]], dma=True)
    t_wt = E("g", lambda g: g.dma_start(out=wt_sb, in_=din["wt"].ap()),
             deps=[norm_tok[7], sstat_tok[7]], dma=True)

    if debug:
        d = dram_dbg("xn", [128, CH * T], BF16)
        E("q", lambda q, d=d: q.dma_start(out=d, in_=xn),
          deps=[ph0_v, ph0_s], dma=True)
        d2 = dram_dbg("stat", [128, 112], F32)
        E("q", lambda q, d=d2: q.dma_start(out=d2, in_=stat),
          deps=[ph0_v, ph0_s], dma=True)

    mix_tok = {}
    omcol = {"k": lambda c: aux[:, 24 + c:25 + c],
             "v": lambda c: aux[:, 32 + c:33 + c],
             "r": lambda c: aux[:, 40 + c:41 + c]}

    def emit_mix_s1(h, p, eng, deps):
        toks = []
        for cd in range(CH):
            if eng == "s":
                def fs1(s, p=p, cd=cd, h=h):
                    dst = mixh[p][:, TH * cd:TH * cd + TH]
                    if h == 0:
                        return s.activation(
                            out=dst[:, 1:TH],
                            in_=xn[:, 2048 * cd:2048 * cd + TH - 1],
                            func=AF.Identity, scale=omcol[p](cd))
                    off = 2048 * cd + TH * h - 1
                    return s.activation(
                        out=dst, in_=xn[:, off:off + TH],
                        func=AF.Identity, scale=omcol[p](cd))
                toks.append(E("s", fs1, deps=deps + [t_om]))
            else:
                def fg1(g, p=p, cd=cd, h=h):
                    dst = mixh[p][:, TH * cd:TH * cd + TH]
                    if h == 0 and cd == 0:
                        return g.tensor_scalar(
                            out=dst[:, 1:TH], in0=xn3[:, cd, 0:TH - 1],
                            scalar1=omcol[p](cd), scalar2=None, op0=OP.mult)
                    off = 2048 * cd + TH * h - 1
                    return g.tensor_scalar(
                        out=dst, in0=xn[:, off:off + TH],
                        scalar1=omcol[p](cd), scalar2=None, op0=OP.mult)
                toks.append(E("g", fg1, deps=deps + [t_om]))
        return toks

    def emit_mix_s2(h, p, s1toks, deps, eng="v"):
        if h == 0:
            E(eng, lambda v, p=p: [
                v.memset(mixh[p][:, TH * cd:TH * cd + 1], 0.0)
                for cd in range(CH)][-1], deps=deps)
        tok = None
        for cd in range(CH):
            def fm2(v, p=p, cd=cd, h=h):
                dst = mixh[p][:, TH * cd:TH * cd + TH]
                off = 2048 * cd + TH * h
                return v.scalar_tensor_tensor(
                    out=dst, in0=xn[:, off:off + TH],
                    scalar=tmcol[p](cd), in1=dst, op0=OP.mult, op1=OP.add)
            tok = E(eng, fm2, deps=[s1toks[cd], t_om] + deps)
        return tok

    def emit_mix(h, deps):
        toks = {}
        s1 = {"k": emit_mix_s1(h, "k", "s", deps),
              "v": emit_mix_s1(h, "v", "s", deps),
              "r": emit_mix_s1(h, "r", "s", deps)}
        for p in "kvr":
            toks[p] = emit_mix_s2(h, p, s1[p], deps)
        return toks

    P.fence()
    ev_all = pr_ev[-1]
    t_rn0 = E("v", lambda v: v.tensor_scalar(
        out=rb, in0=rb, scalar1=1e-24, scalar2=None, op0=OP.max),
        deps=[nrm_ev[-1]])
    t_st = E("v", lambda v: v.tensor_scalar(
        out=tiny[0:1, 0:1], in0=small[0:1, 50:51], scalar1=LOG100,
        scalar2=None, op0=OP.min), deps=[t_small])
    t_st2 = E("s", lambda s: s.activation(out=tiny[0:1, 0:1],
                                          in_=tiny[0:1, 0:1], func=AF.Exp),
              deps=[t_st])
    t_rn1 = E("s", lambda s: (
        s.activation(out=rn, in_=rb, func=AF.Ln),
        s.activation(out=rn, in_=rn, func=AF.Exp, scale=-0.5))[-1],
        deps=[t_rn0])
    t_rn2 = E("v", lambda v: v.tensor_scalar(
        out=rn, in0=rn, scalar1=tiny[0:1, 0:1], scalar2=None, op0=OP.mult),
        deps=[t_rn1, t_st2])

    raw_mm = E("t", lambda t: [
        t.matmul(pp[2 + nt // 2][0:4, 512 * (nt % 2):512 * (nt % 2) + 512],
                 simnf[:, 4 * q2:4 * q2 + 4],
                 proj_bf[:, q2 * T + 512 * nt:q2 * T + 512 * nt + 512],
                 start=(q2 == 0), stop=(q2 == 1))
        for nt in range(4) for q2 in range(2)][-1],
        deps=[t_s6, ev_all])
    t_raw = E("s", lambda s: (
        s.copy(out=raw_sb[:, 0:1024], in_=pp[2][0:4, :]),
        s.copy(out=raw_sb[:, 1024:2048], in_=pp[3][0:4, :]))[-1],
        deps=[raw_mm])
    rnb_mm = E("t", lambda t: [
        t.matmul(pp[hh][0:4, 512 * nt:512 * nt + 512], ones64[0:1, 0:4],
                 rn[0:1, 1024 * hh + 512 * nt:1024 * hh + 512 * nt + 512],
                 start=True, stop=True)
        for hh in range(2) for nt in range(2)][-1],
        deps=[t_rn2])
    t_lg = E("v", lambda v: (
        v.tensor_mul(out=raw_sb[:, 0:1024], in0=raw_sb[:, 0:1024],
                     in1=pp[0][0:4, :]),
        v.tensor_mul(out=raw_sb[:, 1024:2048], in0=raw_sb[:, 1024:2048],
                     in1=pp[1][0:4, :]))[-1],
        deps=[rnb_mm, t_raw, t_rn0])
    lgt_mm = E("t", lambda t: [
        t.transpose(pp[2 + j // 8][:, 4 * (j % 8):4 * (j % 8) + 4],
                    raw_sb[:, 128 * j:128 * j + 128], ident[0:4, 0:4])
        for j in range(16)][-1], deps=[t_lg, hc_ev[-1], ev_all])
    t_lgc = E("v", lambda v: (
        v.tensor_copy(out=LG[:, 0:32], in_=pp[2][:, 0:32]),
        v.tensor_copy(out=LG[:, 32:64], in_=pp[3][:, 0:32]))[-1],
        deps=[lgt_mm])

    Lc = lambda i: LG[:, i:64:4]
    Dc = lambda i: GD[:, i:64:4]
    Ec = lambda i: GE[:, i:64:4]
    Sc = lambda i: SEL[:, i:64:4]
    Nc = lambda i: NB[:, i:64:4]
    dmix = [norm_tok[15], sstat_tok[15], t_small]
    s1k0 = emit_mix_s1(0, "k", "s", dmix)
    t_gm = E("v", lambda v: (
        v.tensor_max(out=GS, in0=Lc(0), in1=Lc(1)),
        v.tensor_max(out=GR, in0=Lc(2), in1=Lc(3)),
        v.tensor_copy(out=aux[:, 8:12], in_=car[:, 0:4]),
        v.tensor_max(out=GS, in0=GS, in1=GR),
        v.tensor_copy(out=aux[:, 8:12], in_=car[:, 0:4]),
        v.tensor_sub(out=Dc(0), in0=Lc(0), in1=GS),
        v.tensor_sub(out=Dc(1), in0=Lc(1), in1=GS),
        v.tensor_sub(out=Dc(2), in0=Lc(2), in1=GS),
        v.tensor_sub(out=Dc(3), in0=Lc(3), in1=GS))[-1], deps=[t_lgc])
    t_ge = E("s", lambda s: s.activation(out=GE, in_=GD, func=AF.Exp),
             deps=[t_gm])
    s1v0 = emit_mix_s1(0, "v", "s", dmix)
    t_gs = E("v", lambda v: (
        v.tensor_add(out=GS, in0=Ec(0), in1=Ec(1)),
        v.tensor_add(out=GR, in0=Ec(2), in1=Ec(3)),
        v.tensor_copy(out=aux[:, 8:12], in_=car[:, 0:4]),
        v.tensor_add(out=GS, in0=GS, in1=GR))[-1], deps=[t_ge])
    mix_tok[0] = {}
    mix_tok[0]["k"] = emit_mix_s2(0, "k", s1k0, dmix)
    t_gr = E("s", lambda s: (
        s.activation(out=GR, in_=GS, func=AF.Ln),
        s.activation(out=GR, in_=GR, func=AF.Exp, scale=-1.0))[-1],
        deps=[t_gs])
    t_sel2 = E("v", lambda v: (
        v.tensor_single_scalar(out=Sc(0), in_=Dc(0), scalar=0.0,
                               op=OP.is_equal),
        v.tensor_single_scalar(out=Sc(1), in_=Dc(1), scalar=0.0,
                               op=OP.is_equal),
        v.tensor_single_scalar(out=Sc(2), in_=Dc(2), scalar=0.0,
                               op=OP.is_equal),
        v.tensor_single_scalar(out=Sc(3), in_=Dc(3), scalar=0.0,
                               op=OP.is_equal),
        v.tensor_scalar(out=Nc(0), in0=Sc(0), scalar1=-1.0, scalar2=1.0,
                        op0=OP.mult, op1=OP.add),
        v.tensor_scalar(out=Nc(1), in0=Sc(1), scalar1=-1.0, scalar2=1.0,
                        op0=OP.mult, op1=OP.add),
        v.tensor_scalar(out=Nc(2), in0=Sc(2), scalar1=-1.0, scalar2=1.0,
                        op0=OP.mult, op1=OP.add),
        v.tensor_mul(out=Sc(1), in0=Sc(1), in1=Nc(0)),
        v.tensor_mul(out=Nc(1), in0=Nc(0), in1=Nc(1)),
        v.tensor_mul(out=GT4[:, 0:64:4], in0=Sc(0), in1=GR),
        v.tensor_mul(out=Sc(2), in0=Sc(2), in1=Nc(1)),
        v.tensor_mul(out=Nc(2), in0=Nc(1), in1=Nc(2)),
        v.tensor_mul(out=GT4[:, 1:64:4], in0=Sc(1), in1=GR),
        v.tensor_mul(out=Sc(3), in0=Sc(3), in1=Nc(2)),
        v.tensor_mul(out=GT4[:, 2:64:4], in0=Sc(2), in1=GR),
        v.tensor_mul(out=GT4[:, 3:64:4], in0=Sc(3), in1=GR))[-1],
        deps=[t_gr])
    g4_mm = E("t", lambda t: [
        t.transpose(pp[j // 8][0:4, 128 * (j % 8):128 * (j % 8) + 128],
                    GT4[:, 4 * j:4 * j + 4], ident)
        for j in range(16)][-1], deps=[t_sel2, t_lg])
    t_g4 = E("s", lambda s: (
        s.copy(out=G4sb[:, 0:1024], in_=pp[0][0:4, :]),
        s.copy(out=G4sb[:, 1024:2048], in_=pp[1][0:4, :]))[-1],
        deps=[g4_mm])
    mix_tok[0]["v"] = emit_mix_s2(0, "v", s1v0, dmix)
    s1r0 = emit_mix_s1(0, "r", "s", dmix + [t_sel2, g4_mm])
    g15_mm = E("t", lambda t: [
        t.matmul(pp[2 + nt // 2][0:15, 512 * (nt % 2):512 * (nt % 2) + 512],
                 selS, G4sb[:, 512 * nt:512 * nt + 512],
                 start=True, stop=True)
        for nt in range(4)][-1], deps=[t_g4, t_sel, t_lgc])
    t_hg = E("v", lambda v: (
        v.tensor_mul(out=hcat[:, 0:1024], in0=hcat[:, 0:1024],
                     in1=pp[2][0:15, :]),
        v.tensor_mul(out=hcat[:, 1024:2048], in0=hcat[:, 1024:2048],
                     in1=pp[3][0:15, :]))[-1],
        deps=[g15_mm, hc_ev[-1]])
    mix_tok[0]["r"] = emit_mix_s2(0, "r", s1r0, dmix + [t_sel2, g4_mm])

    if debug:
        for nm, buf, dt in (("projbf", proj_bf, F32), ("lgfm", raw_sb, F32),
                            ("LG", LG, F32), ("GT4", GT4, F32),
                            ("hg", hcat, BF16),
                            ("rn2", rn, F32)):
            d = dram_dbg(nm, list(buf.shape), dt)
            E("q", lambda q, d=d, b=buf: q.dma_start(out=d, in_=b),
              deps=[t_hg, t_sel2, t_lg], dma=True)

    P.fence()
    it_toks = []
    carps, carqs = [None] * 8, [None] * 8
    psm = {"k": pp[0], "v": pp[1], "r": pp[2]}

    def emit_late(pi, pit):
        s2 = pi % 2
        Dp, Ep = DD[s2], EE[s2]
        pit["tden"] = E("s", lambda s, Dp=Dp, Ep=Ep: s.activation(
            out=Ep, in_=Dp, func=AF.Identity, bias=1.0, scale=1.0 + 1e-9),
            deps=[pit["tro"], pit["tya"], pit["tqs"]])
        pit["t1g"] = E("g", lambda g, Dp=Dp, qb=(qbuf, qbuf2)[pi % 2]:
            g.tensor_mul(out=Dp, in0=Dp, in1=qb[:, 0:TH]),
            deps=[pit["tden"], pit["tqs"], pit["tro"]])
        pit["t20"] = E("g", lambda g, Dp=Dp: g.tensor_add(
            out=Dp, in0=Dp, in1=v_sb))

    def emit_late2(pi, pit):
        Ep = EE[pi % 2]
        pit["t21"] = E("v", lambda v, Ep=Ep, Rb=pp[2 + pi % 2]:
                       v.scalar_tensor_tensor(
                           out=Ep, in0=Rb, scalar=1.0, in1=Ep, op0=OP.add,
                           op1=OP.mult), deps=[pit["ev"]["r"], pit["tden"]])
        pit["trd"] = E("s", lambda s, Ep=Ep: (
            s.activation(out=Ep, in_=Ep, func=AF.Ln),
            s.activation(out=Ep, in_=Ep, func=AF.Exp, scale=-1.0))[-1],
            deps=[pit["t21"]])

    def emit_ty(pi, pit):
        s2 = pi % 2
        h2, c2 = pi // 8, pi % 8
        pit["ty"] = E("g", lambda g, c=c2, h=h2, Dp=DD[s2], Ep=EE[s2]:
                      g.tensor_mul(out=y3[:, c, TH * h:TH * h + TH],
                                   in0=Dp, in1=Ep),
                      deps=[pit["trd"], pit["t20"]])

    for i, (h, c) in enumerate([(h, c) for h in range(2) for c in range(CH)]):
        it = {}
        s2 = i % 2
        D_b, E_b = DD[s2], EE[s2]
        prev = it_toks[i - 1] if i >= 1 else None
        if prev:
            emit_late(i - 1, prev)
        nxt = i + 1
        if nxt < 16:
            nh, ncc = nxt // 8, nxt % 8
            if (ncc, nh) not in slice_toks:
                pv = it_toks[i - 1]["mm"] if i >= 1 else None
                deps = {p: [pv[p]] for p in "kvr"} if pv else {}
                slice_toks[(ncc, nh)] = load_slices(ncc, nh, deps)
        if h == 1 and c == 0:
            mix_tok[1] = emit_mix(1, [it_toks[7]["mm"][p] for p in "kvr"])
        sl = slice_toks[(c, h)]
        it["mm"] = {}
        rb_i = pp[2 + i % 2]
        for p in "kvr":
            def fmm(t, p=p, c=c, rb_i=rb_i):
                tgt = rb_i if p == "r" else psm[p]
                last = None
                for nt in range(2):
                    for cd in range(CH):
                        last = t.matmul(
                            tgt[:, 512 * nt:512 * nt + 512],
                            ws[p][c % 2][:, 128 * cd:128 * cd + 128],
                            mixh[p][:, TH * cd + 512 * nt:
                                     TH * cd + 512 * nt + 512],
                            start=(cd == 0), stop=(cd == 7))
                return last
            pgate = None
            if prev:
                pgate = {"k": prev["t13"], "v": prev["ev"]["v"],
                         "r": it_toks[i - 2]["t21"] if i >= 2 else None}[p]
            it["mm"][p] = E("t", fmm, deps=[sl[p], mix_tok[h][p], pgate])
        it["ev"] = {}
        war_q = it_toks[i - 2]["t1g"] if i >= 2 else None
        if h == 0:
            E("v", lambda v: v.memset(pbuf[:, 0:1], -1e30))
        else:
            E("v", lambda v, c=c: v.tensor_copy(out=pbuf[:, 0:1],
                                                in_=car[:, c:c + 1]),
              deps=[carps[c]])
        tps = E("v", lambda v, c=c: v.tensor_tensor_scan(
            out=pbuf[:, 1:TH + 1], data0=wcol(c).broadcast_to([128, TH]),
            data1=psm["k"], initial=pbuf[:, 0:1], op0=OP.add, op1=OP.max),
            deps=[it["mm"]["k"],
                  carps[i - 1] if 1 <= i <= 8 else None])
        t7 = E("v", lambda v: v.tensor_sub(out=B_b, in0=psm["k"],
                                           in1=pbuf[:, 1:TH + 1]),
               deps=[prev["tqs"] if prev else None])
        tb = E("s", lambda s: s.activation(out=B_b, in_=B_b, func=AF.Exp),
               deps=[t7])
        it["ev"]["v"] = E("s", lambda s: s.copy(out=v_sb, in_=psm["v"]),
                          deps=[it["mm"]["v"], prev["t20"] if prev else None])
        t4 = E("v", lambda v, c=c: v.scalar_tensor_tensor(
            out=A_b, in0=pbuf[:, 0:TH], scalar=wcol(c),
            in1=pbuf[:, 1:TH + 1], op0=OP.add, op1=OP.subtract),
            deps=[prev["tqs"] if prev else None])
        ta = E("s", lambda s: s.activation(out=A_b, in_=A_b, func=AF.Exp),
               deps=[t4])
        it["ev"]["r"] = E("s", lambda s, rb_i=rb_i: s.activation(
            out=rb_i, in_=rb_i, func=AF.Exp, scale=-1.0),
            deps=[it["mm"]["r"]])
        qb = (qbuf, qbuf2)[i % 2]
        if h == 0:
            E("v", lambda v, qb=qb: v.memset(qb[:, 0:1], 0.0), deps=[war_q])
        else:
            E("v", lambda v, c=c, qb=qb: v.tensor_copy(
                out=qb[:, 0:1], in_=car[:, 8 + c:9 + c]),
              deps=[war_q, carqs[c]])
        tya = it_toks[i - 2]["ty"] if i >= 2 else None
        it["t13"] = E("v", lambda v, c=c, E_b=E_b: v.scalar_tensor_tensor(
            out=E_b, in0=pbuf[:, 0:TH], scalar=ucol(c), in1=psm["k"],
            op0=OP.subtract, op1=OP.subtract), deps=[tya])
        it["tdc"] = E("v", lambda v, D_b=D_b, E_b=E_b: v.tensor_scalar(
            out=D_b, in0=E_b, scalar1=30.0, scalar2=0.0,
            op0=OP.min, op1=OP.add))
        tbv = E("v", lambda v, E_b=E_b: v.tensor_mul(
            out=E_b, in0=B_b, in1=v_sb), deps=[tb, it["ev"]["v"]])
        if i >= 1 and i <= 8:
            carqs[i - 1] = E("s", lambda s, i=i, qp=(qbuf, qbuf2)[(i - 1) % 2]:
                s.copy(out=car[:, 7 + i:8 + i], in_=qp[:, TH:TH + 1]),
                deps=[it_toks[i - 1]["tqs"]])
        if h == 0:
            carps[c] = E("s", lambda s, c=c: s.copy(
                out=car[:, c:c + 1], in_=pbuf[:, TH:TH + 1]), deps=[tps])
        it["tro"] = E("s", lambda s, D_b=D_b: s.activation(
            out=D_b, in_=D_b, func=AF.Exp), deps=[it["tdc"]])
        it["tya"] = tya
        it["tqs"] = E("v", lambda v, qb=qb, E_b=E_b: v.tensor_tensor_scan(
            out=qb[:, 1:TH + 1], data0=A_b, data1=E_b,
            initial=qb[:, 0:1], op0=OP.mult, op1=OP.add),
            deps=[ta, tbv, war_q,
                  carqs[i - 2] if 2 <= i <= 9 else None])
        if prev:
            emit_late2(i - 1, prev)
            emit_ty(i - 1, prev)
        if debug and i == 0:
            for nm, buf in (("v0", v_sb),
                            ("p0", pbuf), ("q0", qbuf)):
                d = dram_dbg(nm, list(buf.shape), F32)
                E("q", lambda q, d=d, b=buf: q.dma_start(out=d, in_=b),
                  deps=[it["tqs"]], dma=True)
        it_toks.append(it)
    emit_late(15, it_toks[15])
    emit_late2(15, it_toks[15])
    emit_ty(15, it_toks[15])

    if debug:
        d = dram_dbg("y", [128, CH * T], BF16)
        E("q", lambda q, d=d: q.dma_start(out=d, in_=y),
          deps=[it_toks[-1]["ty"]], dma=True)
        for p in "kvr":
            d = dram_dbg("mix" + p, [128, CH * TH], BF16)
            E("q", lambda q, d=d, b=mixh[p]: q.dma_start(out=d, in_=b),
              deps=[it_toks[-1]["ty"]], dma=True)

    it15 = it_toks[15]
    t_tab0 = E("s", lambda s: s.add_instruction(mybir.InstLoadActFuncSet(
        name=s.bass.get_next_instruction_name(), act_func_set_id=0, ins=[],
        outs=[])), deps=[it15["trd"]])
    store_toks = []
    j_toks = [None] * NTOK
    for j in range(NTOK):
        prev2 = j_toks[j - 2] if j >= 2 else None
        tx = E("q", lambda q, j=j: q.dma_start(out=xt5[j % 2],
                                               in_=x_t[:, j, :]),
               deps=[prev2["out"] if prev2 else None,
                     it15["mm"]["k"] if j < 2 else None], dma=True)
        pss = pp[j % 2]
        psf = pp[2 + j % 2]
        yh = it_toks[7]["ty"] if j < 8 else it15["ty"]
        smm = E("t", lambda t, j=j, pss=pss: [
            t.matmul(pss[:, 512 * nt:512 * nt + 512],
                     y3[:, ce, 128 * j:128 * j + 128],
                     wot3[:, ce, 512 * nt:512 * nt + 512],
                     start=(ce == 0), stop=(ce == 7))
            for ce in range(CH) for nt in range(2)][-1],
            deps=[t_wot, prev2["sev"] if prev2 else None,
                  yh if j in (0, 1, 8, 9) else None,
                  it15["t13"] if j == 0 else None,
                  it15["ev"]["v"] if j == 1 else None])
        sev = E("s", lambda s, j=j, pss=pss: s.copy(out=OT[j % 2], in_=pss),
                deps=[smm, store_toks[j - 2] if j >= 2 else None])

        def ffm(t, j=j, psf=psf):
            for ce in range(CH):
                for nt in range(2):
                    t.matmul(psf[:, 512 * nt:512 * nt + 512],
                             xn3[:, ce, 128 * j:128 * j + 128],
                             wt3[:, ce, 512 * nt:512 * nt + 512],
                             start=(ce == 0), stop=False)
            last = None
            for nt in range(2):
                last = t.matmul(psf[:, 512 * nt:512 * nt + 512],
                                hcat[:, 128 * j:128 * j + 128],
                                wut_sb[:, 512 * nt:512 * nt + 512],
                                start=False, stop=True)
            return last
        fmm = E("t", ffm, deps=[t_wt, t_wut,
                                prev2["ep"] if prev2 else None,
                                prev2["sev2"] if prev2 else None,
                                it_toks[14]["t21"] if j == 0 else None,
                                it15["t21"] if j == 1 else None,
                                ph0_v if j == 0 else None,
                                ph0_s if j == 0 else None,
                                t_hg if j == 0 else None])
        tep = E("s", lambda s, j=j, psf=psf: s.activation(
            out=EP[j % 2], in_=psf, func=AF.Tanh, scale=0.5),
            deps=[fmm, t_tab0])
        sev2 = E("s", lambda s, j=j, psf=psf: s.copy(out=GZ[j % 2], in_=psf),
                 deps=[fmm, prev2["out"] if prev2 else None])
        tz1 = E("v", lambda v, j=j: v.scalar_tensor_tensor(
            out=GZ[j % 2], in0=EP[j % 2], scalar=1.0, in1=GZ[j % 2],
            op0=OP.add, op1=OP.mult), deps=[tep, sev2])
        tz = E("v", lambda v, j=j: v.tensor_mul(
            out=GZ[j % 2], in0=GZ[j % 2], in1=OT[j % 2]), deps=[sev])
        tout = E("v", lambda v, j=j: v.scalar_tensor_tensor(
            out=OT[j % 2], in0=GZ[j % 2], scalar=0.5, in1=xt5[j % 2],
            op0=OP.mult, op1=OP.add),
            deps=[tx])
        st = E("q", lambda q, j=j: q.dma_start(out=out_t[:, j, :],
                                               in_=OT[j % 2]),
               deps=[tout], dma=True)
        store_toks.append(st)
        j_toks[j] = dict(out=tout, z=tz, z1=tz1, ep=tep, sev=sev, sev2=sev2)

    all_q_dmas = [("q", i, True) for i, op in enumerate(P.ops["q"]) if op[2]]
    all_g_dmas = [("g", i, True) for i, op in enumerate(P.ops["g"]) if op[2]]
    E("q", lambda q: None, deps=store_toks + all_q_dmas + all_g_dmas)

    P.resolve()
    with ctx:
        from contextlib import ExitStack as ES2
        with ES2() as sem_ctx:
            sems = {}
            for e in Plan.ENGINES:
                sems[(e, False)] = sem_ctx.enter_context(
                    nc.semaphore(f"sem_{e}"))
                for j in range(Plan.NDMA.get(e, 1)):
                    sems[(e, "dma", j)] = sem_ctx.enter_context(
                        nc.semaphore(f"semd_{e}{j}"))
            for e in ("g", "s", "v"):
                sems[(e, "ch")] = sem_ctx.enter_context(
                    nc.semaphore(f"ch_{e}"))
            with nc.Block() as block:
                @block.sync
                def _(q):
                    P.replay("q", q, sems, chain_counts=chain_counts)

                @block.gpsimd
                def _(g):
                    P.replay("g", g, sems, sems[("g", "ch")],
                             chain_counts=chain_counts)

                @block.tensor
                def _(t):
                    P.replay("t", t, sems, chain_counts=chain_counts)

                @block.scalar
                def _(s):
                    P.replay("s", s, sems, sems[("s", "ch")],
                             chain_counts=chain_counts)

                @block.vector
                def _(v):
                    P.replay("v", v, sems, sems[("v", "ch")],
                             chain_counts=chain_counts)
    return nc, dbg, P


def kernel(**inputs):
    from concourse.bass_utils import run_bass_kernel_spmd
    if "nc" not in _CACHE:
        _CACHE["nc"], _ = build_program(debug=False)
    nc = _CACHE["nc"]
    shared = _prep_shared(inputs)
    x = np.asarray(inputs["L0_attn_x"])
    in_maps = []
    for b in range(8):
        m = dict(shared)
        m["x"] = np.ascontiguousarray(x[b])
        in_maps.append(m)
    res = run_bass_kernel_spmd(nc, in_maps, core_ids=list(range(8)))
    out = np.stack([np.asarray(res.results[b]["out"]) for b in range(8)],
                   axis=0)
    return out.astype(np.float32)


# version 54
# speedup vs baseline: 1.0195x; 1.0195x over previous
import numpy as np
import concourse.bass as bass
import concourse.mybir as mybir

F32 = mybir.dt.float32
BF16 = mybir.dt.bfloat16
U8 = mybir.dt.uint8
OP = mybir.AluOpType
AF = mybir.ActivationFunctionType

T = 2048
D = 1024
CH = 8
NTOK = 16
TH = 1024
NEG30 = -30.0
LN_EPS = 1e-5
LOG100 = float(np.log(100.0))
LN_1EM9 = float(np.log(1e-9))

_CACHE = {}
_TRACE_EMIT = False


class Plan:

    ENGINES = ("q", "g", "t", "s", "v")

    def __init__(self):
        self.ops = {e: [] for e in self.ENGINES}
        self.marked = set()
        self.fence_pending = {}
        self.chain_cum = {}
        self.src = {}

    def emit(self, eng, fn, deps=(), dma=False):
        deps = [d for d in deps if d is not None]
        deps = self.fence_pending.pop(eng, []) + deps
        for d in deps:
            self.marked.add((d[0], d[1]))
        idx = len(self.ops[eng])
        self.ops[eng].append([fn, deps, dma, False])
        if _TRACE_EMIT:
            import traceback
            fr = traceback.extract_stack(limit=3)[0]
            self.src[(eng, idx)] = f"{fr.lineno}"
        return (eng, idx, dma)

    def fence(self):
        last = {}
        for e in self.ENGINES:
            if self.ops[e]:
                last[e] = (e, len(self.ops[e]) - 1, self.ops[e][-1][2])
        for e in self.ENGINES:
            toks = [t for o, t in last.items() if o != e]
            self.fence_pending.setdefault(e, []).extend(toks)

    NDMA = {"q": 8, "g": 4}

    def resolve(self):
        self.cum = {e: [0] * len(self.ops[e]) for e in self.ENGINES}
        self.dma_slot = {}
        for e in self.ENGINES:
            c_cmp = 0
            k = 0
            per_slot = [0] * self.NDMA.get(e, 1)
            for i, op in enumerate(self.ops[e]):
                if op[2] or (e, i) in self.marked:
                    op[3] = True
                    if not op[2]:
                        c_cmp += 1
                if op[2]:
                    j = k % len(per_slot)
                    per_slot[j] += 16
                    self.dma_slot[(e, i)] = (j, per_slot[j])
                    k += 1
                    self.cum[e][i] = per_slot[j]
                else:
                    self.cum[e][i] = c_cmp

    CHAINED = ("g", "s", "v")

    def replay(self, eng_name, eng_obj, sems, chain_sem=None, lag=1,
               chain_counts=None):
        if chain_sem is not None:
            eng_obj = _ChainEng(eng_obj, chain_sem, lag)
        self.chain_cum[eng_name] = [0] * len(self.ops[eng_name])
        maxw = {}
        for i, (fn, deps, dma, inc) in enumerate(self.ops[eng_name]):
            for d in deps:
                de, di, ddma = d[0], d[1], d[2]
                hard_d = len(d) == 4 and d[3]
                if ddma:
                    j, val = self.dma_slot[(de, di)]
                    sem = sems[(de, "dma", j)]
                elif de in self.CHAINED:
                    if de == eng_name:
                        continue
                    sem = sems[(de, "ch")]
                    val = chain_counts[de][di] if chain_counts else 0
                else:
                    if de == eng_name and not hard_d:
                        continue
                    sem = sems[(de, False)]
                    val = self.cum[de][di]
                if maxw.get(id(sem), -1) >= val:
                    continue
                maxw[id(sem)] = val
                eng_obj.wait_ge(sem, val)
            instr = fn(eng_obj)
            if chain_sem is not None:
                self.chain_cum[eng_name][i] = eng_obj._n
            if dma:
                j, val = self.dma_slot[(eng_name, i)]
                sem = sems[(eng_name, "dma", j)]
                if val > 16:
                    instr._wait_ge(sem, val - 16)
                instr.then_inc(sem, 16)
            elif inc and chain_sem is None:
                assert instr is not None, f"marked op {eng_name}[{i}] is None"
                instr.then_inc(sems[(eng_name, False)], 1)


class _ChainEng:

    _SKIP = frozenset((
        "wait_ge", "wait_op", "sem_inc", "sem_clear", "maybe_wait_ge",
        "dma_start", "add_instruction", "preload_act_table",
    ))

    def __init__(self, eng, sem, lag):
        self._eng = eng
        self._sem = sem
        self._lag = lag
        self._n = 0

    def __getattr__(self, name):
        f = getattr(self._eng, name)
        if name in self._SKIP or not callable(f):
            return f

        def wrap(*a, **k):
            r = f(*a, **k)
            if hasattr(r, "then_inc"):
                n = self._n
                w = n + 1 - self._lag
                if w > 0:
                    r._wait_ge(self._sem, w)
                r.then_inc(self._sem, 1)
                self._n = n + 1
            return r

        return wrap


def hard(tok):
    return (tok[0], tok[1], tok[2], True)


def _fm(W):
    R, C = W.shape
    return np.ascontiguousarray(
        W.reshape(R // 128, 128, C).transpose(1, 0, 2).reshape(128, -1))


def _col8(v):
    return np.ascontiguousarray(v.reshape(8, 128).T)


def _prep_shared(inp):
    wd = np.concatenate([inp[f"L0_ffn_Wd{i}"] for i in range(4)], axis=0)
    wu = np.concatenate([inp[f"L0_ffn_Wu{i}"] for i in range(4)], axis=1)
    small = np.zeros((128, 324), np.float32)
    small[:, 0:8] = _col8(inp["L0_attn_time_mix_k"])
    small[:, 8:16] = _col8(inp["L0_attn_time_mix_v"])
    small[:, 16:24] = _col8(inp["L0_attn_time_mix_r"])
    small[:, 24:32] = _col8(inp["L0_attn_time_decay"])
    small[:, 32:40] = _col8(inp["L0_attn_time_first"])
    small[:, 40:42] = np.ascontiguousarray(inp["L0_ffn_bp"].reshape(2, 128).T)
    small[:, 42:50] = np.ascontiguousarray(
        np.asarray(inp["L0_ffn_sim"]).reshape(2, 128, 4)
        .transpose(1, 0, 2).reshape(128, 8))
    small[:, 50] = float(np.asarray(inp["L0_ffn_temp"]).reshape(-1)[0])
    small[:, 51] = 1.0
    sel = np.zeros((4, 15), np.float32)
    sel[0, 0] = sel[1, 1:3] = sel[2, 3:7] = sel[3, 7:15] = 1.0
    small[32:36, 52:67] = sel
    small[:, 67] = np.log(1e-9)
    small[0, 68:196] = 1.0
    small[64, 68:196] = 1.0
    small[:, 196:324] = np.eye(128, dtype=np.float32)
    ms = [_fm(np.ascontiguousarray(np.asarray(inp["L0_attn_W_" + p]).T))
          .reshape(128, 8, 8, 128) for p in "kvr"]
    return dict(
        wkvr=np.ascontiguousarray(
            np.stack(ms, axis=1).transpose(0, 3, 1, 2, 4)
            .reshape(128, 3 * CH * D)),
        wot=_fm(np.ascontiguousarray(np.asarray(inp["L0_attn_W_o"]).T)),
        wt=_fm(np.ascontiguousarray(np.asarray(inp["L0_ffn_W"]).T)),
        wpt=_fm(np.ascontiguousarray(np.asarray(inp["L0_ffn_Wp"]).T)),
        wdt=_fm(np.ascontiguousarray(np.asarray(wd).T)),
        wut=np.ascontiguousarray(np.asarray(wu).T),
        small=small)


def build_program(debug=False):
    _, _, p1 = _build_program(debug, None)
    nc, dbg, _ = _build_program(debug, p1.chain_cum)
    return nc, dbg


def _build_program(debug=False, chain_counts=None):
    nc = bass.Bass("TRN2", target_bir_lowering=False)
    P = Plan()
    E = P.emit

    x_h = nc.dram_tensor("x", [T, D], F32, kind="ExternalInput")
    out_h = nc.dram_tensor("out", [T, D], F32, kind="ExternalOutput")
    din = {n: nc.dram_tensor(n, s, F32, kind="ExternalInput") for n, s in [
        ("wkvr", [128, 3 * CH * D]),
        ("wot", [128, CH * D]), ("wt", [128, CH * D]), ("wpt", [128, CH * 256]),
        ("wdt", [128, CH * 15]), ("wut", [15, D]), ("small", [128, 324])]}

    dbg = {}

    def dram_dbg(name, shape, dtype):
        ap = nc.dram_tensor("dbg_" + name, shape, dtype,
                            kind="ExternalOutput").ap()
        dbg[name] = ap
        return ap

    x_t = x_h.ap().rearrange("(j p) d -> p j d", p=128)
    out_t = out_h.ap().rearrange("(j p) d -> p j d", p=128)

    from contextlib import ExitStack
    ctx = ExitStack()
    sb = lambda n, s, d: ctx.enter_context(nc.sbuf_tensor(n, s, d)).ap()

    xn = sb("xn", [128, CH * T], BF16)
    y = sb("y", [128, CH * T], BF16)
    ws_all = [sb(f"ws_all{i}", [128, 3 * CH * 128], BF16) for i in range(2)]
    ws = {p: [ws_all[i][:, pi * CH * 128:(pi + 1) * CH * 128]
              for i in range(2)] for pi, p in enumerate("kvr")}
    hcat = sb("hcat", [15, T], BF16)
    wut_sb = sb("wut_sb", [15, D], BF16)
    wdt_sb = sb("wdt_sb", [128, CH * 15], BF16)
    small = sb("small_sb", [128, 324], F32)

    stat = sb("stat", [128, 112], F32)
    car = sb("car", [128, 16], F32)
    aux = sb("aux", [128, 48], F32)
    tiny = sb("tiny", [4, 64], F32)
    selS = small[32:36, 52:67]
    ones1 = small[0:1, 68:196]
    ones64 = small[64:65, 68:196]
    onesc = small[:, 51:52]
    simnf = aux[:, 0:8]

    AR = 124976
    arena = ctx.enter_context(nc.sbuf_tensor("arena", [128, AR], U8)).ap()

    def view(off, nbytes, dtype, cols):
        ap = arena[:, off:off + nbytes].bitcast(dtype)
        assert ap.shape[1] == cols, (ap.shape, cols)
        return ap

    xt0 = [view(i * 4096, 4096, F32, 1024) for i in range(16)]
    xn_tm = [view(65536 + i * 4096, 4096, F32, 1024) for i in range(4)]
    wot_sb = view(0, 16384, BF16, CH * D)
    wt_sb = view(16384, 16384, BF16, CH * D)
    VP = view(81920, 8192, F32, 2048)
    raw_sb = VP[0:4, :]
    G4sb = VP[32:36, :]
    rn = VP[64:65, :]
    rb = VP[96:97, :]
    proj_bf = view(90112, 16384, F32, 2 * T)
    proj2 = view(106496, 4096, F32, T // 2)
    gb = 73728
    LG = view(gb, 256, F32, 64)
    GD = view(gb + 256, 256, F32, 64)
    GE = view(gb + 512, 256, F32, 64)
    SEL = view(gb + 768, 256, F32, 64)
    NB = view(gb + 1024, 256, F32, 64)
    GT4 = view(gb + 1280, 256, F32, 64)
    GS = view(gb + 1536, 64, F32, 16)
    GR = view(gb + 1600, 64, F32, 16)
    PH3 = 32768
    mixh = {p: view(PH3 + i * 16384, 16384, BF16, CH * TH)
            for i, p in enumerate("kvr")}

    sc_off = PH3 + 51200
    v_sb = view(sc_off + 1 * 4096, 4096, F32, TH)
    A_b = view(sc_off + 3 * 4096, 4096, F32, TH)
    B_b = view(sc_off + 4 * 4096, 4096, F32, TH)
    DD = [view(sc_off + 0 * 4096, 4096, F32, TH), view(112672, 4096, F32, TH)]
    EE = [view(sc_off + 2 * 4096, 4096, F32, TH), view(116768, 4096, F32, TH)]
    pbuf = view(sc_off + 20480, 4112, F32, 1028)
    qbuf = view(sc_off + 24592, 4112, F32, 1028)
    qbuf2 = view(120864, 4112, F32, 1028)
    PH5 = 32768
    xt5 = [view(PH5 + i * 4096, 4096, F32, 1024) for i in range(2)]
    wpt_sb = view(110592, 4096, BF16, CH * 256)
    OT = [view(PH5 + 8192 + i * 4096, 4096, F32, 1024) for i in range(2)]
    EP = [view(PH5 + 16384 + i * 4096, 4096, F32, 1024) for i in range(2)]
    GZ = [view(PH5 + 24576 + i * 4096, 4096, F32, 1024) for i in range(2)]

    pp = [ctx.enter_context(nc.psum_tensor(f"pp{i}", [128, 1024], F32)).ap()
          for i in range(4)]

    xn3 = xn.rearrange("p (c t) -> p c t", c=CH)
    y3 = y.rearrange("p (c t) -> p c t", c=CH)
    wot3 = wot_sb.rearrange("p (c t) -> p c t", c=CH)
    wt3 = wt_sb.rearrange("p (c t) -> p c t", c=CH)
    wsl_all = din["wkvr"].ap().rearrange("p (s r c e) -> p s r c e",
                                         s=8, r=3, c=CH)

    tmcol = {"k": lambda c: small[:, 0 + c:1 + c],
             "v": lambda c: small[:, 8 + c:9 + c],
             "r": lambda c: small[:, 16 + c:17 + c]}
    wcol = lambda c: small[:, 24 + c:25 + c]
    ucol = lambda c: small[:, 32 + c:33 + c]

    t_small = E("q", lambda q: q.dma_start(out=small, in_=din["small"].ap()),
                dma=True)
    t_wpt = E("g", lambda g: g.dma_start(out=wpt_sb, in_=din["wpt"].ap()),
              dma=True)
    t_wdt = E("g", lambda g: g.dma_start(out=wdt_sb, in_=din["wdt"].ap()),
              dma=True)
    t_wut = E("g", lambda g: g.dma_start(out=wut_sb, in_=din["wut"].ap()),
              dma=True)

    def load_slices(c, h, deps):
        ktok = E("g", lambda g, c=c: g.dma_start(
            out=ws_all[c % 2][:, 0:CH * 128].rearrange(
                "p (c e) -> p c e", c=CH),
            in_=wsl_all[:, c, 0, :, :]),
            deps=deps.get("k", []), dma=True)
        vrtok = E("g", lambda g, c=c: g.dma_start(
            out=ws_all[c % 2][:, CH * 128:3 * CH * 128].rearrange(
                "p (r c e) -> p r c e", r=2, c=CH),
            in_=wsl_all[:, c, 1:3, :, :]),
            deps=deps.get("v", []) + deps.get("r", []), dma=True)
        return {"k": ktok, "v": vrtok, "r": vrtok}

    ident = small[:, 196:324]
    t_ident = t_small

    t_sel = t_small

    slice_toks = {(0, 0): load_slices(0, 0, {}), (1, 0): load_slices(1, 0, {})}
    E("v", lambda v: v.memset(car, 0.0))
    t_om = E("v", lambda v: v.tensor_scalar(
        out=aux[:, 24:48], in0=small[:, 0:24], scalar1=-1.0, scalar2=1.0,
        op0=OP.mult, op1=OP.add), deps=[t_small])

    t_tab6 = E("s", lambda s: s.add_instruction(mybir.InstLoadActFuncSet(
        name=s.bass.get_next_instruction_name(), act_func_set_id=6, ins=[],
        outs=[])))
    t_s2 = E("v", lambda v: v.tensor_mul(out=stat[:, 64:72],
                                         in0=small[:, 42:50],
                                         in1=small[:, 42:50]),
             deps=[t_small])
    sim_mm = E("t", lambda t: t.matmul(pp[2][0:1, 0:8], onesc,
                                       stat[:, 64:72], start=True, stop=True),
               deps=[t_s2])
    t_s3 = E("s", lambda s: s.copy(out=tiny[0:1, 8:16], in_=pp[2][0:1, 0:8]),
             deps=[sim_mm, t_tab6])
    t_s4a = E("v", lambda v: v.tensor_add(
        out=tiny[0:1, 16:20], in0=tiny[0:1, 8:12], in1=tiny[0:1, 12:16]),
        deps=[t_s3])
    t_s4 = E("v", lambda v: v.tensor_scalar(
        out=tiny[0:1, 16:20], in0=tiny[0:1, 16:20],
        scalar1=1e-24, scalar2=None, op0=OP.max),
        deps=[hard(t_s4a)])
    t_s5a = E("s", lambda s: s.activation(
        out=tiny[0:1, 20:24], in_=tiny[0:1, 16:20], func=AF.Ln), deps=[t_s4])
    t_s5 = E("s", lambda s: s.activation(
        out=tiny[0:1, 20:24], in_=tiny[0:1, 20:24], func=AF.Exp,
        scale=-0.5), deps=[hard(t_s5a)])
    simb_mm = E("t", lambda t: t.matmul(pp[2][0:128, 0:4], ones1,
                                        tiny[0:1, 20:24], start=True,
                                        stop=True), deps=[t_s5, t_s3])
    t_s6 = E("v", lambda v: (
        v.tensor_mul(out=simnf[:, 0:4], in0=small[:, 42:46],
                     in1=pp[2][0:128, 0:4]),
        v.tensor_mul(out=simnf[:, 4:8], in0=small[:, 46:50],
                     in1=pp[2][0:128, 0:4]))[-1],
        deps=[simb_mm])

    xload = [None] * NTOK
    norm_tok = [None] * NTOK
    sstat_tok = [None] * NTOK
    vstat_tok = [None] * NTOK
    tr_tok = [None] * NTOK
    evac_tok = []
    stat_scr_v = view(114688, 2048, F32, 512)
    stat_scr_s = view(116736, 2048, F32, 512)
    xb4 = [view(16384 * i, 16384, F32, 4096) for i in range(4)]
    pr_mm, pr_ev, h_mm, hc_ev = [], [], [], []
    nrm_mm, nrm_ev = [], []

    def emit_wave(w0, extra_deps):
        sl = lambda base, w0=w0: stat[:, base + w0:base + w0 + 4]
        t_v0 = E("v", lambda v, sl=sl, w0=w0: (
            v.tensor_add(out=sl(16), in0=sl(16),
                         in1=stat[:, 96 + w0:100 + w0]),
            v.tensor_add(out=sl(0), in0=sl(0),
                         in1=stat[:, 80 + w0:84 + w0]))[-1],
            deps=[sstat_tok[w0 + 3]] + extra_deps)
        t_v1 = E("v", lambda v, sl=sl: (
            v.tensor_scalar(out=sl(32), in0=sl(0), scalar1=1.0 / D,
                            scalar2=None, op0=OP.mult),
            v.tensor_scalar(out=sl(48), in0=sl(16), scalar1=1.0 / D,
                            scalar2=LN_EPS, op0=OP.mult, op1=OP.add),
            v.tensor_mul(out=sl(0), in0=sl(32), in1=sl(32)),
            v.tensor_copy(out=aux[:, 8:12], in_=car[:, 0:4]),
            v.tensor_sub(out=sl(48), in0=sl(48), in1=sl(0)))[-1],
            deps=[t_v0])
        t_r0 = E("s", lambda s, sl=sl: (
            s.activation(out=sl(64), in_=sl(48), func=AF.Ln),
            s.activation(out=sl(64), in_=sl(64), func=AF.Exp,
                         scale=-0.5))[-1],
            deps=[t_v1])
        for jj in range(w0, w0 + 4):
            norm_tok[jj] = E("v", lambda v, jj=jj: v.tensor_scalar(
                out=xn_tm[jj % 4], in0=xt0[jj],
                scalar1=stat[:, 32 + jj:33 + jj],
                scalar2=stat[:, 64 + jj:65 + jj],
                op0=OP.subtract, op1=OP.mult),
                deps=[t_r0, tr_tok[jj - 4] if jj >= 4 else None])
        for jj in range(w0, w0 + 4):
            tr_tok[jj] = E("t", lambda t, jj=jj: [
                t.transpose(pp[jj % 2][:, 128 * i:128 * i + 128],
                            xn_tm[jj % 4][:, 128 * i:128 * i + 128], ident)
                for i in range(8)][-1],
                deps=[norm_tok[jj], t_ident] +
                     ([evac_tok[2 * (jj - 2)], evac_tok[2 * (jj - 2) + 1]]
                      if jj >= 2 else []))
            for half in (0, 1):
                srcp = pp[jj % 2][:, 512 * half:512 * half + 512].rearrange(
                    "p (c t) -> p c t", c=4)
                dst = xn3[:, 4 * half:4 * half + 4, 128 * jj:128 * jj + 128]
                if half == 0:
                    tok = E("v", lambda v, srcp=srcp, dst=dst:
                            v.tensor_copy(out=dst, in_=srcp),
                            deps=[tr_tok[jj]])
                else:
                    tok = E("s", lambda s, srcp=srcp, dst=dst:
                            s.copy(out=dst, in_=srcp),
                            deps=[tr_tok[jj]])
                evac_tok.append(tok)
        nt = w0 // 4
        xn_done = [evac_tok[-1], evac_tok[-2]]
        for q2 in range(2):
            def fproj(t, nt=nt, q2=q2):
                last = None
                for cd in range(CH):
                    last = t.matmul(
                        pp[2][:, 512 * q2:512 * q2 + 512],
                        wpt_sb[:, cd * 256 + q2 * 128:cd * 256 + q2 * 128 + 128],
                        xn3[:, cd, 512 * nt:512 * nt + 512],
                        start=(cd == 0), stop=(cd == 7))
                return last
            pp2_free = pr_ev[-1] if pr_ev else t_s6
            pr_mm.append(E("t", fproj, deps=[t_wpt, pp2_free] + xn_done))
            col = q2 * T + 512 * nt
            ps = pp[2][:, 512 * q2:512 * q2 + 512]
            pr_ev.append(E("s", lambda s, ps=ps, col=col, q2=q2: (
                s.activation(out=proj_bf[:, col:col + 512], in_=ps,
                             func=AF.Identity, bias=small[:, 40 + q2:41 + q2]),
                s.activation(out=proj2[:, 512 * q2:512 * q2 + 512], in_=ps,
                             func=AF.Square, bias=small[:, 40 + q2:41 + q2]))[-1],
                deps=[pr_mm[-1], t_small, nrm_mm[-1] if nrm_mm else None]))
        nrm_mm.append(E("t", lambda t: [
            t.matmul(pp[3][0:1, 512:1024], onesc,
                     proj2[:, 512 * q2:512 * q2 + 512],
                     start=(q2 == 0), stop=(q2 == 1))
            for q2 in range(2)][-1],
            deps=[pr_ev[-1], pr_ev[-2], t_sel,
                  nrm_ev[-1] if nrm_ev else None]))
        nrm_ev.append(E("s", lambda s, nt=nt: s.copy(
            out=rb[:, 512 * nt:512 * nt + 512], in_=pp[3][0:1, 512:1024]),
            deps=[nrm_mm[-1]]))
        def fh(t, nt=nt):
            last = None
            for cd in range(CH):
                last = t.matmul(pp[3][0:15, 0:512],
                                wdt_sb[:, 15 * cd:15 * cd + 15],
                                xn3[:, cd, 512 * nt:512 * nt + 512],
                                start=(cd == 0), stop=(cd == 7))
            return last
        h_mm.append(E("t", fh, deps=[t_wdt, hc_ev[-1] if hc_ev else None]
                      + xn_done))
        hc_ev.append(E("v", lambda v, nt=nt: v.tensor_copy(
            out=hcat[:, 512 * nt:512 * nt + 512], in_=pp[3][0:15, 0:512]),
            deps=[h_mm[-1]]))

    xbat = [(0, 2), (2, 2), (4, 4), (8, 2), (10, 2), (12, 1), (13, 1),
            (14, 1), (15, 1)]
    for j0, nb in xbat:
        bt = E("q", lambda q, j0=j0, nb=nb: q.dma_start(
            out=view(16384 * (j0 // 4) + 4096 * (j0 % 4), 4096 * nb, F32,
                     1024 * nb),
            in_=x_t[:, j0:j0 + nb, :]), dma=True)
        for k in range(nb):
            xload[j0 + k] = bt
    for j in range(NTOK):
        if False:
            pass
        E("v", lambda v, j=j: v.tensor_scalar(
            out=stat_scr_v, in0=xt0[j][:, 0:512], scalar1=0.0, scalar2=None,
            op0=OP.add, op1=OP.add, accum_out=stat[:, j:j + 1]),
            deps=[xload[j]])
        vstat_tok[j] = E("v", lambda v, j=j: v.tensor_scalar(
            out=stat_scr_v, in0=xt0[j][:, 512:1024], scalar1=0.0,
            scalar2=None, op0=OP.add, op1=OP.add,
            accum_out=stat[:, 80 + j:81 + j]),
            deps=[xload[j]])
        E("s", lambda s, j=j: s.activation(
            out=stat_scr_s, in_=xt0[j][:, 0:512], func=AF.Square,
            accum_out=stat[:, 16 + j:17 + j]),
            deps=[xload[j], t_tab6])
        sstat_tok[j] = E("s", lambda s, j=j: s.activation(
            out=stat_scr_s, in_=xt0[j][:, 512:1024], func=AF.Square,
            accum_out=stat[:, 96 + j:97 + j]),
            deps=[xload[j], t_tab6])
        if j in (4, 8, 12):
            emit_wave(j - 4, [])
    emit_wave(12, [])
    ph0_v = evac_tok[-2]
    ph0_s = evac_tok[-1]

    t_wot = E("g", lambda g: g.dma_start(out=wot_sb, in_=din["wot"].ap()),
              deps=[norm_tok[7], sstat_tok[7]], dma=True)
    t_wt = E("g", lambda g: g.dma_start(out=wt_sb, in_=din["wt"].ap()),
             deps=[norm_tok[7], sstat_tok[7]], dma=True)

    if debug:
        d = dram_dbg("xn", [128, CH * T], BF16)
        E("q", lambda q, d=d: q.dma_start(out=d, in_=xn),
          deps=[ph0_v, ph0_s], dma=True)
        d2 = dram_dbg("stat", [128, 112], F32)
        E("q", lambda q, d=d2: q.dma_start(out=d2, in_=stat),
          deps=[ph0_v, ph0_s], dma=True)

    mix_tok = {}
    omcol = {"k": lambda c: aux[:, 24 + c:25 + c],
             "v": lambda c: aux[:, 32 + c:33 + c],
             "r": lambda c: aux[:, 40 + c:41 + c]}

    def emit_mix_s1(h, p, eng, deps):
        toks = []
        for cd in range(CH):
            if eng == "s":
                def fs1(s, p=p, cd=cd, h=h):
                    dst = mixh[p][:, TH * cd:TH * cd + TH]
                    if h == 0:
                        return s.activation(
                            out=dst[:, 1:TH],
                            in_=xn[:, 2048 * cd:2048 * cd + TH - 1],
                            func=AF.Identity, scale=omcol[p](cd))
                    off = 2048 * cd + TH * h - 1
                    return s.activation(
                        out=dst, in_=xn[:, off:off + TH],
                        func=AF.Identity, scale=omcol[p](cd))
                toks.append(E("s", fs1, deps=deps + [t_om]))
            else:
                def fg1(g, p=p, cd=cd, h=h):
                    dst = mixh[p][:, TH * cd:TH * cd + TH]
                    if h == 0 and cd == 0:
                        return g.tensor_scalar(
                            out=dst[:, 1:TH], in0=xn3[:, cd, 0:TH - 1],
                            scalar1=omcol[p](cd), scalar2=None, op0=OP.mult)
                    off = 2048 * cd + TH * h - 1
                    return g.tensor_scalar(
                        out=dst, in0=xn[:, off:off + TH],
                        scalar1=omcol[p](cd), scalar2=None, op0=OP.mult)
                toks.append(E("g", fg1, deps=deps + [t_om]))
        return toks

    def emit_mix_s2(h, p, s1toks, deps, eng="v"):
        if h == 0:
            E(eng, lambda v, p=p: [
                v.memset(mixh[p][:, TH * cd:TH * cd + 1], 0.0)
                for cd in range(CH)][-1], deps=deps)
        tok = None
        for cd in range(CH):
            def fm2(v, p=p, cd=cd, h=h):
                dst = mixh[p][:, TH * cd:TH * cd + TH]
                off = 2048 * cd + TH * h
                return v.scalar_tensor_tensor(
                    out=dst, in0=xn[:, off:off + TH],
                    scalar=tmcol[p](cd), in1=dst, op0=OP.mult, op1=OP.add)
            tok = E(eng, fm2, deps=[s1toks[cd], t_om] + deps)
        return tok

    def emit_mix(h, deps):
        toks = {}
        s1 = {"k": emit_mix_s1(h, "k", "s", deps),
              "v": emit_mix_s1(h, "v", "s", deps),
              "r": emit_mix_s1(h, "r", "s", deps)}
        for p in "kvr":
            toks[p] = emit_mix_s2(h, p, s1[p], deps)
        return toks

    P.fence()
    ev_all = pr_ev[-1]
    t_rn0 = E("v", lambda v: v.tensor_scalar(
        out=rb, in0=rb, scalar1=1e-24, scalar2=None, op0=OP.max),
        deps=[nrm_ev[-1]])
    t_st = E("v", lambda v: v.tensor_scalar(
        out=tiny[0:1, 0:1], in0=small[0:1, 50:51], scalar1=LOG100,
        scalar2=None, op0=OP.min), deps=[t_small])
    t_st2 = E("s", lambda s: s.activation(out=tiny[0:1, 0:1],
                                          in_=tiny[0:1, 0:1], func=AF.Exp),
              deps=[t_st])
    t_rn1 = E("s", lambda s: (
        s.activation(out=rn, in_=rb, func=AF.Ln),
        s.activation(out=rn, in_=rn, func=AF.Exp, scale=-0.5))[-1],
        deps=[t_rn0])
    t_rn2 = E("v", lambda v: v.tensor_scalar(
        out=rn, in0=rn, scalar1=tiny[0:1, 0:1], scalar2=None, op0=OP.mult),
        deps=[t_rn1, t_st2])

    raw_mm = E("t", lambda t: [
        t.matmul(pp[2 + nt // 2][0:4, 512 * (nt % 2):512 * (nt % 2) + 512],
                 simnf[:, 4 * q2:4 * q2 + 4],
                 proj_bf[:, q2 * T + 512 * nt:q2 * T + 512 * nt + 512],
                 start=(q2 == 0), stop=(q2 == 1))
        for nt in range(4) for q2 in range(2)][-1],
        deps=[t_s6, ev_all])
    t_raw = E("s", lambda s: (
        s.copy(out=raw_sb[:, 0:1024], in_=pp[2][0:4, :]),
        s.copy(out=raw_sb[:, 1024:2048], in_=pp[3][0:4, :]))[-1],
        deps=[raw_mm])
    rnb_mm = E("t", lambda t: [
        t.matmul(pp[hh][0:4, 512 * nt:512 * nt + 512], ones64[0:1, 0:4],
                 rn[0:1, 1024 * hh + 512 * nt:1024 * hh + 512 * nt + 512],
                 start=True, stop=True)
        for hh in range(2) for nt in range(2)][-1],
        deps=[t_rn2])
    t_lg = E("v", lambda v: (
        v.tensor_mul(out=raw_sb[:, 0:1024], in0=raw_sb[:, 0:1024],
                     in1=pp[0][0:4, :]),
        v.tensor_mul(out=raw_sb[:, 1024:2048], in0=raw_sb[:, 1024:2048],
                     in1=pp[1][0:4, :]))[-1],
        deps=[rnb_mm, t_raw, t_rn0])
    lgt_mm = E("t", lambda t: [
        t.transpose(pp[2 + j // 8][:, 4 * (j % 8):4 * (j % 8) + 4],
                    raw_sb[:, 128 * j:128 * j + 128], ident[0:4, 0:4])
        for j in range(16)][-1], deps=[t_lg, hc_ev[-1], ev_all])
    t_lgc = E("v", lambda v: (
        v.tensor_copy(out=LG[:, 0:32], in_=pp[2][:, 0:32]),
        v.tensor_copy(out=LG[:, 32:64], in_=pp[3][:, 0:32]))[-1],
        deps=[lgt_mm])

    Lc = lambda i: LG[:, i:64:4]
    Dc = lambda i: GD[:, i:64:4]
    Ec = lambda i: GE[:, i:64:4]
    Sc = lambda i: SEL[:, i:64:4]
    Nc = lambda i: NB[:, i:64:4]
    dmix = [norm_tok[15], sstat_tok[15], t_small]
    s1k0 = emit_mix_s1(0, "k", "s", dmix)
    t_gm = E("v", lambda v: (
        v.tensor_max(out=GS, in0=Lc(0), in1=Lc(1)),
        v.tensor_max(out=GR, in0=Lc(2), in1=Lc(3)),
        v.tensor_copy(out=aux[:, 8:12], in_=car[:, 0:4]),
        v.tensor_max(out=GS, in0=GS, in1=GR),
        v.tensor_copy(out=aux[:, 8:12], in_=car[:, 0:4]),
        v.tensor_sub(out=Dc(0), in0=Lc(0), in1=GS),
        v.tensor_sub(out=Dc(1), in0=Lc(1), in1=GS),
        v.tensor_sub(out=Dc(2), in0=Lc(2), in1=GS),
        v.tensor_sub(out=Dc(3), in0=Lc(3), in1=GS))[-1], deps=[t_lgc])
    t_ge = E("s", lambda s: s.activation(out=GE, in_=GD, func=AF.Exp),
             deps=[t_gm])
    s1v0 = emit_mix_s1(0, "v", "s", dmix)
    t_gs = E("v", lambda v: (
        v.tensor_add(out=GS, in0=Ec(0), in1=Ec(1)),
        v.tensor_add(out=GR, in0=Ec(2), in1=Ec(3)),
        v.tensor_copy(out=aux[:, 8:12], in_=car[:, 0:4]),
        v.tensor_add(out=GS, in0=GS, in1=GR))[-1], deps=[t_ge])
    mix_tok[0] = {}
    mix_tok[0]["k"] = emit_mix_s2(0, "k", s1k0, dmix)
    t_gr = E("s", lambda s: (
        s.activation(out=GR, in_=GS, func=AF.Ln),
        s.activation(out=GR, in_=GR, func=AF.Exp, scale=-1.0))[-1],
        deps=[t_gs])
    t_sel2 = E("v", lambda v: (
        v.tensor_single_scalar(out=Sc(0), in_=Dc(0), scalar=0.0,
                               op=OP.is_equal),
        v.tensor_single_scalar(out=Sc(1), in_=Dc(1), scalar=0.0,
                               op=OP.is_equal),
        v.tensor_single_scalar(out=Sc(2), in_=Dc(2), scalar=0.0,
                               op=OP.is_equal),
        v.tensor_single_scalar(out=Sc(3), in_=Dc(3), scalar=0.0,
                               op=OP.is_equal),
        v.tensor_scalar(out=Nc(0), in0=Sc(0), scalar1=-1.0, scalar2=1.0,
                        op0=OP.mult, op1=OP.add),
        v.tensor_scalar(out=Nc(1), in0=Sc(1), scalar1=-1.0, scalar2=1.0,
                        op0=OP.mult, op1=OP.add),
        v.tensor_scalar(out=Nc(2), in0=Sc(2), scalar1=-1.0, scalar2=1.0,
                        op0=OP.mult, op1=OP.add),
        v.tensor_mul(out=Sc(1), in0=Sc(1), in1=Nc(0)),
        v.tensor_mul(out=Nc(1), in0=Nc(0), in1=Nc(1)),
        v.tensor_mul(out=GT4[:, 0:64:4], in0=Sc(0), in1=GR),
        v.tensor_mul(out=Sc(2), in0=Sc(2), in1=Nc(1)),
        v.tensor_mul(out=Nc(2), in0=Nc(1), in1=Nc(2)),
        v.tensor_mul(out=GT4[:, 1:64:4], in0=Sc(1), in1=GR),
        v.tensor_mul(out=Sc(3), in0=Sc(3), in1=Nc(2)),
        v.tensor_mul(out=GT4[:, 2:64:4], in0=Sc(2), in1=GR),
        v.tensor_mul(out=GT4[:, 3:64:4], in0=Sc(3), in1=GR))[-1],
        deps=[t_gr])
    g4_mm = E("t", lambda t: [
        t.transpose(pp[j // 8][0:4, 128 * (j % 8):128 * (j % 8) + 128],
                    GT4[:, 4 * j:4 * j + 4], ident)
        for j in range(16)][-1], deps=[t_sel2, t_lg])
    t_g4 = E("s", lambda s: (
        s.copy(out=G4sb[:, 0:1024], in_=pp[0][0:4, :]),
        s.copy(out=G4sb[:, 1024:2048], in_=pp[1][0:4, :]))[-1],
        deps=[g4_mm])
    mix_tok[0]["v"] = emit_mix_s2(0, "v", s1v0, dmix)
    s1r0 = emit_mix_s1(0, "r", "s", dmix + [t_sel2, g4_mm])
    g15_mm = E("t", lambda t: [
        t.matmul(pp[2 + nt // 2][0:15, 512 * (nt % 2):512 * (nt % 2) + 512],
                 selS, G4sb[:, 512 * nt:512 * nt + 512],
                 start=True, stop=True)
        for nt in range(4)][-1], deps=[t_g4, t_sel, t_lgc])
    t_hg = E("v", lambda v: (
        v.tensor_mul(out=hcat[:, 0:1024], in0=hcat[:, 0:1024],
                     in1=pp[2][0:15, :]),
        v.tensor_mul(out=hcat[:, 1024:2048], in0=hcat[:, 1024:2048],
                     in1=pp[3][0:15, :]))[-1],
        deps=[g15_mm, hc_ev[-1]])
    mix_tok[0]["r"] = emit_mix_s2(0, "r", s1r0, dmix + [t_sel2, g4_mm])

    if debug:
        for nm, buf, dt in (("projbf", proj_bf, F32), ("lgfm", raw_sb, F32),
                            ("LG", LG, F32), ("GT4", GT4, F32),
                            ("hg", hcat, BF16),
                            ("rn2", rn, F32)):
            d = dram_dbg(nm, list(buf.shape), dt)
            E("q", lambda q, d=d, b=buf: q.dma_start(out=d, in_=b),
              deps=[t_hg, t_sel2, t_lg], dma=True)

    P.fence()
    it_toks = []
    carps, carqs = [None] * 8, [None] * 8
    psm = {"k": pp[0], "v": pp[1], "r": pp[2]}

    def emit_late(pi, pit):
        s2 = pi % 2
        Dp, Ep = DD[s2], EE[s2]
        pit["tden"] = E("s", lambda s, Dp=Dp, Ep=Ep: s.activation(
            out=Ep, in_=Dp, func=AF.Identity, bias=1.0, scale=1.0 + 1e-9),
            deps=[pit["tro"], pit["tya"], pit["tqs"]])
        pit["t1g"] = E("g", lambda g, Dp=Dp, qb=(qbuf, qbuf2)[pi % 2]:
            g.tensor_mul(out=Dp, in0=Dp, in1=qb[:, 0:TH]),
            deps=[pit["tden"], pit["tqs"], pit["tro"]])
        pit["t20"] = E("g", lambda g, Dp=Dp: g.tensor_add(
            out=Dp, in0=Dp, in1=v_sb), deps=[pit["ev"]["v"]])

    def emit_late2(pi, pit):
        Ep = EE[pi % 2]
        pit["t21"] = E("v", lambda v, Ep=Ep, Rb=pp[2 + pi % 2]:
                       v.scalar_tensor_tensor(
                           out=Ep, in0=Rb, scalar=1.0, in1=Ep, op0=OP.add,
                           op1=OP.mult), deps=[pit["ev"]["r"], pit["tden"]])
        pit["trd"] = E("s", lambda s, Ep=Ep: (
            s.activation(out=Ep, in_=Ep, func=AF.Ln),
            s.activation(out=Ep, in_=Ep, func=AF.Exp, scale=-1.0))[-1],
            deps=[pit["t21"]])

    def emit_ty(pi, pit):
        s2 = pi % 2
        h2, c2 = pi // 8, pi % 8
        pit["ty"] = E("g", lambda g, c=c2, h=h2, Dp=DD[s2], Ep=EE[s2]:
                      g.tensor_mul(out=y3[:, c, TH * h:TH * h + TH],
                                   in0=Dp, in1=Ep),
                      deps=[pit["trd"], pit["t20"]])

    for i, (h, c) in enumerate([(h, c) for h in range(2) for c in range(CH)]):
        it = {}
        s2 = i % 2
        D_b, E_b = DD[s2], EE[s2]
        prev = it_toks[i - 1] if i >= 1 else None
        if prev:
            emit_late(i - 1, prev)
        nxt = i + 1
        if nxt < 16:
            nh, ncc = nxt // 8, nxt % 8
            if (ncc, nh) not in slice_toks:
                pv = it_toks[i - 1]["mm"] if i >= 1 else None
                deps = {p: [pv[p]] for p in "kvr"} if pv else {}
                slice_toks[(ncc, nh)] = load_slices(ncc, nh, deps)
        if h == 1 and c == 0:
            mix_tok[1] = emit_mix(1, [it_toks[7]["mm"][p] for p in "kvr"])
        sl = slice_toks[(c, h)]
        it["mm"] = {}
        rb_i = pp[2 + i % 2]
        for p in "kvr":
            def fmm(t, p=p, c=c, rb_i=rb_i):
                tgt = rb_i if p == "r" else psm[p]
                last = None
                for nt in range(2):
                    for cd in range(CH):
                        last = t.matmul(
                            tgt[:, 512 * nt:512 * nt + 512],
                            ws[p][c % 2][:, 128 * cd:128 * cd + 128],
                            mixh[p][:, TH * cd + 512 * nt:
                                     TH * cd + 512 * nt + 512],
                            start=(cd == 0), stop=(cd == 7))
                return last
            pgate = None
            if prev:
                pgate = {"k": prev["t13"], "v": prev["ev"]["v"],
                         "r": it_toks[i - 2]["t21"] if i >= 2 else None}[p]
            extra = [prev["tbv"]] if (p == "v" and prev) else []
            it["mm"][p] = E("t", fmm,
                            deps=[sl[p], mix_tok[h][p], pgate] + extra)
        it["ev"] = {}
        war_q = it_toks[i - 2]["t1g"] if i >= 2 else None
        if h == 0:
            E("v", lambda v: v.memset(pbuf[:, 0:1], -1e30))
        else:
            E("v", lambda v, c=c: v.tensor_copy(out=pbuf[:, 0:1],
                                                in_=car[:, c:c + 1]),
              deps=[carps[c]])
        tps = E("v", lambda v, c=c: v.tensor_tensor_scan(
            out=pbuf[:, 1:TH + 1], data0=wcol(c).broadcast_to([128, TH]),
            data1=psm["k"], initial=pbuf[:, 0:1], op0=OP.add, op1=OP.max),
            deps=[it["mm"]["k"],
                  carps[i - 1] if 1 <= i <= 8 else None])
        t7 = E("v", lambda v: v.tensor_sub(out=B_b, in0=psm["k"],
                                           in1=pbuf[:, 1:TH + 1]),
               deps=[prev["tqs"] if prev else None])
        tb = E("s", lambda s: s.activation(out=B_b, in_=B_b, func=AF.Exp),
               deps=[t7])
        t4 = E("v", lambda v, c=c: v.scalar_tensor_tensor(
            out=A_b, in0=pbuf[:, 0:TH], scalar=wcol(c),
            in1=pbuf[:, 1:TH + 1], op0=OP.add, op1=OP.subtract),
            deps=[prev["tqs"] if prev else None])
        ta = E("s", lambda s: s.activation(out=A_b, in_=A_b, func=AF.Exp),
               deps=[t4])
        it["ev"]["r"] = E("s", lambda s, rb_i=rb_i: s.activation(
            out=rb_i, in_=rb_i, func=AF.Exp, scale=-1.0),
            deps=[it["mm"]["r"]])
        qb = (qbuf, qbuf2)[i % 2]
        if h == 0:
            E("v", lambda v, qb=qb: v.memset(qb[:, 0:1], 0.0), deps=[war_q])
        else:
            E("v", lambda v, c=c, qb=qb: v.tensor_copy(
                out=qb[:, 0:1], in_=car[:, 8 + c:9 + c]),
              deps=[war_q, carqs[c]])
        tya = it_toks[i - 2]["ty"] if i >= 2 else None
        it["t13"] = E("v", lambda v, c=c, E_b=E_b: v.scalar_tensor_tensor(
            out=E_b, in0=pbuf[:, 0:TH], scalar=ucol(c), in1=psm["k"],
            op0=OP.subtract, op1=OP.subtract), deps=[tya])
        it["tdc"] = E("v", lambda v, D_b=D_b, E_b=E_b: v.tensor_scalar(
            out=D_b, in0=E_b, scalar1=30.0, scalar2=0.0,
            op0=OP.min, op1=OP.add))
        tbv = E("v", lambda v, E_b=E_b: v.tensor_mul(
            out=E_b, in0=B_b, in1=psm["v"]), deps=[tb, it["mm"]["v"]])
        it["tbv"] = tbv
        if i >= 1 and i <= 8:
            carqs[i - 1] = E("s", lambda s, i=i, qp=(qbuf, qbuf2)[(i - 1) % 2]:
                s.copy(out=car[:, 7 + i:8 + i], in_=qp[:, TH:TH + 1]),
                deps=[it_toks[i - 1]["tqs"]])
        if h == 0:
            carps[c] = E("s", lambda s, c=c: s.copy(
                out=car[:, c:c + 1], in_=pbuf[:, TH:TH + 1]), deps=[tps])
        it["tro"] = E("s", lambda s, D_b=D_b: s.activation(
            out=D_b, in_=D_b, func=AF.Exp), deps=[it["tdc"]])
        it["ev"]["v"] = E("s", lambda s: s.copy(out=v_sb, in_=psm["v"]),
                          deps=[it["mm"]["v"], prev["t20"] if prev else None])
        it["tya"] = tya
        it["tqs"] = E("v", lambda v, qb=qb, E_b=E_b: v.tensor_tensor_scan(
            out=qb[:, 1:TH + 1], data0=A_b, data1=E_b,
            initial=qb[:, 0:1], op0=OP.mult, op1=OP.add),
            deps=[ta, tbv, war_q,
                  carqs[i - 2] if 2 <= i <= 9 else None])
        if prev:
            emit_late2(i - 1, prev)
            emit_ty(i - 1, prev)
        if debug and i == 0:
            for nm, buf in (("v0", v_sb),
                            ("p0", pbuf), ("q0", qbuf)):
                d = dram_dbg(nm, list(buf.shape), F32)
                E("q", lambda q, d=d, b=buf: q.dma_start(out=d, in_=b),
                  deps=[it["tqs"]], dma=True)
        it_toks.append(it)
    emit_late(15, it_toks[15])
    emit_late2(15, it_toks[15])
    emit_ty(15, it_toks[15])

    if debug:
        d = dram_dbg("y", [128, CH * T], BF16)
        E("q", lambda q, d=d: q.dma_start(out=d, in_=y),
          deps=[it_toks[-1]["ty"]], dma=True)
        for p in "kvr":
            d = dram_dbg("mix" + p, [128, CH * TH], BF16)
            E("q", lambda q, d=d, b=mixh[p]: q.dma_start(out=d, in_=b),
              deps=[it_toks[-1]["ty"]], dma=True)

    it15 = it_toks[15]
    t_tab0 = E("s", lambda s: s.add_instruction(mybir.InstLoadActFuncSet(
        name=s.bass.get_next_instruction_name(), act_func_set_id=0, ins=[],
        outs=[])), deps=[it15["trd"]])
    store_toks = []
    j_toks = [None] * NTOK
    for j in range(NTOK):
        prev2 = j_toks[j - 2] if j >= 2 else None
        tx = E("q", lambda q, j=j: q.dma_start(out=xt5[j % 2],
                                               in_=x_t[:, j, :]),
               deps=[prev2["out"] if prev2 else None,
                     it15["mm"]["k"] if j < 2 else None], dma=True)
        pss = pp[j % 2]
        psf = pp[2 + j % 2]
        yh = it_toks[7]["ty"] if j < 8 else it15["ty"]
        smm = E("t", lambda t, j=j, pss=pss: [
            t.matmul(pss[:, 512 * nt:512 * nt + 512],
                     y3[:, ce, 128 * j:128 * j + 128],
                     wot3[:, ce, 512 * nt:512 * nt + 512],
                     start=(ce == 0), stop=(ce == 7))
            for ce in range(CH) for nt in range(2)][-1],
            deps=[t_wot, prev2["sev"] if prev2 else None,
                  yh if j in (0, 1, 8, 9) else None,
                  it15["t13"] if j == 0 else None,
                  it15["ev"]["v"] if j == 1 else None])
        sev = E("s", lambda s, j=j, pss=pss: s.copy(out=OT[j % 2], in_=pss),
                deps=[smm, store_toks[j - 2] if j >= 2 else None])

        def ffm(t, j=j, psf=psf):
            for ce in range(CH):
                for nt in range(2):
                    t.matmul(psf[:, 512 * nt:512 * nt + 512],
                             xn3[:, ce, 128 * j:128 * j + 128],
                             wt3[:, ce, 512 * nt:512 * nt + 512],
                             start=(ce == 0), stop=False)
            last = None
            for nt in range(2):
                last = t.matmul(psf[:, 512 * nt:512 * nt + 512],
                                hcat[:, 128 * j:128 * j + 128],
                                wut_sb[:, 512 * nt:512 * nt + 512],
                                start=False, stop=True)
            return last
        fmm = E("t", ffm, deps=[t_wt, t_wut,
                                prev2["ep"] if prev2 else None,
                                prev2["sev2"] if prev2 else None,
                                it_toks[14]["t21"] if j == 0 else None,
                                it15["t21"] if j == 1 else None,
                                ph0_v if j == 0 else None,
                                ph0_s if j == 0 else None,
                                t_hg if j == 0 else None])
        tep = E("s", lambda s, j=j, psf=psf: s.activation(
            out=EP[j % 2], in_=psf, func=AF.Tanh, scale=0.5),
            deps=[fmm, t_tab0])
        sev2 = E("s", lambda s, j=j, psf=psf: s.copy(out=GZ[j % 2], in_=psf),
                 deps=[fmm, prev2["out"] if prev2 else None])
        tz1 = E("v", lambda v, j=j: v.scalar_tensor_tensor(
            out=GZ[j % 2], in0=EP[j % 2], scalar=1.0, in1=GZ[j % 2],
            op0=OP.add, op1=OP.mult), deps=[tep, sev2])
        tz = E("v", lambda v, j=j: v.tensor_mul(
            out=GZ[j % 2], in0=GZ[j % 2], in1=OT[j % 2]), deps=[sev])
        tout = E("v", lambda v, j=j: v.scalar_tensor_tensor(
            out=OT[j % 2], in0=GZ[j % 2], scalar=0.5, in1=xt5[j % 2],
            op0=OP.mult, op1=OP.add),
            deps=[tx])
        st = E("q", lambda q, j=j: q.dma_start(out=out_t[:, j, :],
                                               in_=OT[j % 2]),
               deps=[tout], dma=True)
        store_toks.append(st)
        j_toks[j] = dict(out=tout, z=tz, z1=tz1, ep=tep, sev=sev, sev2=sev2)

    all_q_dmas = [("q", i, True) for i, op in enumerate(P.ops["q"]) if op[2]]
    all_g_dmas = [("g", i, True) for i, op in enumerate(P.ops["g"]) if op[2]]
    E("q", lambda q: None, deps=store_toks + all_q_dmas + all_g_dmas)

    P.resolve()
    with ctx:
        from contextlib import ExitStack as ES2
        with ES2() as sem_ctx:
            sems = {}
            for e in Plan.ENGINES:
                sems[(e, False)] = sem_ctx.enter_context(
                    nc.semaphore(f"sem_{e}"))
                for j in range(Plan.NDMA.get(e, 1)):
                    sems[(e, "dma", j)] = sem_ctx.enter_context(
                        nc.semaphore(f"semd_{e}{j}"))
            for e in ("g", "s", "v"):
                sems[(e, "ch")] = sem_ctx.enter_context(
                    nc.semaphore(f"ch_{e}"))
            with nc.Block() as block:
                @block.sync
                def _(q):
                    P.replay("q", q, sems, chain_counts=chain_counts)

                @block.gpsimd
                def _(g):
                    P.replay("g", g, sems, sems[("g", "ch")],
                             chain_counts=chain_counts)

                @block.tensor
                def _(t):
                    P.replay("t", t, sems, chain_counts=chain_counts)

                @block.scalar
                def _(s):
                    P.replay("s", s, sems, sems[("s", "ch")],
                             chain_counts=chain_counts)

                @block.vector
                def _(v):
                    P.replay("v", v, sems, sems[("v", "ch")],
                             chain_counts=chain_counts)
    return nc, dbg, P


def kernel(**inputs):
    from concourse.bass_utils import run_bass_kernel_spmd
    if "nc" not in _CACHE:
        _CACHE["nc"], _ = build_program(debug=False)
    nc = _CACHE["nc"]
    shared = _prep_shared(inputs)
    x = np.asarray(inputs["L0_attn_x"])
    in_maps = []
    for b in range(8):
        m = dict(shared)
        m["x"] = np.ascontiguousarray(x[b])
        in_maps.append(m)
    res = run_bass_kernel_spmd(nc, in_maps, core_ids=list(range(8)))
    out = np.stack([np.asarray(res.results[b]["out"]) for b in range(8)],
                   axis=0)
    return out.astype(np.float32)


# version 55
# speedup vs baseline: 1.0579x; 1.0376x over previous
import numpy as np
import concourse.bass as bass
import concourse.mybir as mybir

F32 = mybir.dt.float32
BF16 = mybir.dt.bfloat16
U8 = mybir.dt.uint8
OP = mybir.AluOpType
AF = mybir.ActivationFunctionType

T = 2048
D = 1024
CH = 8
NTOK = 16
TH = 1024
NEG30 = -30.0
LN_EPS = 1e-5
LOG100 = float(np.log(100.0))
LN_1EM9 = float(np.log(1e-9))

_CACHE = {}
_TRACE_EMIT = False


class Plan:

    ENGINES = ("q", "g", "t", "s", "v")

    def __init__(self):
        self.ops = {e: [] for e in self.ENGINES}
        self.marked = set()
        self.fence_pending = {}
        self.chain_cum = {}
        self.src = {}

    def emit(self, eng, fn, deps=(), dma=False):
        deps = [d for d in deps if d is not None]
        deps = self.fence_pending.pop(eng, []) + deps
        for d in deps:
            self.marked.add((d[0], d[1]))
        idx = len(self.ops[eng])
        self.ops[eng].append([fn, deps, dma, False])
        if _TRACE_EMIT:
            import traceback
            fr = traceback.extract_stack(limit=3)[0]
            self.src[(eng, idx)] = f"{fr.lineno}"
        return (eng, idx, dma)

    def fence(self):
        last = {}
        for e in self.ENGINES:
            if self.ops[e]:
                last[e] = (e, len(self.ops[e]) - 1, self.ops[e][-1][2])
        for e in self.ENGINES:
            toks = [t for o, t in last.items() if o != e]
            self.fence_pending.setdefault(e, []).extend(toks)

    NDMA = {"q": 8, "g": 4}

    def resolve(self):
        self.cum = {e: [0] * len(self.ops[e]) for e in self.ENGINES}
        self.dma_slot = {}
        for e in self.ENGINES:
            c_cmp = 0
            k = 0
            per_slot = [0] * self.NDMA.get(e, 1)
            for i, op in enumerate(self.ops[e]):
                if op[2] or (e, i) in self.marked:
                    op[3] = True
                    if not op[2]:
                        c_cmp += 1
                if op[2]:
                    j = k % len(per_slot)
                    per_slot[j] += 16
                    self.dma_slot[(e, i)] = (j, per_slot[j])
                    k += 1
                    self.cum[e][i] = per_slot[j]
                else:
                    self.cum[e][i] = c_cmp

    CHAINED = ("g", "s", "v")

    def replay(self, eng_name, eng_obj, sems, chain_sem=None, lag=1,
               chain_counts=None):
        if chain_sem is not None:
            eng_obj = _ChainEng(eng_obj, chain_sem, lag)
        self.chain_cum[eng_name] = [0] * len(self.ops[eng_name])
        maxw = {}
        for i, (fn, deps, dma, inc) in enumerate(self.ops[eng_name]):
            for d in deps:
                de, di, ddma = d[0], d[1], d[2]
                hard_d = len(d) == 4 and d[3]
                if ddma:
                    j, val = self.dma_slot[(de, di)]
                    sem = sems[(de, "dma", j)]
                elif de in self.CHAINED:
                    if de == eng_name:
                        continue
                    sem = sems[(de, "ch")]
                    val = chain_counts[de][di] if chain_counts else 0
                else:
                    if de == eng_name and not hard_d:
                        continue
                    sem = sems[(de, False)]
                    val = self.cum[de][di]
                if maxw.get(id(sem), -1) >= val:
                    continue
                maxw[id(sem)] = val
                eng_obj.wait_ge(sem, val)
            instr = fn(eng_obj)
            if chain_sem is not None:
                self.chain_cum[eng_name][i] = eng_obj._n
            if dma:
                j, val = self.dma_slot[(eng_name, i)]
                sem = sems[(eng_name, "dma", j)]
                if val > 16:
                    instr._wait_ge(sem, val - 16)
                instr.then_inc(sem, 16)
            elif inc and chain_sem is None:
                assert instr is not None, f"marked op {eng_name}[{i}] is None"
                instr.then_inc(sems[(eng_name, False)], 1)


class _ChainEng:

    _SKIP = frozenset((
        "wait_ge", "wait_op", "sem_inc", "sem_clear", "maybe_wait_ge",
        "dma_start", "add_instruction", "preload_act_table",
    ))

    def __init__(self, eng, sem, lag):
        self._eng = eng
        self._sem = sem
        self._lag = lag
        self._n = 0

    def __getattr__(self, name):
        f = getattr(self._eng, name)
        if name in self._SKIP or not callable(f):
            return f

        def wrap(*a, **k):
            r = f(*a, **k)
            if hasattr(r, "then_inc"):
                n = self._n
                w = n + 1 - self._lag
                if w > 0:
                    r._wait_ge(self._sem, w)
                r.then_inc(self._sem, 1)
                self._n = n + 1
            return r

        return wrap


def hard(tok):
    return (tok[0], tok[1], tok[2], True)


def _fm(W):
    R, C = W.shape
    return np.ascontiguousarray(
        W.reshape(R // 128, 128, C).transpose(1, 0, 2).reshape(128, -1))


def _col8(v):
    return np.ascontiguousarray(v.reshape(8, 128).T)


def _prep_shared(inp):
    wd = np.concatenate([inp[f"L0_ffn_Wd{i}"] for i in range(4)], axis=0)
    wu = np.concatenate([inp[f"L0_ffn_Wu{i}"] for i in range(4)], axis=1)
    small = np.zeros((128, 324), np.float32)
    small[:, 0:8] = _col8(inp["L0_attn_time_mix_k"])
    small[:, 8:16] = _col8(inp["L0_attn_time_mix_v"])
    small[:, 16:24] = _col8(inp["L0_attn_time_mix_r"])
    small[:, 24:32] = _col8(inp["L0_attn_time_decay"])
    small[:, 32:40] = _col8(inp["L0_attn_time_first"])
    small[:, 40:42] = np.ascontiguousarray(inp["L0_ffn_bp"].reshape(2, 128).T)
    small[:, 42:50] = np.ascontiguousarray(
        np.asarray(inp["L0_ffn_sim"]).reshape(2, 128, 4)
        .transpose(1, 0, 2).reshape(128, 8))
    small[:, 50] = float(np.asarray(inp["L0_ffn_temp"]).reshape(-1)[0])
    small[:, 51] = 1.0
    sel = np.zeros((4, 15), np.float32)
    sel[0, 0] = sel[1, 1:3] = sel[2, 3:7] = sel[3, 7:15] = 1.0
    small[32:36, 52:67] = sel
    small[:, 67] = np.log(1e-9)
    small[0, 68:196] = 1.0
    small[64, 68:196] = 1.0
    small[:, 196:324] = np.eye(128, dtype=np.float32)
    ms = [_fm(np.ascontiguousarray(np.asarray(inp["L0_attn_W_" + p]).T))
          .reshape(128, 8, 8, 128) for p in "kvr"]
    return dict(
        wkvr=np.ascontiguousarray(
            np.stack(ms, axis=1).transpose(0, 3, 1, 2, 4)
            .reshape(128, 3 * CH * D)),
        wot=_fm(np.ascontiguousarray(np.asarray(inp["L0_attn_W_o"]).T)),
        wt=_fm(np.ascontiguousarray(np.asarray(inp["L0_ffn_W"]).T)),
        wpt=_fm(np.ascontiguousarray(np.asarray(inp["L0_ffn_Wp"]).T)),
        wdt=_fm(np.ascontiguousarray(np.asarray(wd).T)),
        wut=np.ascontiguousarray(np.asarray(wu).T),
        small=small)


def build_program(debug=False):
    _, _, p1 = _build_program(debug, None)
    nc, dbg, _ = _build_program(debug, p1.chain_cum)
    return nc, dbg


def _build_program(debug=False, chain_counts=None):
    nc = bass.Bass("TRN2", target_bir_lowering=False)
    P = Plan()
    E = P.emit

    x_h = nc.dram_tensor("x", [T, D], F32, kind="ExternalInput")
    out_h = nc.dram_tensor("out", [T, D], F32, kind="ExternalOutput")
    din = {n: nc.dram_tensor(n, s, F32, kind="ExternalInput") for n, s in [
        ("wkvr", [128, 3 * CH * D]),
        ("wot", [128, CH * D]), ("wt", [128, CH * D]), ("wpt", [128, CH * 256]),
        ("wdt", [128, CH * 15]), ("wut", [15, D]), ("small", [128, 324])]}

    dbg = {}

    def dram_dbg(name, shape, dtype):
        ap = nc.dram_tensor("dbg_" + name, shape, dtype,
                            kind="ExternalOutput").ap()
        dbg[name] = ap
        return ap

    x_t = x_h.ap().rearrange("(j p) d -> p j d", p=128)
    out_t = out_h.ap().rearrange("(j p) d -> p j d", p=128)

    from contextlib import ExitStack
    ctx = ExitStack()
    sb = lambda n, s, d: ctx.enter_context(nc.sbuf_tensor(n, s, d)).ap()

    xn = sb("xn", [128, CH * T], BF16)
    y = sb("y", [128, CH * T], BF16)
    ws_all = [sb(f"ws_all{i}", [128, 3 * CH * 128], BF16) for i in range(2)]
    ws = {p: [ws_all[i][:, pi * CH * 128:(pi + 1) * CH * 128]
              for i in range(2)] for pi, p in enumerate("kvr")}
    hcat = sb("hcat", [15, T], BF16)
    wut_sb = sb("wut_sb", [15, D], BF16)
    wdt_sb = sb("wdt_sb", [128, CH * 15], BF16)
    small = sb("small_sb", [128, 324], F32)

    stat = sb("stat", [128, 112], F32)
    car = sb("car", [128, 16], F32)
    aux = sb("aux", [128, 48], F32)
    tiny = sb("tiny", [4, 64], F32)
    selS = small[32:36, 52:67]
    ones1 = small[0:1, 68:196]
    ones64 = small[64:65, 68:196]
    onesc = small[:, 51:52]
    simnf = aux[:, 0:8]

    AR = 124960
    arena = ctx.enter_context(nc.sbuf_tensor("arena", [128, AR], U8)).ap()

    def view(off, nbytes, dtype, cols):
        ap = arena[:, off:off + nbytes].bitcast(dtype)
        assert ap.shape[1] == cols, (ap.shape, cols)
        return ap

    xt0 = [view(i * 4096, 4096, F32, 1024) for i in range(16)]
    xn_tm = [view(65536 + i * 4096, 4096, F32, 1024) for i in range(4)]
    wot_sb = view(0, 16384, BF16, CH * D)
    wt_sb = view(16384, 16384, BF16, CH * D)
    VP = view(81920, 8192, F32, 2048)
    raw_sb = VP[0:4, :]
    G4sb = VP[32:36, :]
    rn = VP[64:65, :]
    rb = VP[96:97, :]
    proj_bf = view(90112, 16384, F32, 2 * T)
    proj2 = view(106496, 4096, F32, T // 2)
    gb = 73728
    LG = view(gb, 256, F32, 64)
    GD = view(gb + 256, 256, F32, 64)
    GE = view(gb + 512, 256, F32, 64)
    SEL = view(gb + 768, 256, F32, 64)
    NB = view(gb + 1024, 256, F32, 64)
    GT4 = view(gb + 1280, 256, F32, 64)
    GS = view(gb + 1536, 64, F32, 16)
    GR = view(gb + 1600, 64, F32, 16)
    PH3 = 32768
    mixh = {p: view(PH3 + i * 16384, 16384, BF16, CH * TH)
            for i, p in enumerate("kvr")}

    sc_off = PH3 + 51200
    v_sb = view(sc_off + 1 * 4096, 4096, F32, TH)
    A_b = view(sc_off + 3 * 4096, 4096, F32, TH)
    B_b = view(sc_off + 4 * 4096, 4096, F32, TH)
    DD = [view(sc_off + 0 * 4096, 4096, F32, TH), view(112672, 4096, F32, TH)]
    EE = [view(sc_off + 2 * 4096, 4096, F32, TH), view(116768, 4096, F32, TH)]
    pbuf = view(sc_off + 20480, 4112, F32, 1028)
    qbuf = view(sc_off + 24592, 4112, F32, 1028)
    rsb = view(120864, 4096, F32, TH)
    PH5 = 32768
    xt5 = [view(PH5 + i * 4096, 4096, F32, 1024) for i in range(2)]
    wpt_sb = view(110592, 4096, BF16, CH * 256)
    OT = [view(PH5 + 8192 + i * 4096, 4096, F32, 1024) for i in range(2)]
    EP = [view(PH5 + 16384 + i * 4096, 4096, F32, 1024) for i in range(2)]
    GZ = [view(PH5 + 24576 + i * 4096, 4096, F32, 1024) for i in range(2)]

    pp = [ctx.enter_context(nc.psum_tensor(f"pp{i}", [128, 1024], F32)).ap()
          for i in range(4)]

    xn3 = xn.rearrange("p (c t) -> p c t", c=CH)
    y3 = y.rearrange("p (c t) -> p c t", c=CH)
    wot3 = wot_sb.rearrange("p (c t) -> p c t", c=CH)
    wt3 = wt_sb.rearrange("p (c t) -> p c t", c=CH)
    wsl_all = din["wkvr"].ap().rearrange("p (s r c e) -> p s r c e",
                                         s=8, r=3, c=CH)

    tmcol = {"k": lambda c: small[:, 0 + c:1 + c],
             "v": lambda c: small[:, 8 + c:9 + c],
             "r": lambda c: small[:, 16 + c:17 + c]}
    wcol = lambda c: small[:, 24 + c:25 + c]
    ucol = lambda c: small[:, 32 + c:33 + c]

    t_small = E("q", lambda q: q.dma_start(out=small, in_=din["small"].ap()),
                dma=True)
    t_wpt = E("g", lambda g: g.dma_start(out=wpt_sb, in_=din["wpt"].ap()),
              dma=True)
    t_wdt = E("g", lambda g: g.dma_start(out=wdt_sb, in_=din["wdt"].ap()),
              dma=True)
    t_wut = E("g", lambda g: g.dma_start(out=wut_sb, in_=din["wut"].ap()),
              dma=True)

    def load_slices(c, h, deps):
        ktok = E("g", lambda g, c=c: g.dma_start(
            out=ws_all[c % 2][:, 0:CH * 128].rearrange(
                "p (c e) -> p c e", c=CH),
            in_=wsl_all[:, c, 0, :, :]),
            deps=deps.get("k", []), dma=True)
        vrtok = E("g", lambda g, c=c: g.dma_start(
            out=ws_all[c % 2][:, CH * 128:3 * CH * 128].rearrange(
                "p (r c e) -> p r c e", r=2, c=CH),
            in_=wsl_all[:, c, 1:3, :, :]),
            deps=deps.get("v", []) + deps.get("r", []), dma=True)
        return {"k": ktok, "v": vrtok, "r": vrtok}

    ident = small[:, 196:324]
    t_ident = t_small

    t_sel = t_small

    slice_toks = {(0, 0): load_slices(0, 0, {}), (1, 0): load_slices(1, 0, {})}
    E("v", lambda v: v.memset(car, 0.0))
    t_om = E("v", lambda v: v.tensor_scalar(
        out=aux[:, 24:48], in0=small[:, 0:24], scalar1=-1.0, scalar2=1.0,
        op0=OP.mult, op1=OP.add), deps=[t_small])

    t_tab6 = E("s", lambda s: s.add_instruction(mybir.InstLoadActFuncSet(
        name=s.bass.get_next_instruction_name(), act_func_set_id=6, ins=[],
        outs=[])))
    t_s2 = E("v", lambda v: v.tensor_mul(out=stat[:, 64:72],
                                         in0=small[:, 42:50],
                                         in1=small[:, 42:50]),
             deps=[t_small])
    sim_mm = E("t", lambda t: t.matmul(pp[2][0:1, 0:8], onesc,
                                       stat[:, 64:72], start=True, stop=True),
               deps=[t_s2])
    t_s3 = E("s", lambda s: s.copy(out=tiny[0:1, 8:16], in_=pp[2][0:1, 0:8]),
             deps=[sim_mm, t_tab6])
    t_s4a = E("v", lambda v: v.tensor_add(
        out=tiny[0:1, 16:20], in0=tiny[0:1, 8:12], in1=tiny[0:1, 12:16]),
        deps=[t_s3])
    t_s4 = E("v", lambda v: v.tensor_scalar(
        out=tiny[0:1, 16:20], in0=tiny[0:1, 16:20],
        scalar1=1e-24, scalar2=None, op0=OP.max),
        deps=[hard(t_s4a)])
    t_s5a = E("s", lambda s: s.activation(
        out=tiny[0:1, 20:24], in_=tiny[0:1, 16:20], func=AF.Ln), deps=[t_s4])
    t_s5 = E("s", lambda s: s.activation(
        out=tiny[0:1, 20:24], in_=tiny[0:1, 20:24], func=AF.Exp,
        scale=-0.5), deps=[hard(t_s5a)])
    simb_mm = E("t", lambda t: t.matmul(pp[2][0:128, 0:4], ones1,
                                        tiny[0:1, 20:24], start=True,
                                        stop=True), deps=[t_s5, t_s3])
    t_s6 = E("v", lambda v: (
        v.tensor_mul(out=simnf[:, 0:4], in0=small[:, 42:46],
                     in1=pp[2][0:128, 0:4]),
        v.tensor_mul(out=simnf[:, 4:8], in0=small[:, 46:50],
                     in1=pp[2][0:128, 0:4]))[-1],
        deps=[simb_mm])

    xload = [None] * NTOK
    norm_tok = [None] * NTOK
    sstat_tok = [None] * NTOK
    vstat_tok = [None] * NTOK
    tr_tok = [None] * NTOK
    evac_tok = []
    stat_scr_v = view(114688, 2048, F32, 512)
    stat_scr_s = view(116736, 2048, F32, 512)
    xb4 = [view(16384 * i, 16384, F32, 4096) for i in range(4)]
    pr_mm, pr_ev, h_mm, hc_ev = [], [], [], []
    nrm_mm, nrm_ev = [], []

    def emit_wave(w0, extra_deps):
        sl = lambda base, w0=w0: stat[:, base + w0:base + w0 + 4]
        t_v0 = E("v", lambda v, sl=sl, w0=w0: (
            v.tensor_add(out=sl(16), in0=sl(16),
                         in1=stat[:, 96 + w0:100 + w0]),
            v.tensor_add(out=sl(0), in0=sl(0),
                         in1=stat[:, 80 + w0:84 + w0]))[-1],
            deps=[sstat_tok[w0 + 3]] + extra_deps)
        t_v1 = E("v", lambda v, sl=sl: (
            v.tensor_scalar(out=sl(32), in0=sl(0), scalar1=1.0 / D,
                            scalar2=None, op0=OP.mult),
            v.tensor_scalar(out=sl(48), in0=sl(16), scalar1=1.0 / D,
                            scalar2=LN_EPS, op0=OP.mult, op1=OP.add),
            v.tensor_mul(out=sl(0), in0=sl(32), in1=sl(32)),
            v.tensor_copy(out=aux[:, 8:12], in_=car[:, 0:4]),
            v.tensor_sub(out=sl(48), in0=sl(48), in1=sl(0)))[-1],
            deps=[t_v0])
        t_r0 = E("s", lambda s, sl=sl: (
            s.activation(out=sl(64), in_=sl(48), func=AF.Ln),
            s.activation(out=sl(64), in_=sl(64), func=AF.Exp,
                         scale=-0.5))[-1],
            deps=[t_v1])
        for jj in range(w0, w0 + 4):
            norm_tok[jj] = E("v", lambda v, jj=jj: v.tensor_scalar(
                out=xn_tm[jj % 4], in0=xt0[jj],
                scalar1=stat[:, 32 + jj:33 + jj],
                scalar2=stat[:, 64 + jj:65 + jj],
                op0=OP.subtract, op1=OP.mult),
                deps=[t_r0, tr_tok[jj - 4] if jj >= 4 else None])
        for jj in range(w0, w0 + 4):
            tr_tok[jj] = E("t", lambda t, jj=jj: [
                t.transpose(pp[jj % 2][:, 128 * i:128 * i + 128],
                            xn_tm[jj % 4][:, 128 * i:128 * i + 128], ident)
                for i in range(8)][-1],
                deps=[norm_tok[jj], t_ident] +
                     ([evac_tok[2 * (jj - 2)], evac_tok[2 * (jj - 2) + 1]]
                      if jj >= 2 else []))
            for half in (0, 1):
                srcp = pp[jj % 2][:, 512 * half:512 * half + 512].rearrange(
                    "p (c t) -> p c t", c=4)
                dst = xn3[:, 4 * half:4 * half + 4, 128 * jj:128 * jj + 128]
                if half == 0:
                    tok = E("v", lambda v, srcp=srcp, dst=dst:
                            v.tensor_copy(out=dst, in_=srcp),
                            deps=[tr_tok[jj]])
                else:
                    tok = E("s", lambda s, srcp=srcp, dst=dst:
                            s.copy(out=dst, in_=srcp),
                            deps=[tr_tok[jj]])
                evac_tok.append(tok)
        nt = w0 // 4
        xn_done = [evac_tok[-1], evac_tok[-2]]
        for q2 in range(2):
            def fproj(t, nt=nt, q2=q2):
                last = None
                for cd in range(CH):
                    last = t.matmul(
                        pp[2][:, 512 * q2:512 * q2 + 512],
                        wpt_sb[:, cd * 256 + q2 * 128:cd * 256 + q2 * 128 + 128],
                        xn3[:, cd, 512 * nt:512 * nt + 512],
                        start=(cd == 0), stop=(cd == 7))
                return last
            pp2_free = pr_ev[-1] if pr_ev else t_s6
            pr_mm.append(E("t", fproj, deps=[t_wpt, pp2_free] + xn_done))
            col = q2 * T + 512 * nt
            ps = pp[2][:, 512 * q2:512 * q2 + 512]
            pr_ev.append(E("s", lambda s, ps=ps, col=col, q2=q2: (
                s.activation(out=proj_bf[:, col:col + 512], in_=ps,
                             func=AF.Identity, bias=small[:, 40 + q2:41 + q2]),
                s.activation(out=proj2[:, 512 * q2:512 * q2 + 512], in_=ps,
                             func=AF.Square, bias=small[:, 40 + q2:41 + q2]))[-1],
                deps=[pr_mm[-1], t_small, nrm_mm[-1] if nrm_mm else None]))
        nrm_mm.append(E("t", lambda t: [
            t.matmul(pp[3][0:1, 512:1024], onesc,
                     proj2[:, 512 * q2:512 * q2 + 512],
                     start=(q2 == 0), stop=(q2 == 1))
            for q2 in range(2)][-1],
            deps=[pr_ev[-1], pr_ev[-2], t_sel,
                  nrm_ev[-1] if nrm_ev else None]))
        nrm_ev.append(E("s", lambda s, nt=nt: s.copy(
            out=rb[:, 512 * nt:512 * nt + 512], in_=pp[3][0:1, 512:1024]),
            deps=[nrm_mm[-1]]))
        def fh(t, nt=nt):
            last = None
            for cd in range(CH):
                last = t.matmul(pp[3][0:15, 0:512],
                                wdt_sb[:, 15 * cd:15 * cd + 15],
                                xn3[:, cd, 512 * nt:512 * nt + 512],
                                start=(cd == 0), stop=(cd == 7))
            return last
        h_mm.append(E("t", fh, deps=[t_wdt, hc_ev[-1] if hc_ev else None]
                      + xn_done))
        hc_ev.append(E("v", lambda v, nt=nt: v.tensor_copy(
            out=hcat[:, 512 * nt:512 * nt + 512], in_=pp[3][0:15, 0:512]),
            deps=[h_mm[-1]]))

    xbat = [(0, 2), (2, 2), (4, 4), (8, 2), (10, 2), (12, 1), (13, 1),
            (14, 1), (15, 1)]
    for j0, nb in xbat:
        bt = E("q", lambda q, j0=j0, nb=nb: q.dma_start(
            out=view(16384 * (j0 // 4) + 4096 * (j0 % 4), 4096 * nb, F32,
                     1024 * nb),
            in_=x_t[:, j0:j0 + nb, :]), dma=True)
        for k in range(nb):
            xload[j0 + k] = bt
    for j in range(NTOK):
        if False:
            pass
        E("v", lambda v, j=j: v.tensor_scalar(
            out=stat_scr_v, in0=xt0[j][:, 0:512], scalar1=0.0, scalar2=None,
            op0=OP.add, op1=OP.add, accum_out=stat[:, j:j + 1]),
            deps=[xload[j]])
        vstat_tok[j] = E("v", lambda v, j=j: v.tensor_scalar(
            out=stat_scr_v, in0=xt0[j][:, 512:1024], scalar1=0.0,
            scalar2=None, op0=OP.add, op1=OP.add,
            accum_out=stat[:, 80 + j:81 + j]),
            deps=[xload[j]])
        E("s", lambda s, j=j: s.activation(
            out=stat_scr_s, in_=xt0[j][:, 0:512], func=AF.Square,
            accum_out=stat[:, 16 + j:17 + j]),
            deps=[xload[j], t_tab6])
        sstat_tok[j] = E("s", lambda s, j=j: s.activation(
            out=stat_scr_s, in_=xt0[j][:, 512:1024], func=AF.Square,
            accum_out=stat[:, 96 + j:97 + j]),
            deps=[xload[j], t_tab6])
        if j in (4, 8, 12):
            emit_wave(j - 4, [])
    emit_wave(12, [])
    ph0_v = evac_tok[-2]
    ph0_s = evac_tok[-1]

    t_wot = E("g", lambda g: g.dma_start(out=wot_sb, in_=din["wot"].ap()),
              deps=[norm_tok[7], sstat_tok[7]], dma=True)
    t_wt = E("g", lambda g: g.dma_start(out=wt_sb, in_=din["wt"].ap()),
             deps=[norm_tok[7], sstat_tok[7]], dma=True)

    if debug:
        d = dram_dbg("xn", [128, CH * T], BF16)
        E("q", lambda q, d=d: q.dma_start(out=d, in_=xn),
          deps=[ph0_v, ph0_s], dma=True)
        d2 = dram_dbg("stat", [128, 112], F32)
        E("q", lambda q, d=d2: q.dma_start(out=d2, in_=stat),
          deps=[ph0_v, ph0_s], dma=True)

    mix_tok = {}
    omcol = {"k": lambda c: aux[:, 24 + c:25 + c],
             "v": lambda c: aux[:, 32 + c:33 + c],
             "r": lambda c: aux[:, 40 + c:41 + c]}

    def emit_mix_s1(h, p, eng, deps):
        toks = []
        for cd in range(CH):
            if eng == "s":
                def fs1(s, p=p, cd=cd, h=h):
                    dst = mixh[p][:, TH * cd:TH * cd + TH]
                    if h == 0:
                        return s.activation(
                            out=dst[:, 1:TH],
                            in_=xn[:, 2048 * cd:2048 * cd + TH - 1],
                            func=AF.Identity, scale=omcol[p](cd))
                    off = 2048 * cd + TH * h - 1
                    return s.activation(
                        out=dst, in_=xn[:, off:off + TH],
                        func=AF.Identity, scale=omcol[p](cd))
                toks.append(E("s", fs1, deps=deps + [t_om]))
            else:
                def fg1(g, p=p, cd=cd, h=h):
                    dst = mixh[p][:, TH * cd:TH * cd + TH]
                    if h == 0 and cd == 0:
                        return g.tensor_scalar(
                            out=dst[:, 1:TH], in0=xn3[:, cd, 0:TH - 1],
                            scalar1=omcol[p](cd), scalar2=None, op0=OP.mult)
                    off = 2048 * cd + TH * h - 1
                    return g.tensor_scalar(
                        out=dst, in0=xn[:, off:off + TH],
                        scalar1=omcol[p](cd), scalar2=None, op0=OP.mult)
                toks.append(E("g", fg1, deps=deps + [t_om]))
        return toks

    def emit_mix_s2(h, p, s1toks, deps, eng="v"):
        if h == 0:
            E(eng, lambda v, p=p: [
                v.memset(mixh[p][:, TH * cd:TH * cd + 1], 0.0)
                for cd in range(CH)][-1], deps=deps)
        tok = None
        for cd in range(CH):
            def fm2(v, p=p, cd=cd, h=h):
                dst = mixh[p][:, TH * cd:TH * cd + TH]
                off = 2048 * cd + TH * h
                return v.scalar_tensor_tensor(
                    out=dst, in0=xn[:, off:off + TH],
                    scalar=tmcol[p](cd), in1=dst, op0=OP.mult, op1=OP.add)
            tok = E(eng, fm2, deps=[s1toks[cd], t_om] + deps)
        return tok

    def emit_mix(h, deps):
        toks = {}
        s1 = {"k": emit_mix_s1(h, "k", "s", deps),
              "v": emit_mix_s1(h, "v", "s", deps),
              "r": emit_mix_s1(h, "r", "s", deps)}
        for p in "kvr":
            toks[p] = emit_mix_s2(h, p, s1[p], deps)
        return toks

    P.fence()
    ev_all = pr_ev[-1]
    t_rn0 = E("v", lambda v: v.tensor_scalar(
        out=rb, in0=rb, scalar1=1e-24, scalar2=None, op0=OP.max),
        deps=[nrm_ev[-1]])
    t_st = E("v", lambda v: v.tensor_scalar(
        out=tiny[0:1, 0:1], in0=small[0:1, 50:51], scalar1=LOG100,
        scalar2=None, op0=OP.min), deps=[t_small])
    t_st2 = E("s", lambda s: s.activation(out=tiny[0:1, 0:1],
                                          in_=tiny[0:1, 0:1], func=AF.Exp),
              deps=[t_st])
    t_rn1 = E("s", lambda s: (
        s.activation(out=rn, in_=rb, func=AF.Ln),
        s.activation(out=rn, in_=rn, func=AF.Exp, scale=-0.5))[-1],
        deps=[t_rn0])
    t_rn2 = E("v", lambda v: v.tensor_scalar(
        out=rn, in0=rn, scalar1=tiny[0:1, 0:1], scalar2=None, op0=OP.mult),
        deps=[t_rn1, t_st2])

    raw_mm = E("t", lambda t: [
        t.matmul(pp[2 + nt // 2][0:4, 512 * (nt % 2):512 * (nt % 2) + 512],
                 simnf[:, 4 * q2:4 * q2 + 4],
                 proj_bf[:, q2 * T + 512 * nt:q2 * T + 512 * nt + 512],
                 start=(q2 == 0), stop=(q2 == 1))
        for nt in range(4) for q2 in range(2)][-1],
        deps=[t_s6, ev_all])
    t_raw = E("s", lambda s: (
        s.copy(out=raw_sb[:, 0:1024], in_=pp[2][0:4, :]),
        s.copy(out=raw_sb[:, 1024:2048], in_=pp[3][0:4, :]))[-1],
        deps=[raw_mm])
    rnb_mm = E("t", lambda t: [
        t.matmul(pp[hh][0:4, 512 * nt:512 * nt + 512], ones64[0:1, 0:4],
                 rn[0:1, 1024 * hh + 512 * nt:1024 * hh + 512 * nt + 512],
                 start=True, stop=True)
        for hh in range(2) for nt in range(2)][-1],
        deps=[t_rn2])
    t_lg = E("v", lambda v: (
        v.tensor_mul(out=raw_sb[:, 0:1024], in0=raw_sb[:, 0:1024],
                     in1=pp[0][0:4, :]),
        v.tensor_mul(out=raw_sb[:, 1024:2048], in0=raw_sb[:, 1024:2048],
                     in1=pp[1][0:4, :]))[-1],
        deps=[rnb_mm, t_raw, t_rn0])
    lgt_mm = E("t", lambda t: [
        t.transpose(pp[2 + j // 8][:, 4 * (j % 8):4 * (j % 8) + 4],
                    raw_sb[:, 128 * j:128 * j + 128], ident[0:4, 0:4])
        for j in range(16)][-1], deps=[t_lg, hc_ev[-1], ev_all])
    t_lgc = E("v", lambda v: (
        v.tensor_copy(out=LG[:, 0:32], in_=pp[2][:, 0:32]),
        v.tensor_copy(out=LG[:, 32:64], in_=pp[3][:, 0:32]))[-1],
        deps=[lgt_mm])

    Lc = lambda i: LG[:, i:64:4]
    Dc = lambda i: GD[:, i:64:4]
    Ec = lambda i: GE[:, i:64:4]
    Sc = lambda i: SEL[:, i:64:4]
    Nc = lambda i: NB[:, i:64:4]
    dmix = [norm_tok[15], sstat_tok[15], t_small]
    s1k0 = emit_mix_s1(0, "k", "s", dmix)
    t_gm = E("v", lambda v: (
        v.tensor_max(out=GS, in0=Lc(0), in1=Lc(1)),
        v.tensor_max(out=GR, in0=Lc(2), in1=Lc(3)),
        v.tensor_copy(out=aux[:, 8:12], in_=car[:, 0:4]),
        v.tensor_max(out=GS, in0=GS, in1=GR),
        v.tensor_copy(out=aux[:, 8:12], in_=car[:, 0:4]),
        v.tensor_sub(out=Dc(0), in0=Lc(0), in1=GS),
        v.tensor_sub(out=Dc(1), in0=Lc(1), in1=GS),
        v.tensor_sub(out=Dc(2), in0=Lc(2), in1=GS),
        v.tensor_sub(out=Dc(3), in0=Lc(3), in1=GS))[-1], deps=[t_lgc])
    t_ge = E("s", lambda s: s.activation(out=GE, in_=GD, func=AF.Exp),
             deps=[t_gm])
    s1v0 = emit_mix_s1(0, "v", "s", dmix)
    t_gs = E("v", lambda v: (
        v.tensor_add(out=GS, in0=Ec(0), in1=Ec(1)),
        v.tensor_add(out=GR, in0=Ec(2), in1=Ec(3)),
        v.tensor_copy(out=aux[:, 8:12], in_=car[:, 0:4]),
        v.tensor_add(out=GS, in0=GS, in1=GR))[-1], deps=[t_ge])
    mix_tok[0] = {}
    mix_tok[0]["k"] = emit_mix_s2(0, "k", s1k0, dmix)
    t_gr = E("s", lambda s: (
        s.activation(out=GR, in_=GS, func=AF.Ln),
        s.activation(out=GR, in_=GR, func=AF.Exp, scale=-1.0))[-1],
        deps=[t_gs])
    t_sel2 = E("v", lambda v: (
        v.tensor_single_scalar(out=Sc(0), in_=Dc(0), scalar=0.0,
                               op=OP.is_equal),
        v.tensor_single_scalar(out=Sc(1), in_=Dc(1), scalar=0.0,
                               op=OP.is_equal),
        v.tensor_single_scalar(out=Sc(2), in_=Dc(2), scalar=0.0,
                               op=OP.is_equal),
        v.tensor_single_scalar(out=Sc(3), in_=Dc(3), scalar=0.0,
                               op=OP.is_equal),
        v.tensor_scalar(out=Nc(0), in0=Sc(0), scalar1=-1.0, scalar2=1.0,
                        op0=OP.mult, op1=OP.add),
        v.tensor_scalar(out=Nc(1), in0=Sc(1), scalar1=-1.0, scalar2=1.0,
                        op0=OP.mult, op1=OP.add),
        v.tensor_scalar(out=Nc(2), in0=Sc(2), scalar1=-1.0, scalar2=1.0,
                        op0=OP.mult, op1=OP.add),
        v.tensor_mul(out=Sc(1), in0=Sc(1), in1=Nc(0)),
        v.tensor_mul(out=Nc(1), in0=Nc(0), in1=Nc(1)),
        v.tensor_mul(out=GT4[:, 0:64:4], in0=Sc(0), in1=GR),
        v.tensor_mul(out=Sc(2), in0=Sc(2), in1=Nc(1)),
        v.tensor_mul(out=Nc(2), in0=Nc(1), in1=Nc(2)),
        v.tensor_mul(out=GT4[:, 1:64:4], in0=Sc(1), in1=GR),
        v.tensor_mul(out=Sc(3), in0=Sc(3), in1=Nc(2)),
        v.tensor_mul(out=GT4[:, 2:64:4], in0=Sc(2), in1=GR),
        v.tensor_mul(out=GT4[:, 3:64:4], in0=Sc(3), in1=GR))[-1],
        deps=[t_gr])
    g4_mm = E("t", lambda t: [
        t.transpose(pp[j // 8][0:4, 128 * (j % 8):128 * (j % 8) + 128],
                    GT4[:, 4 * j:4 * j + 4], ident)
        for j in range(16)][-1], deps=[t_sel2, t_lg])
    t_g4 = E("s", lambda s: (
        s.copy(out=G4sb[:, 0:1024], in_=pp[0][0:4, :]),
        s.copy(out=G4sb[:, 1024:2048], in_=pp[1][0:4, :]))[-1],
        deps=[g4_mm])
    mix_tok[0]["v"] = emit_mix_s2(0, "v", s1v0, dmix)
    s1r0 = emit_mix_s1(0, "r", "s", dmix + [t_sel2, g4_mm])
    g15_mm = E("t", lambda t: [
        t.matmul(pp[2 + nt // 2][0:15, 512 * (nt % 2):512 * (nt % 2) + 512],
                 selS, G4sb[:, 512 * nt:512 * nt + 512],
                 start=True, stop=True)
        for nt in range(4)][-1], deps=[t_g4, t_sel, t_lgc])
    t_hg = E("v", lambda v: (
        v.tensor_mul(out=hcat[:, 0:1024], in0=hcat[:, 0:1024],
                     in1=pp[2][0:15, :]),
        v.tensor_mul(out=hcat[:, 1024:2048], in0=hcat[:, 1024:2048],
                     in1=pp[3][0:15, :]))[-1],
        deps=[g15_mm, hc_ev[-1]])
    mix_tok[0]["r"] = emit_mix_s2(0, "r", s1r0, dmix + [t_sel2, g4_mm])

    if debug:
        for nm, buf, dt in (("projbf", proj_bf, F32), ("lgfm", raw_sb, F32),
                            ("LG", LG, F32), ("GT4", GT4, F32),
                            ("hg", hcat, BF16),
                            ("rn2", rn, F32)):
            d = dram_dbg(nm, list(buf.shape), dt)
            E("q", lambda q, d=d, b=buf: q.dma_start(out=d, in_=b),
              deps=[t_hg, t_sel2, t_lg], dma=True)

    P.fence()
    it_toks = []
    carps, carqs = [None] * 8, [None] * 8
    psm = {"k": pp[0], "v": pp[1], "r": pp[2]}

    def emit_late(pi, pit):
        s2 = pi % 2
        Dp, Ep = DD[s2], EE[s2]
        pit["tden"] = E("s", lambda s, Dp=Dp, Ep=Ep: s.activation(
            out=Ep, in_=Dp, func=AF.Identity, bias=1.0, scale=1.0 + 1e-9),
            deps=[pit["tro"], pit["tya"]])
        pit["t1g"] = E("g", lambda g, Dp=Dp: g.tensor_mul(
            out=Dp, in0=Dp, in1=qbuf[:, 0:TH]),
            deps=[pit["tden"], pit["tqs"], pit["tro"]])
        pit["t20"] = E("g", lambda g, Dp=Dp: g.tensor_add(
            out=Dp, in0=Dp, in1=v_sb))
        pit["t21"] = E("v", lambda v, Ep=Ep, Rb=rsb:
                       v.scalar_tensor_tensor(
                           out=Ep, in0=Rb, scalar=1.0, in1=Ep, op0=OP.add,
                           op1=OP.mult), deps=[pit["ev"]["r"], pit["tden"]])
        pit["trd"] = E("s", lambda s, Ep=Ep: (
            s.activation(out=Ep, in_=Ep, func=AF.Ln),
            s.activation(out=Ep, in_=Ep, func=AF.Exp, scale=-1.0))[-1],
            deps=[pit["t21"]])

    def emit_ty(pi, pit):
        s2 = pi % 2
        h2, c2 = pi // 8, pi % 8
        pit["ty"] = E("g", lambda g, c=c2, h=h2, Dp=DD[s2], Ep=EE[s2]:
                      g.tensor_mul(out=y3[:, c, TH * h:TH * h + TH],
                                   in0=Dp, in1=Ep),
                      deps=[pit["trd"], pit["t20"]])

    for i, (h, c) in enumerate([(h, c) for h in range(2) for c in range(CH)]):
        it = {}
        s2 = i % 2
        D_b, E_b = DD[s2], EE[s2]
        prev = it_toks[i - 1] if i >= 1 else None
        if prev:
            emit_late(i - 1, prev)
        nxt = i + 1
        if nxt < 16:
            nh, ncc = nxt // 8, nxt % 8
            if (ncc, nh) not in slice_toks:
                pv = it_toks[i - 1]["mm"] if i >= 1 else None
                deps = {p: [pv[p]] for p in "kvr"} if pv else {}
                slice_toks[(ncc, nh)] = load_slices(ncc, nh, deps)
        if h == 1 and c == 0:
            mix_tok[1] = emit_mix(1, [it_toks[7]["mm"][p] for p in "kvr"])
        sl = slice_toks[(c, h)]
        it["mm"] = {}
        rb_i = pp[2 + i % 2]
        for p in "kvr":
            def fmm(t, p=p, c=c, rb_i=rb_i):
                tgt = rb_i if p == "r" else psm[p]
                last = None
                for nt in range(2):
                    for cd in range(CH):
                        last = t.matmul(
                            tgt[:, 512 * nt:512 * nt + 512],
                            ws[p][c % 2][:, 128 * cd:128 * cd + 128],
                            mixh[p][:, TH * cd + 512 * nt:
                                     TH * cd + 512 * nt + 512],
                            start=(cd == 0), stop=(cd == 7))
                return last
            pgate = None
            if prev:
                pgate = {"k": prev["t13"], "v": prev["ev"]["v"],
                         "r": it_toks[i - 2]["ev"]["r"] if i >= 2
                         else None}[p]
            it["mm"][p] = E("t", fmm, deps=[sl[p], mix_tok[h][p], pgate])
        it["ev"] = {}
        war_q = prev["t1g"] if prev else None
        if h == 0:
            E("v", lambda v: v.memset(pbuf[:, 0:1], -1e30))
        else:
            E("v", lambda v, c=c: v.tensor_copy(out=pbuf[:, 0:1],
                                                in_=car[:, c:c + 1]),
              deps=[carps[c]])
        tps = E("v", lambda v, c=c: v.tensor_tensor_scan(
            out=pbuf[:, 1:TH + 1], data0=wcol(c).broadcast_to([128, TH]),
            data1=psm["k"], initial=pbuf[:, 0:1], op0=OP.add, op1=OP.max),
            deps=[it["mm"]["k"],
                  carps[i - 1] if 1 <= i <= 8 else None])
        t7 = E("v", lambda v: v.tensor_sub(out=B_b, in0=psm["k"],
                                           in1=pbuf[:, 1:TH + 1]),
               deps=[prev["tqs"] if prev else None])
        tb = E("s", lambda s: s.activation(out=B_b, in_=B_b, func=AF.Exp),
               deps=[t7])
        it["ev"]["v"] = E("s", lambda s: s.copy(out=v_sb, in_=psm["v"]),
                          deps=[it["mm"]["v"], prev["t20"] if prev else None])
        t4 = E("v", lambda v, c=c: v.scalar_tensor_tensor(
            out=A_b, in0=pbuf[:, 0:TH], scalar=wcol(c),
            in1=pbuf[:, 1:TH + 1], op0=OP.add, op1=OP.subtract),
            deps=[prev["tqs"] if prev else None])
        ta = E("s", lambda s: s.activation(out=A_b, in_=A_b, func=AF.Exp),
               deps=[t4])
        it["ev"]["r"] = E("s", lambda s, rb_i=rb_i: s.activation(
            out=rsb, in_=rb_i, func=AF.Exp, scale=-1.0),
            deps=[it["mm"]["r"],
                  it_toks[i - 1]["t21"] if i >= 1 else None])
        if h == 0:
            E("v", lambda v: v.memset(qbuf[:, 0:1], 0.0), deps=[war_q])
        else:
            E("v", lambda v, c=c: v.tensor_copy(out=qbuf[:, 0:1],
                                                in_=car[:, 8 + c:9 + c]),
              deps=[war_q, carqs[c]])
        tya = it_toks[i - 2]["ty"] if i >= 2 else None
        it["t13"] = E("v", lambda v, c=c, E_b=E_b: v.scalar_tensor_tensor(
            out=E_b, in0=pbuf[:, 0:TH], scalar=ucol(c), in1=psm["k"],
            op0=OP.subtract, op1=OP.subtract), deps=[tya])
        it["tdc"] = E("v", lambda v, D_b=D_b, E_b=E_b: v.tensor_scalar(
            out=D_b, in0=E_b, scalar1=30.0, scalar2=0.0,
            op0=OP.min, op1=OP.add))
        tbv = E("g", lambda g: g.tensor_mul(out=B_b, in0=B_b, in1=v_sb),
                deps=[tb, it["ev"]["v"]])
        if i >= 1 and i <= 8:
            carqs[i - 1] = E("s", lambda s, i=i: s.copy(
                out=car[:, 7 + i:8 + i], in_=qbuf[:, TH:TH + 1]),
                deps=[it_toks[i - 1]["tqs"]])
        if h == 0:
            carps[c] = E("s", lambda s, c=c: s.copy(
                out=car[:, c:c + 1], in_=pbuf[:, TH:TH + 1]), deps=[tps])
        it["tro"] = E("s", lambda s, D_b=D_b: s.activation(
            out=D_b, in_=D_b, func=AF.Exp), deps=[it["tdc"]])
        it["tya"] = tya
        if prev:
            emit_ty(i - 1, prev)
        it["tqs"] = E("v", lambda v: v.tensor_tensor_scan(
            out=qbuf[:, 1:TH + 1], data0=A_b, data1=B_b,
            initial=qbuf[:, 0:1], op0=OP.mult, op1=OP.add),
            deps=[ta, tbv, war_q,
                  carqs[i - 1] if 1 <= i <= 8 else None])
        if debug and i == 0:
            for nm, buf in (("v0", v_sb),
                            ("p0", pbuf), ("q0", qbuf)):
                d = dram_dbg(nm, list(buf.shape), F32)
                E("q", lambda q, d=d, b=buf: q.dma_start(out=d, in_=b),
                  deps=[it["tqs"]], dma=True)
        it_toks.append(it)
    emit_late(15, it_toks[15])
    emit_ty(15, it_toks[15])

    if debug:
        d = dram_dbg("y", [128, CH * T], BF16)
        E("q", lambda q, d=d: q.dma_start(out=d, in_=y),
          deps=[it_toks[-1]["ty"]], dma=True)
        for p in "kvr":
            d = dram_dbg("mix" + p, [128, CH * TH], BF16)
            E("q", lambda q, d=d, b=mixh[p]: q.dma_start(out=d, in_=b),
              deps=[it_toks[-1]["ty"]], dma=True)

    it15 = it_toks[15]
    t_tab0 = E("s", lambda s: s.add_instruction(mybir.InstLoadActFuncSet(
        name=s.bass.get_next_instruction_name(), act_func_set_id=0, ins=[],
        outs=[])), deps=[it15["trd"]])
    store_toks = []
    j_toks = [None] * NTOK
    for j in range(NTOK):
        prev2 = j_toks[j - 2] if j >= 2 else None
        tx = E("q", lambda q, j=j: q.dma_start(out=xt5[j % 2],
                                               in_=x_t[:, j, :]),
               deps=[prev2["out"] if prev2 else None,
                     it15["mm"]["k"] if j < 2 else None], dma=True)
        pss = pp[j % 2]
        psf = pp[2 + j % 2]
        yh = it_toks[7]["ty"] if j < 8 else it15["ty"]
        smm = E("t", lambda t, j=j, pss=pss: [
            t.matmul(pss[:, 512 * nt:512 * nt + 512],
                     y3[:, ce, 128 * j:128 * j + 128],
                     wot3[:, ce, 512 * nt:512 * nt + 512],
                     start=(ce == 0), stop=(ce == 7))
            for ce in range(CH) for nt in range(2)][-1],
            deps=[t_wot, prev2["sev"] if prev2 else None,
                  yh if j in (0, 1, 8, 9) else None,
                  it15["t13"] if j == 0 else None,
                  it15["ev"]["v"] if j == 1 else None])
        sev = E("s", lambda s, j=j, pss=pss: s.copy(out=OT[j % 2], in_=pss),
                deps=[smm, store_toks[j - 2] if j >= 2 else None])

        def ffm(t, j=j, psf=psf):
            for ce in range(CH):
                for nt in range(2):
                    t.matmul(psf[:, 512 * nt:512 * nt + 512],
                             xn3[:, ce, 128 * j:128 * j + 128],
                             wt3[:, ce, 512 * nt:512 * nt + 512],
                             start=(ce == 0), stop=False)
            last = None
            for nt in range(2):
                last = t.matmul(psf[:, 512 * nt:512 * nt + 512],
                                hcat[:, 128 * j:128 * j + 128],
                                wut_sb[:, 512 * nt:512 * nt + 512],
                                start=False, stop=True)
            return last
        fmm = E("t", ffm, deps=[t_wt, t_wut,
                                prev2["ep"] if prev2 else None,
                                prev2["sev2"] if prev2 else None,
                                it_toks[14]["t21"] if j == 0 else None,
                                it15["t21"] if j == 1 else None,
                                ph0_v if j == 0 else None,
                                ph0_s if j == 0 else None,
                                t_hg if j == 0 else None])
        tep = E("s", lambda s, j=j, psf=psf: s.activation(
            out=EP[j % 2], in_=psf, func=AF.Tanh, scale=0.5),
            deps=[fmm, t_tab0])
        sev2 = E("s", lambda s, j=j, psf=psf: s.copy(out=GZ[j % 2], in_=psf),
                 deps=[fmm, prev2["out"] if prev2 else None])
        tz1 = E("v", lambda v, j=j: v.scalar_tensor_tensor(
            out=GZ[j % 2], in0=EP[j % 2], scalar=1.0, in1=GZ[j % 2],
            op0=OP.add, op1=OP.mult), deps=[tep, sev2])
        tz = E("v", lambda v, j=j: v.tensor_mul(
            out=GZ[j % 2], in0=GZ[j % 2], in1=OT[j % 2]), deps=[sev])
        tout = E("v", lambda v, j=j: v.scalar_tensor_tensor(
            out=OT[j % 2], in0=GZ[j % 2], scalar=0.5, in1=xt5[j % 2],
            op0=OP.mult, op1=OP.add),
            deps=[tx])
        st = E("q", lambda q, j=j: q.dma_start(out=out_t[:, j, :],
                                               in_=OT[j % 2]),
               deps=[tout], dma=True)
        store_toks.append(st)
        j_toks[j] = dict(out=tout, z=tz, z1=tz1, ep=tep, sev=sev, sev2=sev2)

    all_q_dmas = [("q", i, True) for i, op in enumerate(P.ops["q"]) if op[2]]
    all_g_dmas = [("g", i, True) for i, op in enumerate(P.ops["g"]) if op[2]]
    E("q", lambda q: None, deps=store_toks + all_q_dmas + all_g_dmas)

    P.resolve()
    with ctx:
        from contextlib import ExitStack as ES2
        with ES2() as sem_ctx:
            sems = {}
            for e in Plan.ENGINES:
                sems[(e, False)] = sem_ctx.enter_context(
                    nc.semaphore(f"sem_{e}"))
                for j in range(Plan.NDMA.get(e, 1)):
                    sems[(e, "dma", j)] = sem_ctx.enter_context(
                        nc.semaphore(f"semd_{e}{j}"))
            for e in ("g", "s", "v"):
                sems[(e, "ch")] = sem_ctx.enter_context(
                    nc.semaphore(f"ch_{e}"))
            with nc.Block() as block:
                @block.sync
                def _(q):
                    P.replay("q", q, sems, chain_counts=chain_counts)

                @block.gpsimd
                def _(g):
                    P.replay("g", g, sems, sems[("g", "ch")],
                             chain_counts=chain_counts)

                @block.tensor
                def _(t):
                    P.replay("t", t, sems, chain_counts=chain_counts)

                @block.scalar
                def _(s):
                    P.replay("s", s, sems, sems[("s", "ch")],
                             chain_counts=chain_counts)

                @block.vector
                def _(v):
                    P.replay("v", v, sems, sems[("v", "ch")],
                             chain_counts=chain_counts)
    return nc, dbg, P


def kernel(**inputs):
    from concourse.bass_utils import run_bass_kernel_spmd
    if "nc" not in _CACHE:
        _CACHE["nc"], _ = build_program(debug=False)
    nc = _CACHE["nc"]
    shared = _prep_shared(inputs)
    x = np.asarray(inputs["L0_attn_x"])
    in_maps = []
    for b in range(8):
        m = dict(shared)
        m["x"] = np.ascontiguousarray(x[b])
        in_maps.append(m)
    res = run_bass_kernel_spmd(nc, in_maps, core_ids=list(range(8)))
    out = np.stack([np.asarray(res.results[b]["out"]) for b in range(8)],
                   axis=0)
    return out.astype(np.float32)


# version 56 (final)
# speedup vs baseline: 1.0791x; 1.0201x over previous
import numpy as np
import concourse.bass as bass
import concourse.mybir as mybir

F32 = mybir.dt.float32
BF16 = mybir.dt.bfloat16
U8 = mybir.dt.uint8
OP = mybir.AluOpType
AF = mybir.ActivationFunctionType

T = 2048
D = 1024
CH = 8
NTOK = 16
TH = 1024
NEG30 = -30.0
LN_EPS = 1e-5
LOG100 = float(np.log(100.0))
LN_1EM9 = float(np.log(1e-9))

_CACHE = {}
_TRACE_EMIT = False


class Plan:

    ENGINES = ("q", "g", "t", "s", "v")

    def __init__(self):
        self.ops = {e: [] for e in self.ENGINES}
        self.marked = set()
        self.fence_pending = {}
        self.chain_cum = {}
        self.src = {}

    def emit(self, eng, fn, deps=(), dma=False):
        deps = [d for d in deps if d is not None]
        deps = self.fence_pending.pop(eng, []) + deps
        for d in deps:
            self.marked.add((d[0], d[1]))
        idx = len(self.ops[eng])
        self.ops[eng].append([fn, deps, dma, False])
        if _TRACE_EMIT:
            import traceback
            fr = traceback.extract_stack(limit=3)[0]
            self.src[(eng, idx)] = f"{fr.lineno}"
        return (eng, idx, dma)

    def fence(self):
        last = {}
        for e in self.ENGINES:
            if self.ops[e]:
                last[e] = (e, len(self.ops[e]) - 1, self.ops[e][-1][2])
        for e in self.ENGINES:
            toks = [t for o, t in last.items() if o != e]
            self.fence_pending.setdefault(e, []).extend(toks)

    NDMA = {"q": 8, "g": 4}

    def resolve(self):
        self.cum = {e: [0] * len(self.ops[e]) for e in self.ENGINES}
        self.dma_slot = {}
        for e in self.ENGINES:
            c_cmp = 0
            k = 0
            per_slot = [0] * self.NDMA.get(e, 1)
            for i, op in enumerate(self.ops[e]):
                if op[2] or (e, i) in self.marked:
                    op[3] = True
                    if not op[2]:
                        c_cmp += 1
                if op[2]:
                    j = k % len(per_slot)
                    per_slot[j] += 16
                    self.dma_slot[(e, i)] = (j, per_slot[j])
                    k += 1
                    self.cum[e][i] = per_slot[j]
                else:
                    self.cum[e][i] = c_cmp

    CHAINED = ("g", "s", "v")

    def replay(self, eng_name, eng_obj, sems, chain_sem=None, lag=1,
               chain_counts=None):
        if chain_sem is not None:
            eng_obj = _ChainEng(eng_obj, chain_sem, lag)
        self.chain_cum[eng_name] = [0] * len(self.ops[eng_name])
        maxw = {}
        for i, (fn, deps, dma, inc) in enumerate(self.ops[eng_name]):
            for d in deps:
                de, di, ddma = d[0], d[1], d[2]
                hard_d = len(d) == 4 and d[3]
                if ddma:
                    j, val = self.dma_slot[(de, di)]
                    sem = sems[(de, "dma", j)]
                elif de in self.CHAINED:
                    if de == eng_name:
                        continue
                    sem = sems[(de, "ch")]
                    val = chain_counts[de][di] if chain_counts else 0
                else:
                    if de == eng_name and not hard_d:
                        continue
                    sem = sems[(de, False)]
                    val = self.cum[de][di]
                if maxw.get(id(sem), -1) >= val:
                    continue
                maxw[id(sem)] = val
                eng_obj.wait_ge(sem, val)
            instr = fn(eng_obj)
            if chain_sem is not None:
                self.chain_cum[eng_name][i] = eng_obj._n
            if dma:
                j, val = self.dma_slot[(eng_name, i)]
                sem = sems[(eng_name, "dma", j)]
                if val > 16:
                    instr._wait_ge(sem, val - 16)
                instr.then_inc(sem, 16)
            elif inc and chain_sem is None:
                assert instr is not None, f"marked op {eng_name}[{i}] is None"
                instr.then_inc(sems[(eng_name, False)], 1)


class _ChainEng:

    _SKIP = frozenset((
        "wait_ge", "wait_op", "sem_inc", "sem_clear", "maybe_wait_ge",
        "dma_start", "add_instruction", "preload_act_table",
    ))

    def __init__(self, eng, sem, lag):
        self._eng = eng
        self._sem = sem
        self._lag = lag
        self._n = 0

    def __getattr__(self, name):
        f = getattr(self._eng, name)
        if name in self._SKIP or not callable(f):
            return f

        def wrap(*a, **k):
            r = f(*a, **k)
            if hasattr(r, "then_inc"):
                n = self._n
                w = n + 1 - self._lag
                if w > 0:
                    r._wait_ge(self._sem, w)
                r.then_inc(self._sem, 1)
                self._n = n + 1
            return r

        return wrap


def hard(tok):
    return (tok[0], tok[1], tok[2], True)


def _fm(W):
    R, C = W.shape
    return np.ascontiguousarray(
        W.reshape(R // 128, 128, C).transpose(1, 0, 2).reshape(128, -1))


def _col8(v):
    return np.ascontiguousarray(v.reshape(8, 128).T)


def _prep_shared(inp):
    wd = np.concatenate([inp[f"L0_ffn_Wd{i}"] for i in range(4)], axis=0)
    wu = np.concatenate([inp[f"L0_ffn_Wu{i}"] for i in range(4)], axis=1)
    small = np.zeros((128, 324), np.float32)
    small[:, 0:8] = _col8(inp["L0_attn_time_mix_k"])
    small[:, 8:16] = _col8(inp["L0_attn_time_mix_v"])
    small[:, 16:24] = _col8(inp["L0_attn_time_mix_r"])
    small[:, 24:32] = _col8(inp["L0_attn_time_decay"])
    small[:, 32:40] = _col8(inp["L0_attn_time_first"])
    small[:, 40:42] = np.ascontiguousarray(inp["L0_ffn_bp"].reshape(2, 128).T)
    small[:, 42:50] = np.ascontiguousarray(
        np.asarray(inp["L0_ffn_sim"]).reshape(2, 128, 4)
        .transpose(1, 0, 2).reshape(128, 8))
    small[:, 50] = float(np.asarray(inp["L0_ffn_temp"]).reshape(-1)[0])
    small[:, 51] = 1.0
    sel = np.zeros((4, 15), np.float32)
    sel[0, 0] = sel[1, 1:3] = sel[2, 3:7] = sel[3, 7:15] = 1.0
    small[32:36, 52:67] = sel
    small[:, 67] = np.log(1e-9)
    small[0, 68:196] = 1.0
    small[64, 68:196] = 1.0
    small[:, 196:324] = np.eye(128, dtype=np.float32)
    ms = [_fm(np.ascontiguousarray(np.asarray(inp["L0_attn_W_" + p]).T))
          .reshape(128, 8, 8, 128) for p in "kvr"]
    return dict(
        wkvr=np.ascontiguousarray(
            np.stack(ms, axis=1).transpose(0, 3, 1, 2, 4)
            .reshape(128, 3 * CH * D)),
        wot=_fm(np.ascontiguousarray(np.asarray(inp["L0_attn_W_o"]).T)),
        wt=_fm(np.ascontiguousarray(np.asarray(inp["L0_ffn_W"]).T)),
        wpt=_fm(np.ascontiguousarray(np.asarray(inp["L0_ffn_Wp"]).T)),
        wdt=_fm(np.ascontiguousarray(np.asarray(wd).T)),
        wut=np.ascontiguousarray(np.asarray(wu).T),
        small=small)


def build_program(debug=False):
    _, _, p1 = _build_program(debug, None)
    nc, dbg, _ = _build_program(debug, p1.chain_cum)
    return nc, dbg


def _build_program(debug=False, chain_counts=None):
    nc = bass.Bass("TRN2", target_bir_lowering=False)
    P = Plan()
    E = P.emit

    x_h = nc.dram_tensor("x", [T, D], F32, kind="ExternalInput")
    out_h = nc.dram_tensor("out", [T, D], F32, kind="ExternalOutput")
    din = {n: nc.dram_tensor(n, s, F32, kind="ExternalInput") for n, s in [
        ("wkvr", [128, 3 * CH * D]),
        ("wot", [128, CH * D]), ("wt", [128, CH * D]), ("wpt", [128, CH * 256]),
        ("wdt", [128, CH * 15]), ("wut", [15, D]), ("small", [128, 324])]}

    dbg = {}

    def dram_dbg(name, shape, dtype):
        ap = nc.dram_tensor("dbg_" + name, shape, dtype,
                            kind="ExternalOutput").ap()
        dbg[name] = ap
        return ap

    x_t = x_h.ap().rearrange("(j p) d -> p j d", p=128)
    out_t = out_h.ap().rearrange("(j p) d -> p j d", p=128)

    from contextlib import ExitStack
    ctx = ExitStack()
    sb = lambda n, s, d: ctx.enter_context(nc.sbuf_tensor(n, s, d)).ap()

    xn = sb("xn", [128, CH * T], BF16)
    y = sb("y", [128, CH * T], BF16)
    ws_all = [sb(f"ws_all{i}", [128, 3 * CH * 128], BF16) for i in range(2)]
    ws = {p: [ws_all[i][:, pi * CH * 128:(pi + 1) * CH * 128]
              for i in range(2)] for pi, p in enumerate("kvr")}
    hcat = sb("hcat", [15, T], BF16)
    wut_sb = sb("wut_sb", [15, D], BF16)
    wdt_sb = sb("wdt_sb", [128, CH * 15], BF16)
    small = sb("small_sb", [128, 324], F32)

    stat = sb("stat", [128, 112], F32)
    car = sb("car", [128, 16], F32)
    aux = sb("aux", [128, 48], F32)
    tiny = sb("tiny", [4, 64], F32)
    selS = small[32:36, 52:67]
    ones1 = small[0:1, 68:196]
    ones64 = small[64:65, 68:196]
    onesc = small[:, 51:52]
    simnf = aux[:, 0:8]

    AR = 124960
    arena = ctx.enter_context(nc.sbuf_tensor("arena", [128, AR], U8)).ap()

    def view(off, nbytes, dtype, cols):
        ap = arena[:, off:off + nbytes].bitcast(dtype)
        assert ap.shape[1] == cols, (ap.shape, cols)
        return ap

    xt0 = [view(i * 4096, 4096, F32, 1024) for i in range(16)]
    xn_tm = [view(65536 + i * 4096, 4096, F32, 1024) for i in range(4)]
    wot_sb = view(0, 16384, BF16, CH * D)
    wt_sb = view(16384, 16384, BF16, CH * D)
    VP = view(81920, 8192, F32, 2048)
    raw_sb = VP[0:4, :]
    G4sb = VP[32:36, :]
    rn = VP[64:65, :]
    rb = VP[96:97, :]
    proj_bf = view(90112, 16384, F32, 2 * T)
    proj2 = view(106496, 4096, F32, T // 2)
    gb = 73728
    LG = view(gb, 256, F32, 64)
    GD = view(gb + 256, 256, F32, 64)
    GE = view(gb + 512, 256, F32, 64)
    SEL = view(gb + 768, 256, F32, 64)
    NB = view(gb + 1024, 256, F32, 64)
    GT4 = view(gb + 1280, 256, F32, 64)
    GS = view(gb + 1536, 64, F32, 16)
    GR = view(gb + 1600, 64, F32, 16)
    PH3 = 32768
    mixh = {p: view(PH3 + i * 16384, 16384, BF16, CH * TH)
            for i, p in enumerate("kvr")}

    sc_off = PH3 + 51200
    v_sb = view(sc_off + 1 * 4096, 4096, F32, TH)
    A_b = view(sc_off + 3 * 4096, 4096, F32, TH)
    B_b = view(sc_off + 4 * 4096, 4096, F32, TH)
    DD = [view(sc_off + 0 * 4096, 4096, F32, TH), view(112672, 4096, F32, TH)]
    EE = [view(sc_off + 2 * 4096, 4096, F32, TH), view(116768, 4096, F32, TH)]
    pbuf = view(sc_off + 20480, 4112, F32, 1028)
    qbuf = view(sc_off + 24592, 4112, F32, 1028)
    rsb = view(120864, 4096, F32, TH)
    PH5 = 32768
    xt5 = [view(PH5 + i * 4096, 4096, F32, 1024) for i in range(2)]
    wpt_sb = view(110592, 4096, BF16, CH * 256)
    OT = [view(PH5 + 8192 + i * 4096, 4096, F32, 1024) for i in range(2)]
    EP = [view(PH5 + 16384 + i * 4096, 4096, F32, 1024) for i in range(2)]
    GZ = [view(PH5 + 24576 + i * 4096, 4096, F32, 1024) for i in range(2)]

    pp = [ctx.enter_context(nc.psum_tensor(f"pp{i}", [128, 1024], F32)).ap()
          for i in range(4)]

    xn3 = xn.rearrange("p (c t) -> p c t", c=CH)
    y3 = y.rearrange("p (c t) -> p c t", c=CH)
    wot3 = wot_sb.rearrange("p (c t) -> p c t", c=CH)
    wt3 = wt_sb.rearrange("p (c t) -> p c t", c=CH)
    wsl_all = din["wkvr"].ap().rearrange("p (s r c e) -> p s r c e",
                                         s=8, r=3, c=CH)

    tmcol = {"k": lambda c: small[:, 0 + c:1 + c],
             "v": lambda c: small[:, 8 + c:9 + c],
             "r": lambda c: small[:, 16 + c:17 + c]}
    wcol = lambda c: small[:, 24 + c:25 + c]
    ucol = lambda c: small[:, 32 + c:33 + c]

    t_small = E("q", lambda q: q.dma_start(out=small, in_=din["small"].ap()),
                dma=True)
    t_wpt = E("g", lambda g: g.dma_start(out=wpt_sb, in_=din["wpt"].ap()),
              dma=True)
    t_wdt = E("g", lambda g: g.dma_start(out=wdt_sb, in_=din["wdt"].ap()),
              dma=True)
    t_wut = E("g", lambda g: g.dma_start(out=wut_sb, in_=din["wut"].ap()),
              dma=True)

    def load_slices(c, h, deps):
        ktok = E("g", lambda g, c=c: g.dma_start(
            out=ws_all[c % 2][:, 0:CH * 128].rearrange(
                "p (c e) -> p c e", c=CH),
            in_=wsl_all[:, c, 0, :, :]),
            deps=deps.get("k", []), dma=True)
        vrtok = E("g", lambda g, c=c: g.dma_start(
            out=ws_all[c % 2][:, CH * 128:3 * CH * 128].rearrange(
                "p (r c e) -> p r c e", r=2, c=CH),
            in_=wsl_all[:, c, 1:3, :, :]),
            deps=deps.get("v", []) + deps.get("r", []), dma=True)
        return {"k": ktok, "v": vrtok, "r": vrtok}

    ident = small[:, 196:324]
    t_ident = t_small

    t_sel = t_small

    slice_toks = {(0, 0): load_slices(0, 0, {}), (1, 0): load_slices(1, 0, {})}
    E("v", lambda v: v.memset(car, 0.0))
    t_om = E("v", lambda v: v.tensor_scalar(
        out=aux[:, 24:48], in0=small[:, 0:24], scalar1=-1.0, scalar2=1.0,
        op0=OP.mult, op1=OP.add), deps=[t_small])

    t_tab6 = E("s", lambda s: s.add_instruction(mybir.InstLoadActFuncSet(
        name=s.bass.get_next_instruction_name(), act_func_set_id=6, ins=[],
        outs=[])))
    t_s2 = E("v", lambda v: v.tensor_mul(out=stat[:, 64:72],
                                         in0=small[:, 42:50],
                                         in1=small[:, 42:50]),
             deps=[t_small])
    sim_mm = E("t", lambda t: t.matmul(pp[2][0:1, 0:8], onesc,
                                       stat[:, 64:72], start=True, stop=True),
               deps=[t_s2])
    t_s3 = E("s", lambda s: s.copy(out=tiny[0:1, 8:16], in_=pp[2][0:1, 0:8]),
             deps=[sim_mm, t_tab6])
    t_s4a = E("v", lambda v: v.tensor_add(
        out=tiny[0:1, 16:20], in0=tiny[0:1, 8:12], in1=tiny[0:1, 12:16]),
        deps=[t_s3])
    t_s4 = E("v", lambda v: v.tensor_scalar(
        out=tiny[0:1, 16:20], in0=tiny[0:1, 16:20],
        scalar1=1e-24, scalar2=None, op0=OP.max),
        deps=[hard(t_s4a)])
    t_s5a = E("s", lambda s: s.activation(
        out=tiny[0:1, 20:24], in_=tiny[0:1, 16:20], func=AF.Ln), deps=[t_s4])
    t_s5 = E("s", lambda s: s.activation(
        out=tiny[0:1, 20:24], in_=tiny[0:1, 20:24], func=AF.Exp,
        scale=-0.5), deps=[hard(t_s5a)])
    simb_mm = E("t", lambda t: t.matmul(pp[2][0:128, 0:4], ones1,
                                        tiny[0:1, 20:24], start=True,
                                        stop=True), deps=[t_s5, t_s3])
    t_s6 = E("v", lambda v: (
        v.tensor_mul(out=simnf[:, 0:4], in0=small[:, 42:46],
                     in1=pp[2][0:128, 0:4]),
        v.tensor_mul(out=simnf[:, 4:8], in0=small[:, 46:50],
                     in1=pp[2][0:128, 0:4]))[-1],
        deps=[simb_mm])

    xload = [None] * NTOK
    norm_tok = [None] * NTOK
    sstat_tok = [None] * NTOK
    vstat_tok = [None] * NTOK
    tr_tok = [None] * NTOK
    evac_tok = []
    stat_scr_v = view(114688, 2048, F32, 512)
    stat_scr_s = view(116736, 2048, F32, 512)
    xb4 = [view(16384 * i, 16384, F32, 4096) for i in range(4)]
    pr_mm, pr_ev, h_mm, hc_ev = [], [], [], []
    nrm_mm, nrm_ev = [], []

    def emit_wave(w0, extra_deps):
        sl = lambda base, w0=w0: stat[:, base + w0:base + w0 + 4]
        t_v0 = E("v", lambda v, sl=sl, w0=w0: (
            v.tensor_add(out=sl(16), in0=sl(16),
                         in1=stat[:, 96 + w0:100 + w0]),
            v.tensor_add(out=sl(0), in0=sl(0),
                         in1=stat[:, 80 + w0:84 + w0]))[-1],
            deps=[sstat_tok[w0 + 3]] + extra_deps)
        t_v1 = E("v", lambda v, sl=sl: (
            v.tensor_scalar(out=sl(32), in0=sl(0), scalar1=1.0 / D,
                            scalar2=None, op0=OP.mult),
            v.tensor_scalar(out=sl(48), in0=sl(16), scalar1=1.0 / D,
                            scalar2=LN_EPS, op0=OP.mult, op1=OP.add),
            v.tensor_mul(out=sl(0), in0=sl(32), in1=sl(32)),
            v.tensor_copy(out=aux[:, 8:12], in_=car[:, 0:4]),
            v.tensor_sub(out=sl(48), in0=sl(48), in1=sl(0)))[-1],
            deps=[t_v0])
        t_r0 = E("s", lambda s, sl=sl: (
            s.activation(out=sl(64), in_=sl(48), func=AF.Ln),
            s.activation(out=sl(64), in_=sl(64), func=AF.Exp,
                         scale=-0.5))[-1],
            deps=[t_v1])
        for jj in range(w0, w0 + 4):
            norm_tok[jj] = E("v", lambda v, jj=jj: v.tensor_scalar(
                out=xn_tm[jj % 4], in0=xt0[jj],
                scalar1=stat[:, 32 + jj:33 + jj],
                scalar2=stat[:, 64 + jj:65 + jj],
                op0=OP.subtract, op1=OP.mult),
                deps=[t_r0, tr_tok[jj - 4] if jj >= 4 else None])
        for jj in range(w0, w0 + 4):
            tr_tok[jj] = E("t", lambda t, jj=jj: [
                t.transpose(pp[jj % 2][:, 128 * i:128 * i + 128],
                            xn_tm[jj % 4][:, 128 * i:128 * i + 128], ident)
                for i in range(8)][-1],
                deps=[norm_tok[jj], t_ident] +
                     ([evac_tok[2 * (jj - 2)], evac_tok[2 * (jj - 2) + 1]]
                      if jj >= 2 else []))
            for half in (0, 1):
                srcp = pp[jj % 2][:, 512 * half:512 * half + 512].rearrange(
                    "p (c t) -> p c t", c=4)
                dst = xn3[:, 4 * half:4 * half + 4, 128 * jj:128 * jj + 128]
                if half == 0:
                    tok = E("v", lambda v, srcp=srcp, dst=dst:
                            v.tensor_copy(out=dst, in_=srcp),
                            deps=[tr_tok[jj]])
                else:
                    tok = E("s", lambda s, srcp=srcp, dst=dst:
                            s.copy(out=dst, in_=srcp),
                            deps=[tr_tok[jj]])
                evac_tok.append(tok)
        nt = w0 // 4
        xn_done = [evac_tok[-1], evac_tok[-2]]
        for q2 in range(2):
            def fproj(t, nt=nt, q2=q2):
                last = None
                for cd in range(CH):
                    last = t.matmul(
                        pp[2][:, 512 * q2:512 * q2 + 512],
                        wpt_sb[:, cd * 256 + q2 * 128:cd * 256 + q2 * 128 + 128],
                        xn3[:, cd, 512 * nt:512 * nt + 512],
                        start=(cd == 0), stop=(cd == 7))
                return last
            pp2_free = pr_ev[-1] if pr_ev else t_s6
            pr_mm.append(E("t", fproj, deps=[t_wpt, pp2_free] + xn_done))
            col = q2 * T + 512 * nt
            ps = pp[2][:, 512 * q2:512 * q2 + 512]
            pr_ev.append(E("s", lambda s, ps=ps, col=col, q2=q2: (
                s.activation(out=proj_bf[:, col:col + 512], in_=ps,
                             func=AF.Identity, bias=small[:, 40 + q2:41 + q2]),
                s.activation(out=proj2[:, 512 * q2:512 * q2 + 512], in_=ps,
                             func=AF.Square, bias=small[:, 40 + q2:41 + q2]))[-1],
                deps=[pr_mm[-1], t_small, nrm_mm[-1] if nrm_mm else None]))
        nrm_mm.append(E("t", lambda t: [
            t.matmul(pp[3][0:1, 512:1024], onesc,
                     proj2[:, 512 * q2:512 * q2 + 512],
                     start=(q2 == 0), stop=(q2 == 1))
            for q2 in range(2)][-1],
            deps=[pr_ev[-1], pr_ev[-2], t_sel,
                  nrm_ev[-1] if nrm_ev else None]))
        nrm_ev.append(E("s", lambda s, nt=nt: s.copy(
            out=rb[:, 512 * nt:512 * nt + 512], in_=pp[3][0:1, 512:1024]),
            deps=[nrm_mm[-1]]))
        def fh(t, nt=nt):
            last = None
            for cd in range(CH):
                last = t.matmul(pp[3][0:15, 0:512],
                                wdt_sb[:, 15 * cd:15 * cd + 15],
                                xn3[:, cd, 512 * nt:512 * nt + 512],
                                start=(cd == 0), stop=(cd == 7))
            return last
        h_mm.append(E("t", fh, deps=[t_wdt, hc_ev[-1] if hc_ev else None]
                      + xn_done))
        hc_ev.append(E("v", lambda v, nt=nt: v.tensor_copy(
            out=hcat[:, 512 * nt:512 * nt + 512], in_=pp[3][0:15, 0:512]),
            deps=[h_mm[-1]]))

    xbat = [(0, 2), (2, 2), (4, 4), (8, 2), (10, 2), (12, 1), (13, 1),
            (14, 1), (15, 1)]
    for j0, nb in xbat:
        bt = E("q", lambda q, j0=j0, nb=nb: q.dma_start(
            out=view(16384 * (j0 // 4) + 4096 * (j0 % 4), 4096 * nb, F32,
                     1024 * nb),
            in_=x_t[:, j0:j0 + nb, :]), dma=True)
        for k in range(nb):
            xload[j0 + k] = bt
    for j in range(NTOK):
        if False:
            pass
        E("v", lambda v, j=j: v.tensor_scalar(
            out=stat_scr_v, in0=xt0[j][:, 0:512], scalar1=0.0, scalar2=None,
            op0=OP.add, op1=OP.add, accum_out=stat[:, j:j + 1]),
            deps=[xload[j]])
        vstat_tok[j] = E("v", lambda v, j=j: v.tensor_scalar(
            out=stat_scr_v, in0=xt0[j][:, 512:1024], scalar1=0.0,
            scalar2=None, op0=OP.add, op1=OP.add,
            accum_out=stat[:, 80 + j:81 + j]),
            deps=[xload[j]])
        E("s", lambda s, j=j: s.activation(
            out=stat_scr_s, in_=xt0[j][:, 0:512], func=AF.Square,
            accum_out=stat[:, 16 + j:17 + j]),
            deps=[xload[j], t_tab6])
        sstat_tok[j] = E("s", lambda s, j=j: s.activation(
            out=stat_scr_s, in_=xt0[j][:, 512:1024], func=AF.Square,
            accum_out=stat[:, 96 + j:97 + j]),
            deps=[xload[j], t_tab6])
        if j in (4, 8, 12):
            emit_wave(j - 4, [])
    emit_wave(12, [])
    ph0_v = evac_tok[-2]
    ph0_s = evac_tok[-1]

    t_wot = E("g", lambda g: g.dma_start(out=wot_sb, in_=din["wot"].ap()),
              deps=[norm_tok[7], sstat_tok[7]], dma=True)
    t_wt = E("g", lambda g: g.dma_start(out=wt_sb, in_=din["wt"].ap()),
             deps=[norm_tok[7], sstat_tok[7]], dma=True)

    if debug:
        d = dram_dbg("xn", [128, CH * T], BF16)
        E("q", lambda q, d=d: q.dma_start(out=d, in_=xn),
          deps=[ph0_v, ph0_s], dma=True)
        d2 = dram_dbg("stat", [128, 112], F32)
        E("q", lambda q, d=d2: q.dma_start(out=d2, in_=stat),
          deps=[ph0_v, ph0_s], dma=True)

    mix_tok = {}
    omcol = {"k": lambda c: aux[:, 24 + c:25 + c],
             "v": lambda c: aux[:, 32 + c:33 + c],
             "r": lambda c: aux[:, 40 + c:41 + c]}

    def emit_mix_s1(h, p, eng, deps):
        toks = []
        for cd in range(CH):
            if eng == "s":
                def fs1(s, p=p, cd=cd, h=h):
                    dst = mixh[p][:, TH * cd:TH * cd + TH]
                    if h == 0:
                        return s.activation(
                            out=dst[:, 1:TH],
                            in_=xn[:, 2048 * cd:2048 * cd + TH - 1],
                            func=AF.Identity, scale=omcol[p](cd))
                    off = 2048 * cd + TH * h - 1
                    return s.activation(
                        out=dst, in_=xn[:, off:off + TH],
                        func=AF.Identity, scale=omcol[p](cd))
                toks.append(E("s", fs1, deps=deps + [t_om]))
            else:
                def fg1(g, p=p, cd=cd, h=h):
                    dst = mixh[p][:, TH * cd:TH * cd + TH]
                    if h == 0 and cd == 0:
                        return g.tensor_scalar(
                            out=dst[:, 1:TH], in0=xn3[:, cd, 0:TH - 1],
                            scalar1=omcol[p](cd), scalar2=None, op0=OP.mult)
                    off = 2048 * cd + TH * h - 1
                    return g.tensor_scalar(
                        out=dst, in0=xn[:, off:off + TH],
                        scalar1=omcol[p](cd), scalar2=None, op0=OP.mult)
                toks.append(E("g", fg1, deps=deps + [t_om]))
        return toks

    def emit_mix_s2(h, p, s1toks, deps, eng="v"):
        if h == 0:
            E(eng, lambda v, p=p: [
                v.memset(mixh[p][:, TH * cd:TH * cd + 1], 0.0)
                for cd in range(CH)][-1], deps=deps)
        tok = None
        for cd in range(CH):
            def fm2(v, p=p, cd=cd, h=h):
                dst = mixh[p][:, TH * cd:TH * cd + TH]
                off = 2048 * cd + TH * h
                return v.scalar_tensor_tensor(
                    out=dst, in0=xn[:, off:off + TH],
                    scalar=tmcol[p](cd), in1=dst, op0=OP.mult, op1=OP.add)
            tok = E(eng, fm2, deps=[s1toks[cd], t_om] + deps)
        return tok

    def emit_mix(h, deps):
        toks = {}
        s1 = {"k": emit_mix_s1(h, "k", "s", deps),
              "v": emit_mix_s1(h, "v", "s", deps),
              "r": emit_mix_s1(h, "r", "s", deps)}
        for p in "kvr":
            toks[p] = emit_mix_s2(h, p, s1[p], deps)
        return toks

    P.fence()
    ev_all = pr_ev[-1]
    t_rn0 = E("v", lambda v: v.tensor_scalar(
        out=rb, in0=rb, scalar1=1e-24, scalar2=None, op0=OP.max),
        deps=[nrm_ev[-1]])
    t_st = E("v", lambda v: v.tensor_scalar(
        out=tiny[0:1, 0:1], in0=small[0:1, 50:51], scalar1=LOG100,
        scalar2=None, op0=OP.min), deps=[t_small])
    t_st2 = E("s", lambda s: s.activation(out=tiny[0:1, 0:1],
                                          in_=tiny[0:1, 0:1], func=AF.Exp),
              deps=[t_st])
    t_rn1 = E("s", lambda s: (
        s.activation(out=rn, in_=rb, func=AF.Ln),
        s.activation(out=rn, in_=rn, func=AF.Exp, scale=-0.5))[-1],
        deps=[t_rn0])
    t_rn2 = E("v", lambda v: v.tensor_scalar(
        out=rn, in0=rn, scalar1=tiny[0:1, 0:1], scalar2=None, op0=OP.mult),
        deps=[t_rn1, t_st2])

    raw_mm = E("t", lambda t: [
        t.matmul(pp[2 + nt // 2][0:4, 512 * (nt % 2):512 * (nt % 2) + 512],
                 simnf[:, 4 * q2:4 * q2 + 4],
                 proj_bf[:, q2 * T + 512 * nt:q2 * T + 512 * nt + 512],
                 start=(q2 == 0), stop=(q2 == 1))
        for nt in range(4) for q2 in range(2)][-1],
        deps=[t_s6, ev_all])
    t_raw = E("s", lambda s: (
        s.copy(out=raw_sb[:, 0:1024], in_=pp[2][0:4, :]),
        s.copy(out=raw_sb[:, 1024:2048], in_=pp[3][0:4, :]))[-1],
        deps=[raw_mm])
    rnb_mm = E("t", lambda t: [
        t.matmul(pp[hh][0:4, 512 * nt:512 * nt + 512], ones64[0:1, 0:4],
                 rn[0:1, 1024 * hh + 512 * nt:1024 * hh + 512 * nt + 512],
                 start=True, stop=True)
        for hh in range(2) for nt in range(2)][-1],
        deps=[t_rn2])
    t_lg = E("v", lambda v: (
        v.tensor_mul(out=raw_sb[:, 0:1024], in0=raw_sb[:, 0:1024],
                     in1=pp[0][0:4, :]),
        v.tensor_mul(out=raw_sb[:, 1024:2048], in0=raw_sb[:, 1024:2048],
                     in1=pp[1][0:4, :]))[-1],
        deps=[rnb_mm, t_raw, t_rn0])
    lgt_mm = E("t", lambda t: [
        t.transpose(pp[2 + j // 8][:, 4 * (j % 8):4 * (j % 8) + 4],
                    raw_sb[:, 128 * j:128 * j + 128], ident[0:4, 0:4])
        for j in range(16)][-1], deps=[t_lg, hc_ev[-1], ev_all])
    t_lgc = E("v", lambda v: (
        v.tensor_copy(out=LG[:, 0:32], in_=pp[2][:, 0:32]),
        v.tensor_copy(out=LG[:, 32:64], in_=pp[3][:, 0:32]))[-1],
        deps=[lgt_mm])

    Lc = lambda i: LG[:, i:64:4]
    Dc = lambda i: GD[:, i:64:4]
    Ec = lambda i: GE[:, i:64:4]
    Sc = lambda i: SEL[:, i:64:4]
    Nc = lambda i: NB[:, i:64:4]
    dmix = [norm_tok[15], sstat_tok[15], t_small]
    s1k0 = emit_mix_s1(0, "k", "s", dmix)
    t_gm = E("v", lambda v: (
        v.tensor_max(out=GS, in0=Lc(0), in1=Lc(1)),
        v.tensor_max(out=GR, in0=Lc(2), in1=Lc(3)),
        v.tensor_copy(out=aux[:, 8:12], in_=car[:, 0:4]),
        v.tensor_max(out=GS, in0=GS, in1=GR),
        v.tensor_copy(out=aux[:, 8:12], in_=car[:, 0:4]),
        v.tensor_sub(out=Dc(0), in0=Lc(0), in1=GS),
        v.tensor_sub(out=Dc(1), in0=Lc(1), in1=GS),
        v.tensor_sub(out=Dc(2), in0=Lc(2), in1=GS),
        v.tensor_sub(out=Dc(3), in0=Lc(3), in1=GS))[-1], deps=[t_lgc])
    t_ge = E("s", lambda s: s.activation(out=GE, in_=GD, func=AF.Exp),
             deps=[t_gm])
    s1v0 = emit_mix_s1(0, "v", "s", dmix)
    t_gs = E("v", lambda v: (
        v.tensor_add(out=GS, in0=Ec(0), in1=Ec(1)),
        v.tensor_add(out=GR, in0=Ec(2), in1=Ec(3)),
        v.tensor_copy(out=aux[:, 8:12], in_=car[:, 0:4]),
        v.tensor_add(out=GS, in0=GS, in1=GR))[-1], deps=[t_ge])
    mix_tok[0] = {}
    mix_tok[0]["k"] = emit_mix_s2(0, "k", s1k0, dmix)
    t_gr = E("s", lambda s: (
        s.activation(out=GR, in_=GS, func=AF.Ln),
        s.activation(out=GR, in_=GR, func=AF.Exp, scale=-1.0))[-1],
        deps=[t_gs])
    t_sel2 = E("v", lambda v: (
        v.tensor_single_scalar(out=Sc(0), in_=Dc(0), scalar=0.0,
                               op=OP.is_equal),
        v.tensor_single_scalar(out=Sc(1), in_=Dc(1), scalar=0.0,
                               op=OP.is_equal),
        v.tensor_single_scalar(out=Sc(2), in_=Dc(2), scalar=0.0,
                               op=OP.is_equal),
        v.tensor_single_scalar(out=Sc(3), in_=Dc(3), scalar=0.0,
                               op=OP.is_equal),
        v.tensor_scalar(out=Nc(0), in0=Sc(0), scalar1=-1.0, scalar2=1.0,
                        op0=OP.mult, op1=OP.add),
        v.tensor_scalar(out=Nc(1), in0=Sc(1), scalar1=-1.0, scalar2=1.0,
                        op0=OP.mult, op1=OP.add),
        v.tensor_scalar(out=Nc(2), in0=Sc(2), scalar1=-1.0, scalar2=1.0,
                        op0=OP.mult, op1=OP.add),
        v.tensor_mul(out=Sc(1), in0=Sc(1), in1=Nc(0)),
        v.tensor_mul(out=Nc(1), in0=Nc(0), in1=Nc(1)),
        v.tensor_mul(out=GT4[:, 0:64:4], in0=Sc(0), in1=GR),
        v.tensor_mul(out=Sc(2), in0=Sc(2), in1=Nc(1)),
        v.tensor_mul(out=Nc(2), in0=Nc(1), in1=Nc(2)),
        v.tensor_mul(out=GT4[:, 1:64:4], in0=Sc(1), in1=GR),
        v.tensor_mul(out=Sc(3), in0=Sc(3), in1=Nc(2)),
        v.tensor_mul(out=GT4[:, 2:64:4], in0=Sc(2), in1=GR),
        v.tensor_mul(out=GT4[:, 3:64:4], in0=Sc(3), in1=GR))[-1],
        deps=[t_gr])
    g4_mm = E("t", lambda t: [
        t.transpose(pp[j // 8][0:4, 128 * (j % 8):128 * (j % 8) + 128],
                    GT4[:, 4 * j:4 * j + 4], ident)
        for j in range(16)][-1], deps=[t_sel2, t_lg])
    t_g4 = E("s", lambda s: (
        s.copy(out=G4sb[:, 0:1024], in_=pp[0][0:4, :]),
        s.copy(out=G4sb[:, 1024:2048], in_=pp[1][0:4, :]))[-1],
        deps=[g4_mm])
    mix_tok[0]["v"] = emit_mix_s2(0, "v", s1v0, dmix)
    s1r0 = emit_mix_s1(0, "r", "s", dmix + [t_sel2, g4_mm])
    g15_mm = E("t", lambda t: [
        t.matmul(pp[2 + nt // 2][0:15, 512 * (nt % 2):512 * (nt % 2) + 512],
                 selS, G4sb[:, 512 * nt:512 * nt + 512],
                 start=True, stop=True)
        for nt in range(4)][-1], deps=[t_g4, t_sel, t_lgc])
    t_hg = E("v", lambda v: (
        v.tensor_mul(out=hcat[:, 0:1024], in0=hcat[:, 0:1024],
                     in1=pp[2][0:15, :]),
        v.tensor_mul(out=hcat[:, 1024:2048], in0=hcat[:, 1024:2048],
                     in1=pp[3][0:15, :]))[-1],
        deps=[g15_mm, hc_ev[-1]])
    mix_tok[0]["r"] = emit_mix_s2(0, "r", s1r0, dmix + [t_sel2, g4_mm])

    if debug:
        for nm, buf, dt in (("projbf", proj_bf, F32), ("lgfm", raw_sb, F32),
                            ("LG", LG, F32), ("GT4", GT4, F32),
                            ("hg", hcat, BF16),
                            ("rn2", rn, F32)):
            d = dram_dbg(nm, list(buf.shape), dt)
            E("q", lambda q, d=d, b=buf: q.dma_start(out=d, in_=b),
              deps=[t_hg, t_sel2, t_lg], dma=True)

    P.fence()
    it_toks = []
    carps, carqs = [None] * 8, [None] * 8
    psm = {"k": pp[0], "v": pp[1], "r": pp[2]}

    def emit_late(pi, pit):
        s2 = pi % 2
        Dp, Ep = DD[s2], EE[s2]
        pit["tden"] = E("s", lambda s, Dp=Dp, Ep=Ep: s.activation(
            out=Ep, in_=Dp, func=AF.Identity, bias=1.0, scale=1.0 + 1e-9),
            deps=[pit["tro"], pit["tya"]])
        pit["t1g"] = E("g", lambda g, Dp=Dp: g.tensor_mul(
            out=Dp, in0=Dp, in1=qbuf[:, 0:TH]),
            deps=[pit["tden"], pit["tqs"], pit["tro"]])
        pit["t20"] = E("g", lambda g, Dp=Dp: g.tensor_add(
            out=Dp, in0=Dp, in1=v_sb))

    def emit_late2(pi, pit):
        Ep = EE[pi % 2]
        pit["t21"] = E("v", lambda v, Ep=Ep, Rb=rsb:
                       v.scalar_tensor_tensor(
                           out=Ep, in0=Rb, scalar=1.0, in1=Ep, op0=OP.add,
                           op1=OP.mult), deps=[pit["ev"]["r"], pit["tden"]])
        pit["trd"] = E("s", lambda s, Ep=Ep: (
            s.activation(out=Ep, in_=Ep, func=AF.Ln),
            s.activation(out=Ep, in_=Ep, func=AF.Exp, scale=-1.0))[-1],
            deps=[pit["t21"]])

    def emit_ty(pi, pit):
        s2 = pi % 2
        h2, c2 = pi // 8, pi % 8
        pit["ty"] = E("g", lambda g, c=c2, h=h2, Dp=DD[s2], Ep=EE[s2]:
                      g.tensor_mul(out=y3[:, c, TH * h:TH * h + TH],
                                   in0=Dp, in1=Ep),
                      deps=[pit["trd"], pit["t20"]])

    for i, (h, c) in enumerate([(h, c) for h in range(2) for c in range(CH)]):
        it = {}
        s2 = i % 2
        D_b, E_b = DD[s2], EE[s2]
        prev = it_toks[i - 1] if i >= 1 else None
        if prev:
            emit_late(i - 1, prev)
        nxt = i + 1
        if nxt < 16:
            nh, ncc = nxt // 8, nxt % 8
            if (ncc, nh) not in slice_toks:
                pv = it_toks[i - 1]["mm"] if i >= 1 else None
                deps = {p: [pv[p]] for p in "kvr"} if pv else {}
                slice_toks[(ncc, nh)] = load_slices(ncc, nh, deps)
        if h == 1 and c == 0:
            mix_tok[1] = emit_mix(1, [it_toks[7]["mm"][p] for p in "kvr"])
        sl = slice_toks[(c, h)]
        it["mm"] = {}
        rb_i = pp[2 + i % 2]
        for p in "kvr":
            def fmm(t, p=p, c=c, rb_i=rb_i):
                tgt = rb_i if p == "r" else psm[p]
                last = None
                for nt in range(2):
                    for cd in range(CH):
                        last = t.matmul(
                            tgt[:, 512 * nt:512 * nt + 512],
                            ws[p][c % 2][:, 128 * cd:128 * cd + 128],
                            mixh[p][:, TH * cd + 512 * nt:
                                     TH * cd + 512 * nt + 512],
                            start=(cd == 0), stop=(cd == 7))
                return last
            pgate = None
            if prev:
                pgate = {"k": prev["t13"], "v": prev["ev"]["v"],
                         "r": it_toks[i - 2]["ev"]["r"] if i >= 2
                         else None}[p]
            it["mm"][p] = E("t", fmm, deps=[sl[p], mix_tok[h][p], pgate])
        it["ev"] = {}
        war_q = prev["t1g"] if prev else None
        if h == 0:
            E("v", lambda v: v.memset(pbuf[:, 0:1], -1e30))
        else:
            E("v", lambda v, c=c: v.tensor_copy(out=pbuf[:, 0:1],
                                                in_=car[:, c:c + 1]),
              deps=[carps[c]])
        tps = E("v", lambda v, c=c: v.tensor_tensor_scan(
            out=pbuf[:, 1:TH + 1], data0=wcol(c).broadcast_to([128, TH]),
            data1=psm["k"], initial=pbuf[:, 0:1], op0=OP.add, op1=OP.max),
            deps=[it["mm"]["k"],
                  carps[i - 1] if 1 <= i <= 8 else None])
        t7 = E("v", lambda v: v.tensor_sub(out=B_b, in0=psm["k"],
                                           in1=pbuf[:, 1:TH + 1]),
               deps=[prev["tqs"] if prev else None])
        tb = E("s", lambda s: s.activation(out=B_b, in_=B_b, func=AF.Exp),
               deps=[t7])
        it["ev"]["v"] = E("s", lambda s: s.copy(out=v_sb, in_=psm["v"]),
                          deps=[it["mm"]["v"], prev["t20"] if prev else None])
        t4 = E("v", lambda v, c=c: v.scalar_tensor_tensor(
            out=A_b, in0=pbuf[:, 0:TH], scalar=wcol(c),
            in1=pbuf[:, 1:TH + 1], op0=OP.add, op1=OP.subtract),
            deps=[prev["tqs"] if prev else None])
        ta = E("s", lambda s: s.activation(out=A_b, in_=A_b, func=AF.Exp),
               deps=[t4])
        if prev:
            emit_late2(i - 1, prev)
        it["ev"]["r"] = E("s", lambda s, rb_i=rb_i: s.activation(
            out=rsb, in_=rb_i, func=AF.Exp, scale=-1.0),
            deps=[it["mm"]["r"],
                  it_toks[i - 1]["t21"] if i >= 1 else None])
        if h == 0:
            E("v", lambda v: v.memset(qbuf[:, 0:1], 0.0), deps=[war_q])
        else:
            E("v", lambda v, c=c: v.tensor_copy(out=qbuf[:, 0:1],
                                                in_=car[:, 8 + c:9 + c]),
              deps=[war_q, carqs[c]])
        tya = it_toks[i - 2]["ty"] if i >= 2 else None
        it["t13"] = E("v", lambda v, c=c, E_b=E_b: v.scalar_tensor_tensor(
            out=E_b, in0=pbuf[:, 0:TH], scalar=ucol(c), in1=psm["k"],
            op0=OP.subtract, op1=OP.subtract), deps=[tya])
        it["tdc"] = E("v", lambda v, D_b=D_b, E_b=E_b: v.tensor_scalar(
            out=D_b, in0=E_b, scalar1=30.0, scalar2=0.0,
            op0=OP.min, op1=OP.add))
        tbv = E("g", lambda g: g.tensor_mul(out=B_b, in0=B_b, in1=v_sb),
                deps=[tb, it["ev"]["v"]])
        if i >= 1 and i <= 8:
            carqs[i - 1] = E("s", lambda s, i=i: s.copy(
                out=car[:, 7 + i:8 + i], in_=qbuf[:, TH:TH + 1]),
                deps=[it_toks[i - 1]["tqs"]])
        if h == 0:
            carps[c] = E("s", lambda s, c=c: s.copy(
                out=car[:, c:c + 1], in_=pbuf[:, TH:TH + 1]), deps=[tps])
        it["tro"] = E("s", lambda s, D_b=D_b: s.activation(
            out=D_b, in_=D_b, func=AF.Exp), deps=[it["tdc"]])
        it["tya"] = tya
        if prev:
            emit_ty(i - 1, prev)
        it["tqs"] = E("v", lambda v: v.tensor_tensor_scan(
            out=qbuf[:, 1:TH + 1], data0=A_b, data1=B_b,
            initial=qbuf[:, 0:1], op0=OP.mult, op1=OP.add),
            deps=[ta, tbv, war_q,
                  carqs[i - 1] if 1 <= i <= 8 else None])
        if debug and i == 0:
            for nm, buf in (("v0", v_sb),
                            ("p0", pbuf), ("q0", qbuf)):
                d = dram_dbg(nm, list(buf.shape), F32)
                E("q", lambda q, d=d, b=buf: q.dma_start(out=d, in_=b),
                  deps=[it["tqs"]], dma=True)
        it_toks.append(it)
    emit_late(15, it_toks[15])
    emit_late2(15, it_toks[15])
    emit_ty(15, it_toks[15])

    if debug:
        d = dram_dbg("y", [128, CH * T], BF16)
        E("q", lambda q, d=d: q.dma_start(out=d, in_=y),
          deps=[it_toks[-1]["ty"]], dma=True)
        for p in "kvr":
            d = dram_dbg("mix" + p, [128, CH * TH], BF16)
            E("q", lambda q, d=d, b=mixh[p]: q.dma_start(out=d, in_=b),
              deps=[it_toks[-1]["ty"]], dma=True)

    it15 = it_toks[15]
    t_tab0 = E("s", lambda s: s.add_instruction(mybir.InstLoadActFuncSet(
        name=s.bass.get_next_instruction_name(), act_func_set_id=0, ins=[],
        outs=[])), deps=[it15["trd"]])
    store_toks = []
    j_toks = [None] * NTOK
    for j in range(NTOK):
        prev2 = j_toks[j - 2] if j >= 2 else None
        tx = E("q", lambda q, j=j: q.dma_start(out=xt5[j % 2],
                                               in_=x_t[:, j, :]),
               deps=[prev2["out"] if prev2 else None,
                     it15["mm"]["k"] if j < 2 else None], dma=True)
        pss = pp[j % 2]
        psf = pp[2 + j % 2]
        yh = it_toks[7]["ty"] if j < 8 else it15["ty"]
        smm = E("t", lambda t, j=j, pss=pss: [
            t.matmul(pss[:, 512 * nt:512 * nt + 512],
                     y3[:, ce, 128 * j:128 * j + 128],
                     wot3[:, ce, 512 * nt:512 * nt + 512],
                     start=(ce == 0), stop=(ce == 7))
            for ce in range(CH) for nt in range(2)][-1],
            deps=[t_wot, prev2["sev"] if prev2 else None,
                  yh if j in (0, 1, 8, 9) else None,
                  it15["t13"] if j == 0 else None,
                  it15["ev"]["v"] if j == 1 else None])
        sev = E("s", lambda s, j=j, pss=pss: s.copy(out=OT[j % 2], in_=pss),
                deps=[smm, store_toks[j - 2] if j >= 2 else None])

        def ffm(t, j=j, psf=psf):
            for ce in range(CH):
                for nt in range(2):
                    t.matmul(psf[:, 512 * nt:512 * nt + 512],
                             xn3[:, ce, 128 * j:128 * j + 128],
                             wt3[:, ce, 512 * nt:512 * nt + 512],
                             start=(ce == 0), stop=False)
            last = None
            for nt in range(2):
                last = t.matmul(psf[:, 512 * nt:512 * nt + 512],
                                hcat[:, 128 * j:128 * j + 128],
                                wut_sb[:, 512 * nt:512 * nt + 512],
                                start=False, stop=True)
            return last
        fmm = E("t", ffm, deps=[t_wt, t_wut,
                                prev2["ep"] if prev2 else None,
                                prev2["sev2"] if prev2 else None,
                                it_toks[14]["t21"] if j == 0 else None,
                                it15["t21"] if j == 1 else None,
                                ph0_v if j == 0 else None,
                                ph0_s if j == 0 else None,
                                t_hg if j == 0 else None])
        tep = E("s", lambda s, j=j, psf=psf: s.activation(
            out=EP[j % 2], in_=psf, func=AF.Tanh, scale=0.5),
            deps=[fmm, t_tab0])
        sev2 = E("s", lambda s, j=j, psf=psf: s.copy(out=GZ[j % 2], in_=psf),
                 deps=[fmm, prev2["out"] if prev2 else None])
        tz1 = E("v", lambda v, j=j: v.scalar_tensor_tensor(
            out=GZ[j % 2], in0=EP[j % 2], scalar=1.0, in1=GZ[j % 2],
            op0=OP.add, op1=OP.mult), deps=[tep, sev2])
        tz = E("v", lambda v, j=j: v.tensor_mul(
            out=GZ[j % 2], in0=GZ[j % 2], in1=OT[j % 2]), deps=[sev])
        tout = E("v", lambda v, j=j: v.scalar_tensor_tensor(
            out=OT[j % 2], in0=GZ[j % 2], scalar=0.5, in1=xt5[j % 2],
            op0=OP.mult, op1=OP.add),
            deps=[tx])
        st = E("q", lambda q, j=j: q.dma_start(out=out_t[:, j, :],
                                               in_=OT[j % 2]),
               deps=[tout], dma=True)
        store_toks.append(st)
        j_toks[j] = dict(out=tout, z=tz, z1=tz1, ep=tep, sev=sev, sev2=sev2)

    all_q_dmas = [("q", i, True) for i, op in enumerate(P.ops["q"]) if op[2]]
    all_g_dmas = [("g", i, True) for i, op in enumerate(P.ops["g"]) if op[2]]
    E("q", lambda q: None, deps=store_toks + all_q_dmas + all_g_dmas)

    P.resolve()
    with ctx:
        from contextlib import ExitStack as ES2
        with ES2() as sem_ctx:
            sems = {}
            for e in Plan.ENGINES:
                sems[(e, False)] = sem_ctx.enter_context(
                    nc.semaphore(f"sem_{e}"))
                for j in range(Plan.NDMA.get(e, 1)):
                    sems[(e, "dma", j)] = sem_ctx.enter_context(
                        nc.semaphore(f"semd_{e}{j}"))
            for e in ("g", "s", "v"):
                sems[(e, "ch")] = sem_ctx.enter_context(
                    nc.semaphore(f"ch_{e}"))
            with nc.Block() as block:
                @block.sync
                def _(q):
                    P.replay("q", q, sems, chain_counts=chain_counts)

                @block.gpsimd
                def _(g):
                    P.replay("g", g, sems, sems[("g", "ch")],
                             chain_counts=chain_counts)

                @block.tensor
                def _(t):
                    P.replay("t", t, sems, chain_counts=chain_counts)

                @block.scalar
                def _(s):
                    P.replay("s", s, sems, sems[("s", "ch")],
                             chain_counts=chain_counts)

                @block.vector
                def _(v):
                    P.replay("v", v, sems, sems[("v", "ch")],
                             chain_counts=chain_counts)
    return nc, dbg, P


def kernel(**inputs):
    from concourse.bass_utils import run_bass_kernel_spmd
    if "nc" not in _CACHE:
        _CACHE["nc"], _ = build_program(debug=False)
    nc = _CACHE["nc"]
    shared = _prep_shared(inputs)
    x = np.asarray(inputs["L0_attn_x"])
    in_maps = []
    for b in range(8):
        m = dict(shared)
        m["x"] = np.ascontiguousarray(x[b])
        in_maps.append(m)
    res = run_bass_kernel_spmd(nc, in_maps, core_ids=list(range(8)))
    out = np.stack([np.asarray(res.results[b]["out"]) for b in range(8)],
                   axis=0)
    return out.astype(np.float32)
